# version 12
# baseline (speedup 1.0000x reference)
import numpy as np
import ml_dtypes
from contextlib import ExitStack
import concourse.bass as bass
import concourse.mybir as mybir
from concourse.bass_utils import run_bass_kernel_spmd

F32 = mybir.dt.float32
BF16 = mybir.dt.bfloat16
I32 = mybir.dt.int32
AF = mybir.ActivationFunctionType
ALU = mybir.AluOpType
AX = mybir.AxisListType.X

T = 2048
D = 1024
NC = T // 128
NDC = D // 128
LN_EPS = 1e-5

ENGS = ("sync", "gpsimd", "vector", "scalar", "tensor")


class Bld:
    def __init__(self, nc):
        self.nc = nc
        self.q = {e: [] for e in ENGS}
        self.dmapool = {"sync": [f"ws{i}" for i in range(8)],
                        "gpsimd": [f"gs{i}" for i in range(4)],
                        "st": ["st0", "st1"]}
        self.dmarr = {"sync": 0, "gpsimd": 0, "st": 0}
        allsems = ["t", "v", "s", "g"]
        for v in self.dmapool.values():
            allsems += v
        self.allsems = allsems
        self.cnt = {s: 0 for s in allsems}
        self.lastw = {}
        self.dirty = {e: set() for e in ENGS}
        self.tensor_ops = 0
        self.rdirty = {e: set() for e in ENGS}
        self.sems = {}

    def wait(self, eng, tok):
        if tok is None:
            return
        s, v = tok
        if v <= 0:
            return
        k = (eng, s)
        if self.lastw.get(k, 0) >= v:
            return
        self.lastw[k] = v
        self.q[eng].append(lambda E, s=s, v=v: E.wait_ge(self.sems[s], v))

    def op(self, eng, f, reads=(), writes=(), inc=None, ninc=1, track=True):
        if track:
            rw = set(reads) | set(writes)
            if (rw & self.dirty[eng]) or (set(writes) & self.rdirty[eng]):
                self.dirty[eng].clear()
                self.rdirty[eng].clear()
                self.q[eng].append(lambda E: E.drain())
            self.dirty[eng].update(writes)
            self.rdirty[eng].update(reads)
        if eng == "tensor":
            self.tensor_ops += 1
        if inc is not None:
            self.cnt[inc] += ninc
            v = self.cnt[inc]
            self.q[eng].append(lambda E, f=f, s=inc, n=ninc: f(E).then_inc(self.sems[s], n))
            return (inc, v)
        self.q[eng].append(lambda E, f=f: f(E))
        return None

    def barrier(self, eng, inc):
        self.dirty[eng].clear()
        self.rdirty[eng].clear()
        self.q[eng].append(lambda E: E.drain())
        self.cnt[inc] += 1
        v = self.cnt[inc]
        self.q[eng].append(lambda E, s=inc: E.nop().then_inc(self.sems[s], 1))
        return (inc, v)

    def dma(self, queue, out, in_, reads=(), writes=()):
        pool = self.dmapool[queue]
        sem = pool[self.dmarr[queue] % len(pool)]
        self.dmarr[queue] += 1
        qeng = "sync" if queue in ("sync", "st") else "gpsimd"
        if self.cnt[sem] > 0:
            self.wait(qeng, (sem, self.cnt[sem]))
        return self.op(qeng, lambda E, o=out, i=in_: E.dma_start(out=o, in_=i),
                       reads=reads, writes=writes, inc=sem, ninc=16, track=False)

    def dma_toks(self, queue):
        return [(sm, self.cnt[sm]) for sm in self.dmapool[queue]]

    def wait_dma_all(self, eng, queue):
        for tok in self.dma_toks(queue):
            self.wait(eng, tok)

    def flush_into(self, block):
        q = self.q
        self.q = {e: [] for e in ENGS}

        @block.sync
        def _(E):
            for f in q["sync"]:
                f(E)

        @block.gpsimd
        def _(E):
            for f in q["gpsimd"]:
                f(E)

        @block.vector
        def _(E):
            for f in q["vector"]:
                f(E)

        @block.scalar
        def _(E):
            for f in q["scalar"]:
                f(E)

        @block.tensor
        def _(E):
            for f in q["tensor"]:
                f(E)


def _host_consts():
    c = {}
    c["c_id_bf"] = np.eye(128, dtype=ml_dtypes.bfloat16)
    c["c_id_f32"] = np.eye(128, dtype=np.float32)
    c["c_ones_f32"] = np.ones((128, 128), dtype=np.float32)
    h2 = np.zeros((128, 2), dtype=ml_dtypes.bfloat16)
    h2[0:64, 0] = 1.0
    h2[64:128, 1] = 1.0
    c["c_h2"] = h2
    n1a = np.zeros((2, 128), dtype=ml_dtypes.bfloat16); n1a[0, :] = -1.0
    n1b = np.zeros((2, 128), dtype=ml_dtypes.bfloat16); n1b[1, :] = -1.0
    c["c_neg1a"] = n1a
    c["c_neg1b"] = n1b
    dk = 64
    inv = (1.0 / 10000.0 ** (np.arange(0, dk, 2, dtype=np.float32) / np.float32(dk))).astype(np.float32)
    t = np.arange(T, dtype=np.float32)
    f = (t[:, None] * inv[None, :]).astype(np.float32)
    cos = np.cos(f.astype(np.float64)).astype(np.float32)
    sin = np.sin(f.astype(np.float64)).astype(np.float32)
    CA = np.concatenate([cos, cos], axis=1)
    CB = np.concatenate([-sin, sin], axis=1)
    c["c_ropeA"] = np.ascontiguousarray(np.concatenate([CA, CA], axis=1))
    c["c_ropeB"] = np.ascontiguousarray(np.concatenate([CB, CB], axis=1))
    H, C = 16, 64
    gamma = (1.0 - np.exp2(-5.0 - np.arange(H, dtype=np.float32))).astype(np.float32)
    i = np.arange(C, dtype=np.float32)
    dq = (gamma[:, None] ** (i + 1.0)).astype(np.float32)
    dkv = (gamma[:, None] ** (C - 1.0 - i)).astype(np.float32)
    rel = i[:, None] - i[None, :]
    Dmat = np.where(rel >= 0, gamma[:, None, None] ** np.maximum(rel, 0.0), 0.0).astype(np.float32)
    c["c_dq"] = np.ascontiguousarray(np.tile(dq.T, (2, 1)))
    c["c_dkv"] = np.ascontiguousarray(np.tile(dkv.T, (2, 1)))
    dmT = Dmat.transpose(2, 0, 1).reshape(64, 16 * 64)
    c["c_dmatT"] = np.ascontiguousarray(np.tile(dmT, (2, 1)))
    jj = np.arange(128)[:, None]
    ii = np.arange(128)[None, :]
    c["c_keepji"] = (jj <= ii).astype(np.float32)
    for sidx in range(4):
        w = sidx * 128
        fill = np.zeros((128, 512), dtype=np.float32)
        kk = np.arange(128)[:, None]
        qq = np.arange(512)[None, :]
        fill[kk > qq + (0 - 0) + (w * 0)] = 0.0
        keep = kk <= (qq - w + 0)
        fill = np.where(keep, 0.0, -1e30).astype(np.float32)
        c[f"c_fill{sidx}"] = fill
    return c


def _prep_shared(inp):
    d = {}
    d["w_l0_qkvT"] = np.ascontiguousarray(inp["L0_attn_W_qkv"].T)
    d["w_l0_projT"] = np.ascontiguousarray(inp["L0_attn_W_proj"].T)
    d["w_l0_bpg"] = np.ascontiguousarray(np.broadcast_to(inp["L0_attn_b_proj"][None, :], (128, 1024)))
    d["w_l0_gam"] = np.ascontiguousarray(np.broadcast_to(inp["L0_attn_ln_gamma"][None, :], (128, 1024)))
    d["w_l0_bet"] = np.ascontiguousarray(np.broadcast_to(inp["L0_attn_ln_beta"][None, :], (128, 1024)))
    d["w_l0_cent"] = np.ascontiguousarray(inp["L0_attn_spatial_centers"])
    d["w_l0_wgT"] = np.ascontiguousarray(inp["L0_ffn_w_gate"].T)
    d["w_l0_wuT"] = np.ascontiguousarray(inp["L0_ffn_w_up"].T)
    d["w_l0_wdT"] = np.ascontiguousarray(inp["L0_ffn_w_down"].T)
    d["w_l1_q"] = np.ascontiguousarray(inp["L1_attn_Wq"])
    d["w_l1_k"] = np.ascontiguousarray(inp["L1_attn_Wk"])
    d["w_l1_v"] = np.ascontiguousarray(inp["L1_attn_Wv"])
    d["w_l1_g"] = np.ascontiguousarray(inp["L1_attn_Wg"])
    d["w_l1_o"] = np.ascontiguousarray(inp["L1_attn_Wo"])
    d["w_l1_gn"] = np.ascontiguousarray(np.tile(inp["L1_attn_g_norm_w"][None, :], (128, 2)))
    d["w_l1_f0"] = np.ascontiguousarray(inp["L1_ffn_weight"])
    d["w_l1_f1"] = np.ascontiguousarray(inp["L1_ffn_aux_weight1"])
    d["w_l1_f2"] = np.ascontiguousarray(inp["L1_ffn_aux_weight2"])
    d["w_l1_fb"] = np.ascontiguousarray(inp["L1_ffn_bias"].reshape(8, 128).T)
    d["w_l2_q"] = np.ascontiguousarray(inp["L2_attn_w_q"].transpose(1, 0, 2).reshape(1024, 1024))
    d["w_l2_k"] = np.ascontiguousarray(inp["L2_attn_w_k"].transpose(1, 0, 2).reshape(1024, 1024))
    d["w_l2_v"] = np.ascontiguousarray(inp["L2_attn_w_v"].transpose(1, 0, 2).reshape(1024, 1024))
    d["w_l2_o"] = np.ascontiguousarray(inp["L2_attn_w_o"])
    d["w_l2_wr"] = np.ascontiguousarray(inp["L2_ffn_Wr"])
    d["w_l2_w1"] = np.ascontiguousarray(inp["L2_ffn_W1"].reshape(8, 1024, 12, 128).transpose(0, 2, 1, 3))
    d["w_l2_w3"] = np.ascontiguousarray(inp["L2_ffn_W3"].reshape(8, 1024, 12, 128).transpose(0, 2, 1, 3))
    d["w_l2_w2"] = np.ascontiguousarray(inp["L2_ffn_W2"])
    d["w_l3_q"] = np.ascontiguousarray(inp["L3_attn_Wq"])
    d["w_l3_k"] = np.ascontiguousarray(inp["L3_attn_Wk"])
    d["w_l3_v"] = np.ascontiguousarray(inp["L3_attn_Wv"])
    d["w_l3_o"] = np.ascontiguousarray(inp["L3_attn_Wo"])
    d["w_l3_proj"] = np.ascontiguousarray(inp["L3_attn_proj"])
    d["w_l3_wgT"] = np.ascontiguousarray(inp["L3_ffn_w_gate"].T)
    d["w_l3_wuT"] = np.ascontiguousarray(inp["L3_ffn_w_up"].T)
    d["w_l3_wdT"] = np.ascontiguousarray(inp["L3_ffn_w_down"].T)
    d.update(_host_consts())
    return d


_DRAM_SHAPES = {
    "w_l0_qkvT": ([1024, 3072], F32), "w_l0_projT": ([1024, 1024], F32),
    "w_l0_bpg": ([128, 1024], F32), "w_l0_gam": ([128, 1024], F32),
    "w_l0_bet": ([128, 1024], F32), "w_l0_cent": ([1024, 2], F32),
    "w_l0_wgT": ([1024, 4096], F32), "w_l0_wuT": ([1024, 4096], F32),
    "w_l0_wdT": ([4096, 1024], F32),
    "w_l1_q": ([1024, 1024], F32), "w_l1_k": ([1024, 1024], F32),
    "w_l1_v": ([1024, 2048], F32), "w_l1_g": ([1024, 2048], F32),
    "w_l1_o": ([2048, 1024], F32), "w_l1_gn": ([128, 256], F32),
    "w_l1_f0": ([1024, 1024], F32), "w_l1_f1": ([1024, 1024], F32),
    "w_l1_f2": ([1024, 1024], F32), "w_l1_fb": ([128, 8], F32),
    "w_l2_q": ([1024, 1024], F32), "w_l2_k": ([1024, 1024], F32),
    "w_l2_v": ([1024, 1024], F32), "w_l2_o": ([1024, 1024], F32),
    "w_l2_wr": ([1024, 8], F32),
    "w_l2_w1": ([8, 12, 1024, 128], F32), "w_l2_w3": ([8, 12, 1024, 128], F32),
    "w_l2_w2": ([8, 1536, 1024], F32),
    "w_l3_q": ([1024, 1024], F32), "w_l3_k": ([1024, 1024], F32),
    "w_l3_v": ([1024, 1024], F32), "w_l3_o": ([1024, 1024], F32),
    "w_l3_proj": ([64, 64], F32),
    "w_l3_wgT": ([1024, 4096], F32), "w_l3_wuT": ([1024, 4096], F32),
    "w_l3_wdT": ([4096, 1024], F32),
    "c_ropeA": ([2048, 128], F32), "c_ropeB": ([2048, 128], F32),
    "c_dq": ([128, 16], F32), "c_dkv": ([128, 16], F32),
    "c_dmatT": ([128, 1024], F32),
    "c_keepji": ([128, 128], F32),
    "c_fill0": ([128, 512], F32), "c_fill1": ([128, 512], F32),
    "c_fill2": ([128, 512], F32), "c_fill3": ([128, 512], F32),
    "c_id_bf": ([128, 128], BF16), "c_id_f32": ([128, 128], F32),
    "c_h2": ([128, 2], BF16), "c_neg1a": ([2, 128], BF16), "c_neg1b": ([2, 128], BF16),
    "c_ones_f32": ([128, 128], F32),
}


class Prog:
    def __init__(self, layers=(0, 1, 2, 3)):
        self.layers = layers
        self.nc = bass.Bass("TRN2", target_bir_lowering=False)
        self.b = Bld(self.nc)
        self.d = {}
        self.d["x"] = self.nc.dram_tensor("x", [T, D], F32, kind="ExternalInput").ap()
        for n, (sh, dt) in _DRAM_SHAPES.items():
            self.d[n] = self.nc.dram_tensor(n, sh, dt, kind="ExternalInput").ap()
        self.d_out = self.nc.dram_tensor("y", [T, D], F32, kind="ExternalOutput").ap()
        self.scr = {}
        for n, sh, dt in (("s_XqT", [1024, 2048], BF16),
                          ("s_QT", [1024, 2048], BF16),
                          ("s_KT", [1024, 2048], BF16),
                          ("s_V", [2048, 1056], BF16),
                          ("s_wgq", [8, 1024, 512], BF16),
                          ("s_wuq", [8, 1024, 512], BF16),
                          ("s_wdq", [4096, 1024], BF16)):
            self.scr[n] = self.nc.dram_tensor(n, sh, dt, kind="Internal").ap()
        self.pfree = {}
        self.marks = []

    def mark(self, name):
        self.marks.append((name, self.b.tensor_ops))

    def usepsum(self, eng, slot):
        self.b.wait(eng, self.pfree.get(slot))

    def xc(self, c):
        return self.S["xres"][:, c * D:(c + 1) * D]

    def build(self):
        nc = self.nc
        b = self.b
        with ExitStack() as outer:
            S = {}
            ent = outer.enter_context

            def sbo(name, shape, dt=F32):
                S[name] = ent(nc.sbuf_tensor(name, shape, dt))
            sbo("xres", [128, NC * D])
            sbo("cols", [128, 128])
            sbo("tiny", [128, 64])
            sbo("id_bf", [128, 128], BF16)
            sbo("id_f32", [128, 128])
            sbo("ones_f32", [128, 128])
            sbo("keepji", [128, 128])
            self.S = S
            for s in b.allsems:
                b.sems[s] = ent(nc.semaphore(s))
            P = {}
            for i in range(6):
                P[f"p{i}"] = ent(nc.psum_tensor(f"p{i}", [128, 512], F32))
            P["bt0"] = ent(nc.psum_tensor("bt0", [128, 1024], BF16))
            P["bt1"] = ent(nc.psum_tensor("bt1", [128, 1024], BF16))
            self.P = P

            with nc.Block() as blk:
                for nm, dn in (("id_bf", "c_id_bf"), ("id_f32", "c_id_f32"),
                               ("ones_f32", "c_ones_f32"), ("keepji", "c_keepji")):
                    b.dma("sync", S[nm][:, :], self.d[dn], writes=(nm,))
                for c in range(NC):
                    b.dma("sync", self.xc(c), self.d["x"][c * 128:(c + 1) * 128, :],
                          writes=("xres",))
                b.op("vector", lambda E: E.memset(S["tiny"][:, 63:64], LN_EPS),
                     writes=("tiny",))
                for tok in b.dma_toks("sync"):
                    b.wait("vector", tok)
                    b.wait("scalar", tok)
                    b.wait("tensor", tok)
                    b.wait("gpsimd", tok)
                b.flush_into(blk)

            for li in self.layers:
                self.mark(f"=L{li}")
                getattr(self, f"layer{li}")()
            self.mark("=end")

            with nc.Block() as blk:
                vt = b.barrier("vector", "v")
                st_ = b.barrier("scalar", "s")
                b.wait("sync", vt)
                b.wait("sync", st_)
                last = None
                for c in range(NC):
                    last = b.dma("st", self.d_out[c * 128:(c + 1) * 128, :], self.xc(c),
                                 reads=("xres",))
                b.wait("sync", last)
                b.flush_into(blk)
        return nc

    def phaseA(self, xnT, nXqT=None, want_var2=False, comb=None, wr=None):
        self.mark("phaseA")
        b, S, P = self.b, self.S, self.P
        sc0, sc1 = S["A_sc0"], S["A_sc1"]
        scb = S["A_scb"]
        scI = S["A_scI"]
        if comb is not None:
            wr = S["A_wr"][:, 0:64]
            wr_tok = b.dma("sync", wr.rearrange("p (dc e) -> p dc e", e=8),
                           self.apx(self.d["w_l2_wr"], 0, [[8, 128], [128 * 8, 8], [1, 8]]),
                           writes=("A_wr",))
            b.wait("tensor", wr_tok)
        for c in range(NC):
            xcc = self.xc(c)
            mu = S["cols"][:, c:c + 1]
            rs = S["cols"][:, 16 + c:17 + c]
            t = b.op("vector", lambda E, o=mu, i=xcc: E.reduce_sum(o, i, axis=AX),
                     reads=("xres",), writes=("cols",), inc="v")
            b.op("vector", lambda E, o=mu: E.tensor_single_scalar(o, o, 1.0 / D, ALU.mult),
                 reads=("cols",), writes=("cols",))
            nm = S["tiny"][:, 0:1]
            b.op("vector", lambda E, o=nm, i=mu: E.tensor_single_scalar(o, i, -1.0, ALU.mult),
                 reads=("cols",), writes=("tiny",))
            tv = b.barrier("vector", "v")
            b.wait("scalar", tv)
            ssq = S["tiny"][:, 1:2]
            ts = b.op("scalar", lambda E, o=sc0[:, :], i=xcc, nm=nm, a=ssq:
                      E.activation(o, i, AF.Square, bias=nm, accum_out=a),
                      reads=("xres", "tiny"), writes=("A_sc0", "tiny"), inc="s")
            ts2 = b.op("scalar", lambda E, o=S["tiny"][:, 2:3], i=ssq:
                       E.activation(o, i, AF.Sqrt, scale=1.0 / D, bias=S["tiny"][:, 63:64]),
                       reads=("tiny",), writes=("tiny",), inc="s")
            b.wait("vector", ts2)
            b.op("vector", lambda E, o=rs, i=S["tiny"][:, 2:3]: E.reciprocal(o, i),
                 reads=("tiny",), writes=("cols",))
            bcol = S["tiny"][:, 3:4]
            b.op("vector", lambda E, o=bcol, a=mu, c2=rs: E.tensor_mul(o, a, c2),
                 reads=("cols",), writes=("tiny",))
            b.op("vector", lambda E, o=bcol: E.tensor_single_scalar(o, o, -1.0, ALU.mult),
                 reads=("tiny",), writes=("tiny",))
            tx = b.op("vector", lambda E, o=sc1[:, :], i=xcc, a=rs, bb=bcol:
                      E.tensor_scalar(o, i, a, bb, ALU.mult, ALU.add),
                      reads=("xres", "cols", "tiny"), writes=("A_sc1",), inc="v")
            tb = b.op("vector", lambda E, o=scb[:, 0:D], i=sc1[:, :]: E.tensor_copy(o, i),
                      reads=("A_sc1",), writes=("A_scb",), inc="v")
            if want_var2:
                b.wait("scalar", tx)
                ss2 = S["tiny"][:, 4:5]
                t5 = b.op("scalar", lambda E, o=sc0[:, :], i=sc1[:, :], a=ss2:
                          E.activation(o, i, AF.Square, accum_out=a),
                          reads=("A_sc1",), writes=("A_sc0", "tiny"), inc="s")
                t6 = b.op("scalar", lambda E, o=S["tiny"][:, 5:6], i=ss2:
                          E.activation(o, i, AF.Sqrt, scale=1.0 / D, bias=S["tiny"][:, 63:64]),
                          reads=("tiny",), writes=("tiny",), inc="s")
                b.wait("vector", t6)
                rs2 = S["cols"][:, 32 + c:33 + c]
                b.op("vector", lambda E, o=rs2, i=S["tiny"][:, 5:6]: E.reciprocal(o, i),
                     reads=("tiny",), writes=("cols",))
            if nXqT is not None:
                rm = S["tiny"][:, 6:7]
                b.op("vector", lambda E, o=rm, i=sc1[:, :]:
                     E.reduce_max(o, i, axis=AX, apply_absolute_value=True),
                     reads=("A_sc1",), writes=("tiny",))
                b.op("vector", lambda E, o=rm: E.tensor_single_scalar(o, rm, 1e-5, ALU.max),
                     reads=("tiny",), writes=("tiny",))
                adq = S["cols"][:, 64 + c:65 + c]
                b.op("vector", lambda E, o=adq, i=rm: E.tensor_single_scalar(o, i, 1.0 / 127.0, ALU.mult),
                     reads=("tiny",), writes=("cols",))
                scq = S["cols"][:, 48 + c:49 + c]
                b.op("vector", lambda E, o=S["tiny"][:, 7:8], i=rm: E.reciprocal(o, i),
                     reads=("tiny",), writes=("tiny",))
                b.op("vector", lambda E, o=scq, i=S["tiny"][:, 7:8]:
                     E.tensor_single_scalar(o, i, 127.0, ALU.mult),
                     reads=("tiny",), writes=("cols",))
                b.op("vector", lambda E, o=scI[:, :], i=sc1[:, :], a=scq:
                     E.tensor_single_scalar(o, i, a, ALU.mult),
                     reads=("A_sc1", "cols"), writes=("A_scI",))
                b.op("vector", lambda E, o=sc0[:, :], i=sc1[:, :], a=scq, ci=scI[:, :]:
                     E.scalar_tensor_tensor(o, i, a, ci, ALU.mult, ALU.subtract),
                     reads=("A_sc1", "cols", "A_scI"), writes=("A_sc0",))
                b.op("vector", lambda E, o=sc1[:, :], i=sc0[:, :], ci=scI[:, :]:
                     E.scalar_tensor_tensor(o, i, 0.5, ci, ALU.is_ge, ALU.add),
                     reads=("A_sc0", "A_scI"), writes=("A_sc1",))
                tq = b.op("vector", lambda E, o=scb[:, D:2 * D], i=sc0[:, :], t1=sc1[:, :]:
                          E.scalar_tensor_tensor(o, i, -0.5, t1, ALU.is_le, ALU.subtract),
                          reads=("A_sc0", "A_sc1"), writes=("A_scb",), inc="v")
            if comb is not None:
                self._router_chunk(c, sc1, comb, wr)
            self._transpose_chunk_to_T(scb[:, 0:D], "A_scb", xnT, c, tb)
            if nXqT is not None:
                self._transpose_chunk_to_T(scb[:, D:2 * D], "A_scb", nXqT, c, tq)

    def _transpose_chunk_to_T(self, src_ap, src_name, dstT, c, ready_tok):
        self.mark("_transpose_chunk_to_T")
        b, S, P = self.b, self.S, self.P
        dname, dten = dstT
        b.wait("tensor", ready_tok)
        for half in range(2):
            slot = "bt0" if half == 0 else "bt1"
            pt = P[slot]
            self.usepsum("tensor", slot)
            tt = None
            for k in range(4):
                dc = half * 4 + k
                tt = b.op("tensor", lambda E, o=pt[:, k * 128:(k + 1) * 128],
                          i=src_ap[:, dc * 128:(dc + 1) * 128], idm=S["id_bf"][:, :]:
                          E.transpose(o, i, idm), inc="t")
            b.wait("scalar", tt)
            dst = dten[:, :].rearrange("p (dc t) -> p dc t", t=T)[:, half * 4:(half + 1) * 4,
                                                                 c * 128:(c + 1) * 128]
            te = b.op("scalar", lambda E, o=dst, i=pt[:, 0:512].rearrange("p (k f) -> p k f", f=128):
                      E.copy(o, i), writes=(dname,), inc="s")
            self.pfree[slot] = te

    def _router_chunk(self, c, xn_sc1, comb, wr):
        self.mark("_router_chunk")
        b, S, P = self.b, self.S, self.P
        tx = b.barrier("vector", "v")
        b.wait("tensor", tx)
        tpos = S["A_rt"]
        for dc in range(NDC):
            self.usepsum("tensor", "p5")
            tt = b.op("tensor", lambda E, o=P["p5"][:, 0:128], i=xn_sc1[:, dc * 128:(dc + 1) * 128],
                      idm=S["id_f32"][:, :]: E.transpose(o, i, idm), inc="t")
            b.wait("scalar", tt)
            te = b.op("scalar", lambda E, o=tpos[:, dc * 128:(dc + 1) * 128], i=P["p5"][:, 0:128]:
                      E.copy(o, i), writes=("A_rt",), inc="s")
            self.pfree["p5"] = te
        tsc = b.barrier("scalar", "s")
        b.wait("tensor", tsc)
        self.usepsum("tensor", "p4")
        tl = None
        for dc in range(NDC):
            tl = b.op("tensor", lambda E, o=P["p4"][:, 0:8], l=tpos[:, dc * 128:(dc + 1) * 128],
                      r=wr[:, dc * 8:(dc + 1) * 8], st=(dc == 0), sp=(dc == NDC - 1):
                      E.matmul(o, l, r, start=st, stop=sp), inc="t")
        b.wait("vector", tl)
        l8 = S["A_mrow"][:, 0:8]
        tcp = b.op("vector", lambda E, o=l8, i=P["p4"][:, 0:8]: E.tensor_copy(o, i),
                   writes=("A_mrow",), inc="v")
        self.pfree["p4"] = tcp
        m1 = S["tiny"][:, 8:9]
        b.op("vector", lambda E, o=m1, i=l8: E.reduce_max(o, i, axis=AX),
             reads=("A_mrow",), writes=("tiny",))
        t8 = S["A_mrow"][:, 8:16]
        b.op("vector", lambda E, o=t8, i=l8, a=m1: E.tensor_single_scalar(o, i, a, ALU.is_equal),
             reads=("A_mrow", "tiny"), writes=("A_mrow",))
        b.op("vector", lambda E, o=t8, i=t8, l=l8: E.scalar_tensor_tensor(o, i, -1e30, l, ALU.mult, ALU.add),
             reads=("A_mrow",), writes=("A_mrow",))
        m2 = S["tiny"][:, 9:10]
        b.op("vector", lambda E, o=m2, i=t8: E.reduce_max(o, i, axis=AX),
             reads=("A_mrow",), writes=("tiny",))
        nm1 = S["tiny"][:, 10:11]
        b.op("vector", lambda E, o=nm1, i=m1: E.tensor_single_scalar(o, i, -1.0, ALU.mult),
             reads=("tiny",), writes=("tiny",))
        tv2 = b.barrier("vector", "v")
        b.wait("scalar", tv2)
        e8 = S["A_mrow"][:, 16:24]
        ts3 = b.op("scalar", lambda E, o=e8, i=l8, bb=nm1: E.activation(o, i, AF.Exp, bias=bb),
                   reads=("A_mrow", "tiny"), writes=("A_mrow",), inc="s")
        b.wait("vector", ts3)
        g8 = S["A_mrow"][:, 24:32]
        b.op("vector", lambda E, o=g8, i=l8, a=m2: E.tensor_single_scalar(o, i, a, ALU.is_ge),
             reads=("A_mrow", "tiny"), writes=("A_mrow",))
        b.op("vector", lambda E, o=e8, i=e8, j=g8: E.tensor_mul(o, i, j),
             reads=("A_mrow",), writes=("A_mrow",))
        zs = S["tiny"][:, 11:12]
        b.op("vector", lambda E, o=zs, i=e8: E.reduce_sum(o, i, axis=AX),
             reads=("A_mrow",), writes=("tiny",))
        rz = S["tiny"][:, 12:13]
        b.op("vector", lambda E, o=rz, i=zs: E.reciprocal(o, i),
             reads=("tiny",), writes=("tiny",))
        b.op("vector", lambda E, o=comb[:, c * 8:(c + 1) * 8], i=e8, a=rz:
             E.tensor_single_scalar(o, i, a, ALU.mult),
             reads=("A_mrow", "tiny"), writes=("comb",))

    def mmsep(self, base):
        b = self.b
        last = getattr(self, "_lastbase", None)
        if last is not None and base is not None and {last, base} == {0, 64}:
            self.usepsum("tensor", "p2")
            b.op("tensor", lambda E: E.matmul(self.P["p2"][:, 504:512],
                                              self.S["id_bf"][:, :],
                                              self.S["id_bf"][:, 0:8],
                                              start=True, stop=True))
        self._lastbase = base

    def phase_barrier(self):
        b = self.b
        toks = b.dma_toks("sync") + b.dma_toks("gpsimd") + b.dma_toks("st")
        toks.append(b.barrier("vector", "v"))
        toks.append(b.barrier("scalar", "s"))
        toks.append(b.barrier("tensor", "t"))
        toks.append(b.barrier("gpsimd", "g"))
        for e in ENGS:
            own = {"vector": "v", "scalar": "s", "tensor": "t", "gpsimd": "g"}.get(e)
            for tok in toks:
                if tok[0] != own:
                    b.wait(e, tok)
        self.pfree = {}
        return toks

    def layer0(self):
        nc, b, S, P, d = self.nc, self.b, self.S, self.P, self.d
        parts = getattr(self, "l0_parts", ("A", "bias", "attn", "ffn"))
        with ExitStack() as lctx:
            S["xnT"] = lctx.enter_context(nc.sbuf_tensor("l0_xnT", [128, NDC * T], BF16))
            xnT = ("xnT", S["xnT"])
            with ExitStack() as actx:
                for nm, sh, dt in (("A_sc0", [128, D], F32), ("A_sc1", [128, D], F32),
                                   ("A_scb", [128, 2 * D], BF16), ("A_scI", [128, D], I32),
                                   ("A_mrow", [128, 32], F32),
                                   ("nXqT", [128, NDC * T], BF16)):
                    S[nm] = actx.enter_context(nc.sbuf_tensor("l0" + nm, sh, dt))
                with nc.Block() as blk:
                    if "A" in parts:
                        self.phaseA(xnT, nXqT=("nXqT", S["nXqT"]), want_var2=True)
                    st = b.barrier("scalar", "s")
                    b.wait("sync", st)
                    for dc in range(NDC):
                        b.dma("sync", self.scr["s_XqT"][dc * 128:(dc + 1) * 128, :],
                              S["nXqT"][:, dc * T:(dc + 1) * T], reads=("nXqT",))
                    self.phase_barrier()
                    b.flush_into(blk)
            if "attn" in parts:
                with ExitStack() as actx:
                    for nm, sh, dt in (("wsl", [128, 2 * 2048], BF16),
                                       ("stg", [128, 2048], BF16),
                                       ("p1w0", [128, 4096], F32), ("p1w1", [128, 4096], F32)):
                        S[nm] = actx.enter_context(nc.sbuf_tensor("l0p" + nm, sh, dt))
                    with nc.Block() as blk:
                        for g in range(4):
                            self._attn_proj_group(
                                g, wq=(d["w_l0_qkvT"], 0), wk=(d["w_l0_qkvT"], 1024),
                                wv=(d["w_l0_qkvT"], 2048), scale=0.125)
                        self._blw_pass1(d["w_l0_wgT"], d["w_l0_wuT"], d["w_l0_wdT"])
                        self.phase_barrier()
                        b.flush_into(blk)
        self._epi0 = True
        with ExitStack() as actx:
            for nm, sh, dt in (("B2", [128, 9 * 1536], BF16),
                               ("QTg", [128, 4 * T], BF16), ("KTg", [128, 4 * T], BF16),
                               ("Vg", [128, NC * 264], BF16),
                               ("PTq", [128, 2 * 16 * 512], BF16),
                               ("OTg", [128, 2 * T], BF16),
                               ("wsl", [128, 2 * 2048], BF16),
                               ("f0", [128, D], F32), ("f1", [128, D], F32),
                               ("f2", [128, D], F32), ("EP", [128, D], F32),
                               ("nrow", [128, 16], F32),
                               ("h2", [128, 2], BF16),
                               ("mrs", [128, 1024], F32),
                               ("obf", [128, 128], BF16), ("sq", [128, 1024], BF16)):
                S[nm] = actx.enter_context(nc.sbuf_tensor("l0" + nm, sh, dt))
            S["N2B"] = S["EP"]
            with nc.Block() as blk:
                b.dma("sync", S["h2"][:, :], self.d["c_h2"], writes=("h2",))
                if "bias" in parts:
                    self._l0_build_bias()
                self.phase_barrier()
                t0 = b.dma("sync", S["f0"][:, :], d["w_l0_bpg"], writes=("f0",))
                t1 = b.dma("sync", S["f2"][:, :], d["w_l0_bet"], writes=("f2",))
                t2 = b.dma("sync", S["f1"][:, :], d["w_l0_gam"], writes=("f1",))
                b.wait_dma_all("vector", "sync")
                b.op("vector", lambda E, o=S["f0"][:, :], i=S["f0"][:, :], j=S["f2"][:, :]:
                     E.tensor_add(o, i, j), reads=("f0", "f2"), writes=("f0",))
                if "attn" in parts:
                    for g in range(4):
                        self._l0_attn_group(g)
                self.phase_barrier()
                b.flush_into(blk)
        self._epi0 = False
        if "ffn" in parts:
            self._bitlinear_ffn(
                wgT=d["w_l0_wgT"], wuT=d["w_l0_wuT"], wdT=d["w_l0_wdT"], tag="l0f")
        S.pop("xnT", None)


    def layer2(self):
        nc, b, S, P, d = self.nc, self.b, self.S, self.P, self.d
        with ExitStack() as lctx:
            S["xnT"] = lctx.enter_context(nc.sbuf_tensor("l2_xnT", [128, NDC * T], BF16))
            S["comb"] = lctx.enter_context(nc.sbuf_tensor("l2_comb", [128, NC * 8], F32))
            xnT = ("xnT", S["xnT"])
            with ExitStack() as actx:
                for nm, sh, dt in (("A_sc0", [128, D], F32), ("A_sc1", [128, D], F32),
                                   ("A_scb", [128, 2 * D], BF16), ("A_scI", [128, D], I32),
                                   ("A_mrow", [128, 32], F32), ("A_wr", [128, 64], F32),
                                   ("A_rt", [128, D], F32)):
                    S[nm] = actx.enter_context(nc.sbuf_tensor("l2" + nm, sh, dt))
                with nc.Block() as blk:
                    self.phaseA(xnT, comb=S["comb"])
                    self.phase_barrier()
                    b.flush_into(blk)
            with ExitStack() as actx:
                for nm, sh, dt in (("wsl", [128, 2 * 2048], BF16),
                                   ("stg", [128, 2048], BF16)):
                    S[nm] = actx.enter_context(nc.sbuf_tensor("l2p" + nm, sh, dt))
                with nc.Block() as blk:
                    for g in range(4):
                        self._attn_proj_group(
                            g, wq=(d["w_l2_q"], 0), wk=(d["w_l2_k"], 0),
                            wv=(d["w_l2_v"], 0), scale=0.125)
                    self.phase_barrier()
                    b.flush_into(blk)
            with ExitStack() as actx:
                for nm, sh, dt in (("QTg", [128, 4 * T], BF16), ("KTg", [128, 4 * T], BF16),
                                   ("Vg", [128, NC * 264], BF16),
                                   ("PTq", [128, 2 * 16 * 512], BF16),
                                   ("OTg", [128, 2 * T], BF16),
                                   ("wsl", [128, 2 * 2048], BF16),
                                   ("f2", [128, D], F32),
                                   ("fills", [128, 4 * 512], F32),
                                   ("h2", [128, 2], BF16),
                                   ("mrs", [128, 1024], F32),
                                   ("obf", [128, 128], BF16), ("sq", [128, 1024], BF16)):
                    S[nm] = actx.enter_context(nc.sbuf_tensor("l2" + nm, sh, dt))
                with nc.Block() as blk:
                    b.dma("sync", S["h2"][:, :], self.d["c_h2"], writes=("h2",))
                    for i in range(4):
                        b.dma("sync", S["fills"][:, i * 512:(i + 1) * 512],
                              self.d[f"c_fill{i}"], writes=("fills",))
                    for eng in ("vector", "scalar", "tensor", "gpsimd"):
                        b.wait_dma_all(eng, "sync")
                    for g in range(4):
                        self._attn_group_common(
                            g, wq=None, wk=None, wv=None, scale=0.125,
                            projT=d["w_l2_o"], causal=True, biased=False,
                            tag=f"l2g{g}")
                    self.phase_barrier()
                    b.flush_into(blk)
            with ExitStack() as actx:
                for nm, sh, dt in (("zT", [128, 12 * T], BF16),
                                   ("W2b", [128, 12 * 1024], BF16),
                                   ("w1s", [128, 2 * 8 * 128], BF16),
                                   ("w3s", [128, 2 * 8 * 128], BF16),
                                   ("fg", [128, 512], F32)):
                    S[nm] = actx.enter_context(nc.sbuf_tensor("l2m" + nm, sh, dt))
                with nc.Block() as blk:
                    for e in range(8):
                        self._moe_expert(e)
                    self.phase_barrier()
                    b.flush_into(blk)
        for nm in ("xnT", "comb"):
            S.pop(nm, None)

    def _moe_expert(self, e):
        self.mark("_moe_expert")
        nc, b, S, P, d = self.nc, self.b, self.S, self.P, self.d
        b.wait("gpsimd", ("t", b.cnt["t"]))
        w2t = b.dma("gpsimd", S["W2b"][:, :].rearrange("p (hc e2) -> p hc e2", e2=1024),
                    self.apx(d["w_l2_w2"], e * 1536 * 1024,
                             [[1024, 128], [128 * 1024, 12], [1, 1024]]), writes=("W2b",))
        zt_guard = self.pfree.get("l2_zt")
        wtoks = {}

        def issue13(hc):
            if hc >= 12:
                return
            sl = (hc % 2) * 1024
            t1 = b.dma("gpsimd", S["w1s"][:, sl:sl + 1024].rearrange("p (dc e2) -> p dc e2", e2=128),
                       self.apx(d["w_l2_w1"], (e * 12 + hc) * 1024 * 128,
                                [[128, 128], [128 * 128, 8], [1, 128]]), writes=("w1s",))
            t3 = b.dma("gpsimd", S["w3s"][:, sl:sl + 1024].rearrange("p (dc e2) -> p dc e2", e2=128),
                       self.apx(d["w_l2_w3"], (e * 12 + hc) * 1024 * 128,
                                [[128, 128], [128 * 128, 8], [1, 128]]), writes=("w3s",))
            wtoks[hc] = (t1, t3)
        b.wait("gpsimd", ("t", b.cnt["t"]))
        issue13(0)
        issue13(1)
        for hc in range(12):
            sl = (hc % 2) * 1024
            b.wait("tensor", wtoks[hc][0])
            b.wait("tensor", wtoks[hc][1])
            for qt in range(4):
                self.usepsum("tensor", "p4")
                self.usepsum("tensor", "p5")
                th = tg = None
                for dc in range(NDC):
                    th = b.op("tensor", lambda E, o=P["p4"][:, :],
                              l=S["w1s"][:, sl + dc * 128: sl + (dc + 1) * 128],
                              r=S["xnT"][:, dc * T + qt * 512: dc * T + (qt + 1) * 512],
                              st=(dc == 0), sp=(dc == NDC - 1):
                              E.matmul(o, l, r, start=st, stop=sp), inc="t")
                for dc in range(NDC):
                    tg = b.op("tensor", lambda E, o=P["p5"][:, :],
                              l=S["w3s"][:, sl + dc * 128: sl + (dc + 1) * 128],
                              r=S["xnT"][:, dc * T + qt * 512: dc * T + (qt + 1) * 512],
                              st=(dc == 0), sp=(dc == NDC - 1):
                              E.matmul(o, l, r, start=st, stop=sp), inc="t")
                b.wait("scalar", th)
                b.wait("scalar", zt_guard)
                tsl = b.op("scalar", lambda E, o=S["fg"][:, :], i=P["p4"][:, :]:
                           E.activation(o, i, AF.Silu), writes=("fg",), inc="s")
                self.pfree["p4"] = tsl
                b.wait("vector", tsl)
                b.wait("vector", tg)
                b.wait("vector", zt_guard)
                tz = b.op("vector", lambda E, o=S["zT"][:, hc * T + qt * 512: hc * T + (qt + 1) * 512],
                          i=S["fg"][:, :], j=P["p5"][:, :]:
                          E.tensor_mul(o, i, j), reads=("fg",), writes=("zT",), inc="v")
                self.pfree["p5"] = tz
            b.wait("gpsimd", ("t", b.cnt["t"]))
            issue13(hc + 2)
        tzb = b.barrier("vector", "v")
        b.wait("tensor", tzb)
        b.wait("tensor", w2t)
        lastmm = None
        for qc in range(NC):
            combcol = S["comb"][:, qc * 8 + e: qc * 8 + e + 1]
            for half in range(2):
                slot = f"p{half}"
                self.usepsum("tensor", slot)
                tm = None
                for hc in range(12):
                    tm = b.op("tensor", lambda E, o=P[slot][:, :],
                              l=S["zT"][:, hc * T + qc * 128: hc * T + (qc + 1) * 128],
                              r=S["W2b"][:, hc * 1024 + half * 512: hc * 1024 + (half + 1) * 512],
                              st=(hc == 0), sp=(hc == 11):
                              E.matmul(o, l, r, start=st, stop=sp), inc="t")
                lastmm = tm
                b.wait("vector", tm)
                xh = self.xc(qc)[:, half * 512:(half + 1) * 512]
                tv = b.op("vector", lambda E, o=xh, i=P[slot][:, :], a=combcol, j=xh:
                          E.scalar_tensor_tensor(o, i, a, j, ALU.mult, ALU.add),
                          reads=("comb", "xres"), writes=("xres",), inc="v")
                self.pfree[slot] = tv
        self.pfree["l2_zt"] = lastmm




    def layer3(self):
        nc, b, S, P, d = self.nc, self.b, self.S, self.P, self.d
        with ExitStack() as lctx:
            S["xnT"] = lctx.enter_context(nc.sbuf_tensor("l3_xnT", [128, NDC * T], BF16))
            xnT = ("xnT", S["xnT"])
            with ExitStack() as actx:
                for nm, sh, dt in (("A_sc0", [128, D], F32), ("A_sc1", [128, D], F32),
                                   ("A_scb", [128, 2 * D], BF16), ("A_scI", [128, D], I32),
                                   ("nXqT", [128, NDC * T], BF16)):
                    S[nm] = actx.enter_context(nc.sbuf_tensor("l3" + nm, sh, dt))
                with nc.Block() as blk:
                    self.phaseA(xnT, nXqT=("nXqT", S["nXqT"]))
                    st = b.barrier("scalar", "s")
                    b.wait("sync", st)
                    for dc in range(NDC):
                        b.dma("sync", self.scr["s_XqT"][dc * 128:(dc + 1) * 128, :],
                              S["nXqT"][:, dc * T:(dc + 1) * T], reads=("nXqT",))
                    self.phase_barrier()
                    b.flush_into(blk)
            with ExitStack() as actx:
                for nm, sh, dt in (("wsl", [128, 2 * 2048], BF16),
                                   ("stg", [128, 2048], BF16),
                                   ("p1w0", [128, 4096], F32), ("p1w1", [128, 4096], F32)):
                    S[nm] = actx.enter_context(nc.sbuf_tensor("l3p" + nm, sh, dt))
                with nc.Block() as blk:
                    for g in range(4):
                        self._attn_proj_group(
                            g, wq=(d["w_l3_q"], 0), wk=(d["w_l3_k"], 0),
                            wv=(d["w_l3_v"], 0), scale=1.0)
                    self._blw_pass1(d["w_l3_wgT"], d["w_l3_wuT"], d["w_l3_wdT"])
                    self.phase_barrier()
                    b.flush_into(blk)
            with ExitStack() as actx:
                for nm, sh, dt in (("QTg", [128, 2 * T], BF16), ("KTg", [128, 2 * T], BF16),
                                   ("Vg", [128, NC * 264], BF16),
                                   ("OTg", [128, 2 * T], BF16),
                                   ("wsl", [128, 2 * 2048], BF16),
                                   ("f0", [128, 1024], F32), ("f1", [128, 1024], F32),
                                   ("pj", [128, 64], BF16),
                                   ("A2", [128, 128], BF16),
                                   ("fkt", [128, 4 * 64], BF16),
                                   ("KVf", [128, 2 * 72], F32), ("KVb", [128, 2 * 72], BF16),
                                   ("obf", [128, 128], BF16)):
                    S[nm] = actx.enter_context(nc.sbuf_tensor("l3" + nm, sh, dt))
                with nc.Block() as blk:
                    pt = b.dma("gpsimd", S["pj"][0:64, :], d["w_l3_proj"], writes=("pj",))
                    pt2 = b.dma("gpsimd", S["pj"][64:128, :], d["w_l3_proj"], writes=("pj",))
                    b.wait("tensor", pt)
                    b.wait("tensor", pt2)
                    for g in range(4):
                        self._l3_attn_group(g)
                    self.phase_barrier()
                    b.flush_into(blk)
        self._bitlinear_ffn(
            wgT=d["w_l3_wgT"], wuT=d["w_l3_wuT"], wdT=d["w_l3_wdT"], tag="l3f")
        S.pop("xnT", None)

    def _l3_attn_group(self, g):
        self.mark("_l3_attn_group")
        nc, b, S, P, d = self.nc, self.b, self.S, self.P, self.d
        self._lastbase = None
        b.wait("sync", ("t", b.cnt["t"]))
        b.wait("sync", ("s", b.cnt["s"]))
        tq = b.dma("sync", S["QTg"][:, :].rearrange("p (ec t) -> p ec t", t=T),
                   self.apx(self.scr["s_QT"], g * 256 * T,
                            [[T, 128], [128 * T, 2], [1, T]]), writes=("QTg",))
        tk = b.dma("sync", S["KTg"][:, :].rearrange("p (ec t) -> p ec t", t=T),
                   self.apx(self.scr["s_KT"], g * 256 * T,
                            [[T, 128], [128 * T, 2], [1, T]]), writes=("KTg",))
        tv_ = b.dma("sync", S["Vg"][:, :].rearrange("p (c w) -> p c w", w=264),
                    self.apx(self.scr["s_V"], g * 264,
                             [[1056, 128], [128 * 1056, NC], [1, 264]]), writes=("Vg",))
        for tok in (tq, tk, tv_):
            b.wait("tensor", tok)
            b.wait("scalar", tok)
            b.wait("vector", tok)
        for name in ("QTg", "KTg"):
            dstT = S[name]
            for ec in range(2):
                for qt in range(4):
                    slot = f"p{qt % 2}"
                    self.usepsum("tensor", slot)
                    tm = None
                    for hf in range(2):
                        self.mmsep(hf * 64)
                        tm = b.op("tensor", lambda E, o=P[slot][hf * 64:(hf + 1) * 64, :],
                                  l=S["pj"][hf * 64:(hf + 1) * 64, :],
                                  r=dstT[hf * 64:(hf + 1) * 64, ec * T + qt * 512: ec * T + (qt + 1) * 512],
                                  tp=(hf * 64, hf * 64):
                                  E.matmul(o, l, r, start=True, stop=True,
                                           tile_position=tp), inc="t")
                    b.wait("vector", tm)
                    b.wait("vector", self.pfree.get(f"l3fm{qt % 2}"))
                    fs0 = S["f0"][:, (qt % 2) * 512:(qt % 2) * 512 + 512]
                    fs1 = S["f1"][:, (qt % 2) * 512:(qt % 2) * 512 + 512]
                    t1 = b.op("vector", lambda E, o=fs0, i=P[slot][:, :]:
                              E.tensor_single_scalar(o, i, 0.0, ALU.min),
                              writes=("f0",), inc="v")
                    b.wait("scalar", t1)
                    t2 = b.op("scalar", lambda E, o=fs1, i=fs0:
                              E.activation(o, i, AF.Exp), reads=("f0",), writes=("f1",), inc="s")
                    b.wait("vector", t2)
                    t3 = b.op("vector", lambda E, o=dstT[:, ec * T + qt * 512: ec * T + (qt + 1) * 512],
                              i=P[slot][:, :], j=fs1:
                              E.scalar_tensor_tensor(o, i, 0.0, j, ALU.max, ALU.add),
                              reads=("f1",), writes=(name,), inc="v")
                    self.pfree[slot] = t3
                    self.pfree[f"l3fm{qt % 2}"] = t3
        tfm = b.barrier("vector", "v")
        b.wait("tensor", tfm)
        b.op("vector", lambda E, o=S["KVf"][:, :]: E.memset(o, 0.0), writes=("KVf",))
        tz = b.op("vector", lambda E, o=S["KVb"][:, :]: E.memset(o, 0.0),
                  writes=("KVb",), inc="v")
        b.wait("tensor", tz)
        for c in range(NC):
            for gh in range(4):
                ec, hf = gh // 2, gh % 2
                p0 = hf * 64
                kvo = (gh // 2) * 72
                self.usepsum("tensor", "p0")
                self.mmsep(p0)
                ta = b.op("tensor", lambda E, o=P["p0"][:, 0:128],
                          l=S["KTg"][p0:p0 + 64, ec * T + c * 128: ec * T + (c + 1) * 128],
                          r=S["QTg"][p0:p0 + 64, ec * T + c * 128: ec * T + (c + 1) * 128]:
                          E.matmul(o, l, r, start=True, stop=True), inc="t")
                b.wait("vector", ta)
                b.wait("vector", self.pfree.get("l3a2"))
                ta2 = b.op("vector", lambda E, o=S["A2"][:, :], i=P["p0"][:, 0:128],
                           j=S["keepji"][:, :]:
                           E.tensor_mul(o, i, j), reads=("keepji", "A2"), writes=("A2",), inc="v")
                self.pfree["p0"] = ta2
                bslot = "bt0" if (c * 4 + gh) % 2 == 0 else "bt1"
                self.usepsum("tensor", bslot)
                self.mmsep(p0)
                tt = b.op("tensor", lambda E, o=P[bslot][:, 0:64],
                          i=S["KTg"][p0:p0 + 64, ec * T + c * 128: ec * T + (c + 1) * 128],
                          idm=S["id_bf"][p0:p0 + 64, p0:p0 + 64]:
                          E.transpose(o, i, idm), inc="t")
                b.wait("scalar", tt)
                b.wait("scalar", self.pfree.get(f"l3fk{gh}"))
                tfk = b.op("scalar", lambda E, o=S["fkt"][:, gh * 64:(gh + 1) * 64],
                           i=P[bslot][:, 0:64]: E.copy(o, i), writes=("fkt",), inc="s")
                self.pfree[bslot] = tfk
                b.wait("tensor", ta2)
                b.wait("tensor", self.pfree.get(f"l3kv{gh}"))
                self.usepsum("tensor", "p2")
                self._lastbase = None
                b.op("tensor", lambda E, o=P["p2"][:, 0:65], l=S["A2"][:, :],
                     r=S["Vg"][:, c * 264 + gh * 66: c * 264 + gh * 66 + 65]:
                     E.matmul(o, l, r, start=True, stop=False), inc="t")
                self.mmsep(p0)
                to = b.op("tensor", lambda E, o=P["p2"][:, 0:65],
                          l=S["QTg"][p0:p0 + 64, ec * T + c * 128: ec * T + (c + 1) * 128],
                          r=S["KVb"][p0:p0 + 64, kvo:kvo + 65],
                          tp=(p0, 0):
                          E.matmul(o, l, r, start=False, stop=True, tile_position=tp), inc="t")
                self.pfree["l3a2"] = to
                b.wait("tensor", tfk)
                self.usepsum("tensor", "p4")
                tu = b.op("tensor", lambda E, o=P["p4"][p0:p0 + 64, 0:65],
                          l=S["fkt"][:, gh * 64:(gh + 1) * 64],
                          r=S["Vg"][:, c * 264 + gh * 66: c * 264 + gh * 66 + 65],
                          tp=(0, p0):
                          E.matmul(o, l, r, start=True, stop=True, tile_position=tp), inc="t")
                self.pfree[f"l3fk{gh}"] = tu
                b.wait("vector", tu)
                tkv = b.op("vector", lambda E, o=S["KVf"][p0:p0 + 64, kvo:kvo + 65],
                           i=P["p4"][p0:p0 + 64, 0:65], j=S["KVf"][p0:p0 + 64, kvo:kvo + 65]:
                           E.tensor_add(o, i, j), reads=("KVf",), writes=("KVf",), inc="v")
                self.pfree["p4"] = tkv
                b.wait("vector", to)
                tkb = b.op("vector", lambda E, o=S["KVb"][p0:p0 + 64, kvo:kvo + 65],
                           i=S["KVf"][p0:p0 + 64, kvo:kvo + 65]: E.tensor_copy(o, i),
                           reads=("KVf",), writes=("KVb",), inc="v")
                self.pfree[f"l3kv{gh}"] = tkb
                par = (c * 4 + gh) % 2
                b.wait("vector", to)
                b.wait("vector", self.pfree.get(f"obf_g{par}"))
                rec = S["tiny"][:, 35 + gh:36 + gh]
                b.op("vector", lambda E, o=rec, i=P["p2"][:, 64:65]:
                     E.tensor_single_scalar(o, i, 1e-6, ALU.add), writes=("tiny",))
                b.op("vector", lambda E, o=rec, i=rec: E.reciprocal(o, i),
                     reads=("tiny",), writes=("tiny",))
                obs = S["obf"][:, par * 64:(par + 1) * 64]
                tob = b.op("vector", lambda E, o=obs, i=P["p2"][:, 0:64],
                           a=rec: E.tensor_single_scalar(o, i, a, ALU.mult),
                           reads=("tiny", "obf"), writes=("obf",), inc="v")
                self.pfree["p2"] = tob
                b.wait("tensor", tob)
                bslot2 = "bt1" if par == 0 else "bt0"
                self.usepsum("tensor", bslot2)
                tt2 = b.op("tensor", lambda E, o=P[bslot2][p0:p0 + 64, 0:128], i=obs,
                           idm=S["id_bf"][:, :], tp=(0, p0):
                           E.transpose(o, i, idm, tile_position=tp), inc="t")
                self.pfree[f"obf_g{par}"] = tt2
                b.wait("scalar", tt2)
                te2 = b.op("scalar", lambda E,
                           o=S["OTg"][p0:p0 + 64, ec * T + c * 128: ec * T + (c + 1) * 128],
                           i=P[bslot2][p0:p0 + 64, 0:128]: E.copy(o, i),
                           writes=("OTg",), inc="s")
                self.pfree[bslot2] = te2
        self._attn_out_proj(g, d["w_l3_o"])


    def layer1(self):
        nc, b, S, P, d = self.nc, self.b, self.S, self.P, self.d
        gamma = (1.0 - np.exp2(-5.0 - np.arange(16, dtype=np.float32))).astype(np.float32)
        self._l1_gC = [float(g ** np.float32(64)) for g in gamma]
        with ExitStack() as lctx:
            S["xnT"] = lctx.enter_context(nc.sbuf_tensor("l1_xnT", [128, NDC * T], BF16))
            xnT = ("xnT", S["xnT"])
            with ExitStack() as actx:
                for nm, sh, dt in (("A_sc0", [128, D], F32), ("A_sc1", [128, D], F32),
                                   ("A_scb", [128, 2 * D], BF16), ("A_scI", [128, D], I32)):
                    S[nm] = actx.enter_context(nc.sbuf_tensor("l1" + nm, sh, dt))
                with nc.Block() as blk:
                    self.phaseA(xnT)
                    self.phase_barrier()
                    b.flush_into(blk)
            with ExitStack() as actx:
                for nm, sh, dt in (("wsl", [128, 4096], BF16),
                                   ("Qraw", [64, 32 * 128], BF16), ("Kraw", [64, 32 * 128], BF16),
                                   ("Vtok", [64, 32 * 256], BF16),
                                   ("Gs", [64, 32 * 256], BF16),
                                   ("rpA", [64, 32 * 128], BF16), ("rpB", [64, 32 * 128], BF16),
                                   ("QT", [64, 2 * T], BF16), ("KT", [64, 2 * T], BF16),
                                   ("Ktok", [64, 32 * 128], BF16),
                                   ("OTg", [128, 2 * T], BF16),
                                   ("dmat", [64, 128], BF16),
                                   ("dqc", [128, 16], F32), ("dkc", [128, 16], F32),
                                   ("gn2", [128, 256], BF16),
                                   ("A2", [128, 128], BF16),
                                   ("Sf", [128, 2 * 128], F32), ("Sb", [128, 2 * 128], BF16),
                                   ("f0", [128, 256], F32),
                                   ("ob", [128, 512], BF16), ("o2s", [128, 256], BF16)):
                    S[nm] = actx.enter_context(nc.sbuf_tensor("l1" + nm, sh, dt))
                with nc.Block() as blk:
                    for cn, dn in (("dqc", "c_dq"), ("dkc", "c_dkv")):
                        b.dma("sync", S[cn][:, :], d[dn], writes=(cn,))
                    b.dma("gpsimd", S["gn2"][:, :], d["w_l1_gn"], writes=("gn2",))
                    for qh in range(32):
                        b.dma("gpsimd", S["rpA"][0:64, qh * 128:(qh + 1) * 128],
                              d["c_ropeA"][qh * 64:(qh + 1) * 64, :], writes=("rpA",))
                        b.dma("gpsimd", S["rpB"][0:64, qh * 128:(qh + 1) * 128],
                              d["c_ropeB"][qh * 64:(qh + 1) * 64, :], writes=("rpB",))
                    for eng in ("vector", "scalar", "tensor", "gpsimd"):
                        b.wait_dma_all(eng, "sync")
                        b.wait_dma_all(eng, "gpsimd")
                    for g in range(getattr(self, "l1_ng", 8)):
                        self._l1_group(g)
                    self.phase_barrier()
                    b.flush_into(blk)
            with ExitStack() as actx:
                for nm, sh, dt in (("hA", [128, NDC * T], BF16), ("hB", [128, NDC * T], BF16),
                                   ("wfull", [128, 2 * 8 * 1024], BF16),
                                   ("fb", [128, 8], F32), ("ft", [128, 512], F32)):
                    S[nm] = actx.enter_context(nc.sbuf_tensor("l1f" + nm, sh, dt))
                with nc.Block() as blk:
                    b.dma("sync", S["fb"][:, :], d["w_l1_fb"], writes=("fb",))
                    b.wait_dma_all("scalar", "sync")
                    self._l1_ffn()
                    self.phase_barrier()
                    b.flush_into(blk)
        S.pop("xnT", None)

    def _l1_ffn(self):
        self.mark("_l1_ffn")
        nc, b, S, P, d = self.nc, self.b, self.S, self.P, self.d
        wnames = ("w_l1_f0", "w_l1_f1", "w_l1_f2")
        ftoks = {}

        def issue_f(step):
            if step >= 3:
                return
            fsl = (step % 2) * 8192
            ftoks[step] = b.dma("gpsimd",
                                S["wfull"][:, fsl:fsl + 8192].rearrange("p (dcI e) -> p dcI e", e=1024),
                                self.apx(d[wnames[step]], 0, [[1024, 128], [128 * 1024, 8], [1, 1024]]),
                                writes=("wfull",))
        b.wait("gpsimd", ("t", b.cnt["t"]))
        issue_f(0)
        issue_f(1)
        for step, wn in enumerate(wnames):
            fsl = (step % 2) * 8192
            src = S["xnT"] if step == 0 else (S["hA"] if step == 1 else S["hB"])
            srcn = "xnT" if step == 0 else ("hA" if step == 1 else "hB")
            dst = S["hA"] if step == 0 else (S["hB"] if step == 1 else None)
            dstn = "hA" if step == 0 else ("hB" if step == 1 else None)
            b.wait("tensor", ftoks[step])
            for dcO in range(NDC):
                for qt in range(4):
                    slot = f"p{qt % 2}"
                    self.usepsum("tensor", slot)
                    tm = None
                    for dcI in range(NDC):
                        tm = b.op("tensor", lambda E, o=P[slot][:, :],
                                  l=S["wfull"][:, fsl + dcI * 1024 + dcO * 128: fsl + dcI * 1024 + (dcO + 1) * 128],
                                  r=src[:, dcI * T + qt * 512: dcI * T + (qt + 1) * 512],
                                  st=(dcI == 0), sp=(dcI == NDC - 1):
                                  E.matmul(o, l, r, start=st, stop=sp), inc="t")
                    if step < 2:
                        b.wait("scalar", tm)
                        te = b.op("scalar", lambda E, o=dst[:, dcO * T + qt * 512: dcO * T + (qt + 1) * 512],
                                  i=P[slot][:, :]: E.copy(o, i), writes=(dstn,), inc="s")
                        self.pfree[slot] = te
                    else:
                        b.wait("scalar", tm)
                        b.wait("scalar", self.pfree.get("l1ft"))
                        tta = b.op("scalar", lambda E, o=S["ft"][:, :], i=P[slot][:, :],
                                   bb=S["fb"][:, dcO:dcO + 1]:
                                   E.activation(o, i, AF.Tanh, bias=bb),
                                   reads=("fb",), writes=("ft",), inc="s")
                        self.pfree[slot] = tta
                        b.wait("tensor", tta)
                        for k4 in range(4):
                            qc = qt * 4 + k4
                            self.usepsum("tensor", "p5")
                            tt = b.op("tensor", lambda E, o=P["p5"][:, 0:128],
                                      i=S["ft"][:, k4 * 128:(k4 + 1) * 128],
                                      idm=S["id_f32"][:, :]:
                                      E.transpose(o, i, idm), inc="t")
                            self.pfree["l1ft"] = tt
                            b.wait("vector", tt)
                            xh = self.xc(qc)[:, dcO * 128:(dcO + 1) * 128]
                            tv = b.op("vector", lambda E, o=xh, i=P["p5"][:, 0:128], j=xh:
                                      E.tensor_add(o, i, j), reads=("xres",), writes=("xres",), inc="v")
                            self.pfree["p5"] = tv
            b.wait("gpsimd", ("t", b.cnt["t"]))
            issue_f(step + 2)

    def _l1_group(self, g):
        nc, b, S, P, d = self.nc, self.b, self.S, self.P, self.d
        self.mark("lg_proj")
        b.wait("gpsimd", ("t", b.cnt["t"]))
        tq = b.dma("gpsimd", self.apx(S["wsl"][:, :], 0, [[4096, 128], [512, 8], [1, 128]]),
                   self.apx(d["w_l1_q"], g * 128, [[1024, 128], [128 * 1024, 8], [1, 128]]),
                   writes=("wsl",))
        tk = b.dma("gpsimd", self.apx(S["wsl"][:, :], 128, [[4096, 128], [512, 8], [1, 128]]),
                   self.apx(d["w_l1_k"], g * 128, [[1024, 128], [128 * 1024, 8], [1, 128]]),
                   writes=("wsl",))
        tv = b.dma("gpsimd", self.apx(S["wsl"][:, :], 256, [[4096, 128], [512, 8], [1, 256]]),
                   self.apx(d["w_l1_v"], g * 256, [[2048, 128], [128 * 2048, 8], [1, 256]]),
                   writes=("wsl",))
        tdm = b.dma("gpsimd", S["dmat"][:, :], d["c_dmatT"][0:64, g * 128:(g + 1) * 128],
                    writes=("dmat",))
        for tok in (tq, tk, tv):
            b.wait("tensor", tok)
        for qh in range(32):
            slot = f"p{qh % 2}"
            self.usepsum("tensor", slot)
            tm = None
            for dc in range(NDC):
                tm = b.op("tensor", lambda E, o=P[slot][0:64, 0:512],
                          l=S["xnT"][:, dc * T + qh * 64: dc * T + (qh + 1) * 64],
                          r=S["wsl"][:, dc * 512:(dc + 1) * 512],
                          st=(dc == 0), sp=(dc == NDC - 1):
                          E.matmul(o, l, r, start=st, stop=sp), inc="t")
            b.wait("scalar", tm)
            b.op("scalar", lambda E, o=S["Qraw"][0:64, qh * 128:(qh + 1) * 128],
                 i=P[slot][0:64, 0:128]: E.copy(o, i), writes=("Qraw",), inc="s")
            b.op("scalar", lambda E, o=S["Kraw"][0:64, qh * 128:(qh + 1) * 128],
                 i=P[slot][0:64, 128:256]: E.copy(o, i), writes=("Kraw",), inc="s")
            te = b.op("scalar", lambda E, o=S["Vtok"][0:64, qh * 256:(qh + 1) * 256],
                      i=P[slot][0:64, 256:512]: E.copy(o, i), writes=("Vtok",), inc="s")
            self.pfree[slot] = te
        self.mark("lg_G")
        tgb = b.barrier("tensor", "t")
        b.wait("gpsimd", tgb)
        tg = b.dma("gpsimd", self.apx(S["wsl"][:, :], 0, [[4096, 128], [256, 8], [1, 256]]),
                   self.apx(d["w_l1_g"], g * 256, [[2048, 128], [128 * 2048, 8], [1, 256]]),
                   writes=("wsl",))
        b.wait("tensor", tg)
        for qh in range(32):
            slot = f"p{qh % 2}"
            self.usepsum("tensor", slot)
            tm = None
            for dc in range(NDC):
                tm = b.op("tensor", lambda E, o=P[slot][0:64, 0:256],
                          l=S["xnT"][:, dc * T + qh * 64: dc * T + (qh + 1) * 64],
                          r=S["wsl"][:, dc * 256:(dc + 1) * 256],
                          st=(dc == 0), sp=(dc == NDC - 1):
                          E.matmul(o, l, r, start=st, stop=sp), inc="t")
            b.wait("scalar", tm)
            b.wait("scalar", self.pfree.get(f"obg{qh % 2}"))
            obs = S["ob"][0:64, (qh % 2) * 256:(qh % 2) * 256 + 256]
            tsl = b.op("scalar", lambda E, o=obs, i=P[slot][0:64, 0:256]:
                       E.activation(o, i, AF.Silu), writes=("ob",), inc="s")
            self.pfree[slot] = tsl
            b.wait("vector", tsl)
            tgs = b.op("vector", lambda E, o=S["Gs"][0:64, qh * 256:(qh + 1) * 256],
                       i=obs, j=S["gn2"][0:64, :]:
                       E.tensor_mul(o, i, j), reads=("ob", "gn2"), writes=("Gs",), inc="v")
            self.pfree[f"obg{qh % 2}"] = tgs
        self.mark("lg_rope")
        trd = b.barrier("scalar", "s")
        b.wait("vector", trd)
        for raw, rawn in ((S["Qraw"], "Qraw"), (S["Kraw"], "Kraw")):
            WR = raw.shape[1]
            for b4 in range(8):
                o1 = b4 * 512
                b.op("vector", lambda E, o=S["ob"][0:64, 0:512],
                     i=self.apx(raw[:, :], o1 + 32, [[WR, 64], [64, 8], [-32, 2], [1, 32]]),
                     j=S["rpB"][0:64, o1:o1 + 512]:
                     E.tensor_mul(o, i, j), reads=(rawn, "rpB", "ob"), writes=("ob",))
                b.op("vector", lambda E, o=raw[0:64, o1:o1 + 512], i=raw[0:64, o1:o1 + 512],
                     j=S["rpA"][0:64, o1:o1 + 512]:
                     E.tensor_mul(o, i, j), reads=(rawn, "rpA"), writes=(rawn,))
                b.op("vector", lambda E, o=raw[0:64, o1:o1 + 512], i=raw[0:64, o1:o1 + 512],
                     j=S["ob"][0:64, 0:512]:
                     E.tensor_add(o, i, j), reads=(rawn, "ob"), writes=(rawn,))
        for h in range(2):
            b.op("vector", lambda E,
                 o=self.apx(S["Ktok"][:, :], h * 64, [[4096, 64], [128, 32], [1, 64]]),
                 i=self.apx(S["Kraw"][:, :], h * 64, [[4096, 64], [128, 32], [1, 64]]),
                 a=S["dkc"][0:64, g * 2 + h: g * 2 + h + 1]:
                 E.tensor_single_scalar(o, i, a, ALU.mult),
                 reads=("Kraw", "dkc", "Ktok"), writes=("Ktok",))
        trv = b.barrier("vector", "v")
        b.wait("tensor", trv)
        for raw, dstT, dstn, sc in ((S["Qraw"], S["QT"], "QT", 0.125),
                                    (S["Kraw"], S["KT"], "KT", 1.0)):
            for qh in range(32):
                for h in range(2):
                    bslot = "bt0" if (qh * 2 + h) % 2 == 0 else "bt1"
                    self.usepsum("tensor", bslot)
                    tt = b.op("tensor", lambda E, o=P[bslot][0:64, 0:64],
                              i=raw[0:64, qh * 128 + h * 64: qh * 128 + (h + 1) * 64],
                              idm=S["id_bf"][0:64, 0:64]: E.transpose(o, i, idm), inc="t")
                    b.wait("scalar", tt)
                    te = b.op("scalar", lambda E,
                              o=dstT[0:64, h * T + qh * 64: h * T + (qh + 1) * 64],
                              i=P[bslot][0:64, 0:64], sc=sc:
                              E.activation(o, i, AF.Copy, scale=sc), writes=(dstn,), inc="s")
                    self.pfree[bslot] = te
        self.mark("lg_scan")
        b.op("vector", lambda E, o=S["Sf"][0:64, :]: E.memset(o, 0.0), writes=("Sf",))
        tz = b.op("vector", lambda E, o=S["Sb"][0:64, :]: E.memset(o, 0.0), writes=("Sb",), inc="v")
        tscl = b.barrier("scalar", "s")
        b.wait("tensor", tz)
        b.wait("tensor", tscl)
        b.wait("vector", tdm)
        b.wait("vector", ("t", b.cnt["t"]))
        gCs = [self._l1_gC[g * 2 + h] for h in range(2)]
        tutok = {}

        def issue_tu(c):
            if c >= 32:
                return
            bank = "p5" if c % 2 == 0 else "p3"
            self.usepsum("tensor", bank)
            t2 = None
            for h in range(2):
                t2 = b.op("tensor", lambda E, o=P[bank][0:64, h * 128:(h + 1) * 128],
                          l=S["Ktok"][0:64, c * 128 + h * 64: c * 128 + (h + 1) * 64],
                          r=S["Vtok"][0:64, c * 256 + h * 128: c * 256 + (h + 1) * 128]:
                          E.matmul(o, l, r, start=True, stop=True), inc="t")
            tutok[c] = t2
        issue_tu(0)
        issue_tu(1)
        tsb_prev = tz
        for c in range(32):
            bank2 = "p4" if c % 2 == 0 else "p2"
            tat = {}
            for h in range(2):
                self.usepsum("tensor", f"p{h}")
                tat[h] = b.op("tensor", lambda E, o=P[f"p{h}"][0:64, 0:64],
                              l=S["KT"][0:64, h * T + c * 64: h * T + (c + 1) * 64],
                              r=S["QT"][0:64, h * T + c * 64: h * T + (c + 1) * 64]:
                              E.matmul(o, l, r, start=True, stop=True), inc="t")
            ta2t = {}
            for h in range(2):
                b.wait("vector", tat[h])
                b.wait("vector", self.pfree.get(f"l1a2{h}"))
                ta2t[h] = b.op("vector", lambda E, o=S["A2"][0:64, h * 64:(h + 1) * 64],
                               i=P[f"p{h}"][0:64, 0:64],
                               j=S["dmat"][0:64, h * 64:(h + 1) * 64]:
                               E.tensor_mul(o, i, j), reads=("dmat", "A2"), writes=("A2",), inc="v")
                self.pfree[f"p{h}"] = ta2t[h]
            self.usepsum("tensor", bank2)
            toat = {}
            toit = {}
            for h in range(2):
                b.wait("tensor", ta2t[h])
                toat[h] = b.op("tensor", lambda E, o=P[bank2][0:64, h * 128:(h + 1) * 128],
                               l=S["A2"][0:64, h * 64:(h + 1) * 64],
                               r=S["Vtok"][0:64, c * 256 + h * 128: c * 256 + (h + 1) * 128]:
                               E.matmul(o, l, r, start=True, stop=True), inc="t")
                self.pfree[f"l1a2{h}"] = toat[h]
            b.wait("tensor", tsb_prev)
            for h in range(2):
                toit[h] = b.op("tensor", lambda E,
                               o=P[bank2][0:64, 256 + h * 128: 256 + (h + 1) * 128],
                               l=S["QT"][0:64, h * T + c * 64: h * T + (c + 1) * 64],
                               r=S["Sb"][0:64, h * 128:(h + 1) * 128]:
                               E.matmul(o, l, r, start=True, stop=True), inc="t")
            ubank = "p5" if c % 2 == 0 else "p3"
            b.wait("vector", tutok[c])
            tsu = None
            for h in range(2):
                tsu = b.op("vector", lambda E, o=S["Sf"][0:64, h * 128:(h + 1) * 128],
                           i=S["Sf"][0:64, h * 128:(h + 1) * 128], sc=gCs[h],
                           j=P[ubank][0:64, h * 128:(h + 1) * 128]:
                           E.scalar_tensor_tensor(o, i, sc, j, ALU.mult, ALU.add),
                           reads=("Sf",), writes=("Sf",), inc="v")
            self.pfree[ubank] = tsu
            b.wait("vector", toit[1])
            for h in range(2):
                tsb_prev = b.op("vector", lambda E, o=S["Sb"][0:64, h * 128:(h + 1) * 128],
                                i=S["Sf"][0:64, h * 128:(h + 1) * 128]: E.tensor_copy(o, i),
                                reads=("Sf", "Sb"), writes=("Sb",), inc="v")
            ogt, ogn = (S["Qraw"], "Qraw") if c < 16 else (S["Kraw"], "Kraw")
            co = (c % 16) * 256
            b.wait("vector", toat[1])
            for h in range(2):
                b.op("vector", lambda E, o=S["f0"][0:64, h * 128:(h + 1) * 128],
                     i=P[bank2][0:64, 256 + h * 128: 256 + (h + 1) * 128],
                     a=S["dqc"][0:64, g * 2 + h: g * 2 + h + 1]:
                     E.tensor_single_scalar(o, i, a, ALU.mult),
                     reads=("dqc", "f0"), writes=("f0",))
            tev = None
            for h in range(2):
                tev = b.op("vector", lambda E, o=ogt[0:64, co + h * 128: co + (h + 1) * 128],
                           i=S["f0"][0:64, h * 128:(h + 1) * 128],
                           j=P[bank2][0:64, h * 128:(h + 1) * 128]:
                           E.tensor_add(o, i, j), reads=("f0", ogn), writes=(ogn,), inc="v")
            self.pfree[bank2] = tev
            issue_tu(c + 2)
        self.mark("lg_epi")
        teb = b.barrier("vector", "v")
        b.wait("scalar", teb)
        for c in range(32):
            ogt, ogn = (S["Qraw"], "Qraw") if c < 16 else (S["Kraw"], "Kraw")
            osl = ogt[0:64, (c % 16) * 256:((c % 16) + 1) * 256]
            b.wait("scalar", self.pfree.get("o2h0"))
            tsq = b.op("scalar", lambda E,
                       o=S["o2s"][0:64, 0:256], i=osl:
                       E.activation(o, i, AF.Square), reads=(ogn,), writes=("o2s",), inc="s")
            b.wait("vector", tsq)
            rt = None
            for h in range(2):
                rt = b.op("vector", lambda E, o=S["Sf"][0:64, c * 2 + h: c * 2 + h + 1],
                          i=S["o2s"][0:64, h * 128:(h + 1) * 128]:
                          E.reduce_sum(o, i, axis=AX), reads=("o2s", "Sf"), writes=("Sf",), inc="v")
            self.pfree["o2h0"] = rt
        tms = b.barrier("vector", "v")
        b.wait("scalar", tms)
        tsr = b.op("scalar", lambda E, o=S["Sf"][0:64, 64:128], i=S["Sf"][0:64, 0:64]:
                   E.activation(o, i, AF.Sqrt, scale=1.0 / 128.0, bias=S["tiny"][0:64, 63:64]),
                   reads=("Sf", "tiny"), writes=("Sf",), inc="s")
        b.wait("vector", tsr)
        b.op("vector", lambda E, o=S["Sf"][0:64, 0:64], i=S["Sf"][0:64, 64:128]:
             E.reciprocal(o, i), reads=("Sf",), writes=("Sf",))
        te = None
        for c in range(32):
            ogt, ogn = (S["Qraw"], "Qraw") if c < 16 else (S["Kraw"], "Kraw")
            osl = ogt[0:64, (c % 16) * 256:((c % 16) + 1) * 256]
            ohf = (c % 2) * 256
            b.wait("vector", self.pfree.get(f"obh{c % 2}"))
            for h in range(2):
                b.op("vector", lambda E, o=S["ob"][0:64, ohf + h * 128: ohf + (h + 1) * 128],
                     i=osl[:, h * 128:(h + 1) * 128],
                     a=S["Sf"][0:64, c * 2 + h: c * 2 + h + 1]:
                     E.tensor_single_scalar(o, i, a, ALU.mult),
                     reads=(ogn, "Sf", "ob"), writes=("ob",))
            tob = b.op("vector", lambda E, o=S["ob"][0:64, ohf:ohf + 256],
                       i=S["ob"][0:64, ohf:ohf + 256], j=S["Gs"][0:64, c * 256:(c + 1) * 256]:
                       E.tensor_mul(o, i, j), reads=("ob", "Gs"), writes=("ob",), inc="v")
            b.wait("tensor", tob)
            tt = None
            for h in range(2):
                bslot = "bt0" if h == 0 else "bt1"
                self.usepsum("tensor", bslot)
                tt = b.op("tensor", lambda E, o=P[bslot][:, 0:64],
                          i=S["ob"][0:64, ohf + h * 128: ohf + (h + 1) * 128],
                          idm=S["id_bf"][0:64, 0:64]: E.transpose(o, i, idm), inc="t")
                b.wait("scalar", tt)
                te = b.op("scalar", lambda E,
                          o=S["OTg"][:, h * T + c * 64:(h * T + (c + 1) * 64)],
                          i=P[bslot][:, 0:64]: E.copy(o, i), writes=("OTg",), inc="s")
                self.pfree[bslot] = te
            self.pfree[f"obh{c % 2}"] = tt
        b.wait("tensor", te)
        self._attn_out_proj(g, d["w_l1_o"])

    def _l0_build_bias(self):
        self.mark("_l0_build_bias")
        nc, b, S, P, d = self.nc, self.b, self.S, self.P, self.d
        for cc in range(8):
            b.dma("sync", S["f0"][:, cc * 2:cc * 2 + 2],
                  d["w_l0_cent"][cc * 128:(cc + 1) * 128, :], writes=("f0",))
        b.wait_dma_all("vector", "sync")
        b.wait_dma_all("tensor", "sync")
        for cc in range(8):
            b.op("vector", lambda E, o=S["f1"][:, cc * 2:cc * 2 + 2], i=S["f0"][:, cc * 2:cc * 2 + 2]:
                 E.tensor_mul(o, i, i), reads=("f0",), writes=("f1",))
            b.op("vector", lambda E, o=S["nrow"][:, cc:cc + 1], i=S["f1"][:, cc * 2:cc * 2 + 2]:
                 E.reduce_sum(o, i, axis=AX), reads=("f1",), writes=("nrow",))
        tv = b.barrier("vector", "v")
        b.wait("tensor", tv)
        for cc in range(8):
            self.usepsum("tensor", "p5")
            tt = b.op("tensor", lambda E, o=P["p5"][0:2, 0:128], i=S["f0"][:, cc * 2:cc * 2 + 2],
                      idm=S["id_f32"][:, :]: E.transpose(o, i, idm), inc="t")
            b.wait("scalar", tt)
            te = b.op("scalar", lambda E, o=S["f1"][0:2, cc * 128:(cc + 1) * 128],
                      i=P["p5"][0:2, 0:128]:
                      E.copy(o, i), writes=("f1",), inc="s")
            self.pfree["p5"] = te
        tv = b.barrier("vector", "v")
        b.wait("tensor", tv)
        self.usepsum("tensor", "p4")
        t1 = b.op("tensor", lambda E, o=P["p4"][0:8, 0:128], i=S["nrow"][:, 0:8],
                  idm=S["id_f32"][:, :]: E.transpose(o, i, idm), inc="t")
        b.wait("scalar", t1)
        t2 = b.op("scalar", lambda E, o=S["f0"][0:8, 512:640], i=P["p4"][0:8, 0:128]:
                  E.copy(o, i), writes=("f0",), inc="s")
        self.pfree["p4"] = t2
        b.wait("sync", t2)
        src = S["f0"]
        fl = b.dma("sync", S["N2B"][0:1, :].rearrange("o (a b) -> o a b", b=128),
                   src[0:8, 512:640], reads=("f0",), writes=("N2B",))
        b.wait("gpsimd", fl)
        n2b = S["N2B"]
        rp = b.dma("gpsimd", n2b[:, :].rearrange("p (o n) -> p o n", o=1),
                   bass.AP(n2b.tensor if hasattr(n2b, "tensor") else n2b, 0,
                           [[1, 1], [0, 128], [1, D]]),
                   reads=("N2B",), writes=("N2B",))
        b.wait("tensor", rp)
        b.wait("vector", rp)
        mx = S["tiny"][:, 16:17]
        b.op("vector", lambda E, o=mx: E.memset(o, 0.0), writes=("tiny",))
        tvv = b.barrier("vector", "v")
        b.wait("tensor", tvv)
        for icc in range(8):
            for jt in range(2):
                slot = f"p{jt}"
                self.usepsum("tensor", slot)
                tm = b.op("tensor", lambda E, o=P[slot][:, :], l=S["f1"][0:2, icc * 128:(icc + 1) * 128],
                          r=S["f1"][0:2, jt * 512:(jt + 1) * 512]:
                          E.matmul(o, l, r, start=True, stop=True), inc="t")
                b.wait("vector", tm)
                b.op("vector", lambda E, o=S["f0"][:, jt * 512:(jt + 1) * 512], i=P[slot][:, :],
                     n2j=S["N2B"][:, jt * 512:(jt + 1) * 512]:
                     E.scalar_tensor_tensor(o, i, -2.0, n2j, ALU.mult, ALU.add),
                     reads=("N2B",), writes=("f0",))
                tfree = b.op("vector", lambda E, o=S["f0"][:, jt * 512:(jt + 1) * 512],
                             a=S["nrow"][:, icc:icc + 1]:
                             E.tensor_single_scalar(o, o, a, ALU.add),
                             reads=("f0", "nrow"), writes=("f0",), inc="v")
                self.pfree[slot] = tfree
            b.op("vector", lambda E, o=S["f0"][:, :]: E.tensor_single_scalar(o, o, 0.0, ALU.max),
                 reads=("f0",), writes=("f0",))
            tv2 = b.barrier("vector", "v")
            b.wait("scalar", tv2)
            td = b.op("scalar", lambda E, o=S["f0"][:, :], i=S["f0"][:, :]:
                      E.activation(o, i, AF.Sqrt), reads=("f0",), writes=("f0",), inc="s")
            b.wait("vector", td)
            b.op("vector", lambda E, o=S["tiny"][:, 17:18], i=S["f0"][:, :]:
                 E.reduce_max(o, i, axis=AX), reads=("f0",), writes=("tiny",))
            b.op("vector", lambda E, o=mx, i=S["tiny"][:, 17:18], j=mx:
                 E.tensor_max(o, i, j), reads=("tiny",), writes=("tiny",))
        tv3 = b.barrier("vector", "v")
        b.wait("tensor", tv3)
        self.usepsum("tensor", "p4")
        t4 = b.op("tensor", lambda E, o=P["p4"][0:1, 0:128], i=mx, idm=S["id_f32"][:, :]:
                  E.transpose(o, i, idm), inc="t")
        b.wait("vector", t4)
        b.op("vector", lambda E, o=S["tiny"][0:1, 18:19], i=P["p4"][0:1, 0:128]:
             E.reduce_max(o, i, axis=AX), writes=("tiny",))
        tmx = b.barrier("vector", "v")
        self.pfree["p4"] = tmx
        b.op("vector", lambda E, o=S["tiny"][:, 20:21]: E.memset(o, 0.0), writes=("tiny",))
        b.op("vector", lambda E, o=S["tiny"][0:1, 19:20], i=S["tiny"][0:1, 18:19]:
             E.tensor_single_scalar(o, i, 1e-6, ALU.max), reads=("tiny",), writes=("tiny",))
        b.op("vector", lambda E, o=S["tiny"][0:1, 20:21], i=S["tiny"][0:1, 19:20]:
             E.reciprocal(o, i), reads=("tiny",), writes=("tiny",))
        b.op("vector", lambda E, o=S["tiny"][0:1, 20:21], i=S["tiny"][0:1, 20:21]:
             E.tensor_single_scalar(o, i, -0.1, ALU.mult), reads=("tiny",), writes=("tiny",))
        b.op("vector", lambda E, o=S["tiny"][0:1, 21:22], i=S["tiny"][0:1, 18:19]:
             E.tensor_single_scalar(o, i, 1e-6, ALU.is_gt), reads=("tiny",), writes=("tiny",))
        tf = b.op("vector", lambda E, o=S["tiny"][0:1, 20:21], i=S["tiny"][0:1, 20:21],
                  j=S["tiny"][0:1, 21:22]: E.tensor_mul(o, i, j),
                  reads=("tiny",), writes=("tiny",), inc="v")
        b.wait("tensor", tf)
        self.usepsum("tensor", "p4")
        trp = b.op("tensor", lambda E, o=P["p4"][:, 0:1], l=S["ones_f32"][:, :],
                   r=S["tiny"][:, 20:21]:
                   E.matmul(o, l, r, start=True, stop=True), inc="t")
        b.wait("vector", trp)
        rp2 = b.op("vector", lambda E, o=S["tiny"][:, 22:23], i=P["p4"][:, 0:1]:
                   E.tensor_copy(o, i), writes=("tiny",), inc="v")
        self.pfree["p4"] = rp2
        fcol = S["tiny"][:, 22:23]
        for kt in range(9):
            b.op("vector", lambda E, o=S["B2"][:, kt * 1536:(kt + 1) * 1536]: E.memset(o, 0.0),
                 writes=("B2",))
        tz = b.barrier("vector", "v")
        b.wait("sync", tz)
        for icc in range(8):
            for jt in range(2):
                slot = f"p{jt}"
                self.usepsum("tensor", slot)
                tm = b.op("tensor", lambda E, o=P[slot][:, :], l=S["f1"][0:2, icc * 128:(icc + 1) * 128],
                          r=S["f1"][0:2, jt * 512:(jt + 1) * 512]:
                          E.matmul(o, l, r, start=True, stop=True), inc="t")
                b.wait("vector", tm)
                b.op("vector", lambda E, o=S["f0"][:, jt * 512:(jt + 1) * 512], i=P[slot][:, :],
                     n2j=S["N2B"][:, jt * 512:(jt + 1) * 512]:
                     E.scalar_tensor_tensor(o, i, -2.0, n2j, ALU.mult, ALU.add),
                     reads=("N2B",), writes=("f0",))
                tfree = b.op("vector", lambda E, o=S["f0"][:, jt * 512:(jt + 1) * 512],
                             a=S["nrow"][:, icc:icc + 1]:
                             E.tensor_single_scalar(o, o, a, ALU.add),
                             reads=("f0", "nrow"), writes=("f0",), inc="v")
                self.pfree[slot] = tfree
            b.op("vector", lambda E, o=S["f0"][:, :]: E.tensor_single_scalar(o, o, 0.0, ALU.max),
                 reads=("f0",), writes=("f0",))
            tv2 = b.barrier("vector", "v")
            b.wait("scalar", tv2)
            td = b.op("scalar", lambda E, o=S["f0"][:, :], i=S["f0"][:, :]:
                      E.activation(o, i, AF.Sqrt), reads=("f0",), writes=("f0",), inc="s")
            b.wait("vector", td)
            tb = b.op("vector", lambda E, o=S["f0"][:, :], a=fcol:
                      E.tensor_single_scalar(o, o, a, ALU.mult),
                      reads=("f0", "tiny"), writes=("f0",), inc="v")
            b.wait("gpsimd", tb)
            f0t = S["f0"]
            d1 = b.dma("gpsimd", S["B2"][1:128, icc * 1536 + 1: icc * 1536 + 1 + D],
                       f0t[0:127, :], reads=("f0",), writes=("B2",))
            d2 = b.dma("gpsimd", S["B2"][0:1, (icc + 1) * 1536 + 1: (icc + 1) * 1536 + 1 + D],
                       f0t[127:128, :], reads=("f0",), writes=("B2",))
            b.wait("vector", d1)
            b.wait("vector", d2)

    def _l0_attn_group(self, g):
        nc, b, S, P, d = self.nc, self.b, self.S, self.P, self.d
        self._attn_group_common(
            g, wq=(d["w_l0_qkvT"], 0), wk=(d["w_l0_qkvT"], 1024),
            wv=(d["w_l0_qkvT"], 2048), scale=0.125,
            projT=d["w_l0_projT"], causal=False, biased=True, tag=f"l0g{g}")

    @staticmethod
    def apx(base, offset_elems, dims):
        th = base.tensor
        return bass.AP(th, offset_elems, [list(x) for x in dims])

    def _attn_proj_group(self, g, wq, wk, wv, scale):
        self.mark("_attn_proj_group")
        nc, b, S, P, d = self.nc, self.b, self.S, self.P, self.d
        for (wap, coff), kind in ((wq, "q"), (wk, "k"), (wv, "v")):
            half = {"q": 0, "k": 1, "v": 0}[kind]
            wsl = S["wsl"][:, half * 2048:(half + 1) * 2048]
            src = self.apx(wap, (coff + g * 256), [[wap.shape[1], 128], [128 * wap.shape[1], 8], [1, 256]])
            b.wait("gpsimd", ("t", b.cnt["t"]))
            wt = b.dma("gpsimd", wsl.rearrange("p (dc e) -> p dc e", e=256), src,
                       writes=("wsl",))
            b.wait("tensor", wt)
            if kind in ("q", "k"):
                scr = self.scr["s_QT" if kind == "q" else "s_KT"]
                sc = scale if kind == "q" else 1.0
                for ec in range(2):
                    for qt in range(4):
                        slot = f"p{qt % 2}"
                        ssl = (qt % 2) * 512
                        self.usepsum("tensor", slot)
                        tm = None
                        for dc in range(NDC):
                            tm = b.op("tensor", lambda E, o=P[slot][:, :],
                                      l=wsl[:, dc * 256 + ec * 128: dc * 256 + (ec + 1) * 128],
                                      r=S["xnT"][:, dc * T + qt * 512: dc * T + (qt + 1) * 512],
                                      st=(dc == 0), sp=(dc == NDC - 1):
                                      E.matmul(o, l, r, start=st, stop=sp), inc="t")
                        b.wait("scalar", tm)
                        b.wait("scalar", self.pfree.get(f"stq{qt % 2}"))
                        te = b.op("scalar", lambda E, o=S["stg"][:, ssl:ssl + 512],
                                  i=P[slot][:, :], sc=sc:
                                  E.activation(o, i, AF.Copy, scale=sc),
                                  writes=("stg",), inc="s")
                        self.pfree[slot] = te
                        b.wait("sync", te)
                        self.pfree[f"stq{qt % 2}"] = b.dma(
                            "sync", scr[g * 256 + ec * 128: g * 256 + (ec + 1) * 128,
                                        qt * 512:(qt + 1) * 512],
                            S["stg"][:, ssl:ssl + 512], reads=("stg",))
            else:
                if g == 0:
                    vones = b.op("vector", lambda E, o=self.apx(S["stg"][:, :], 1024 + 64,
                                 [[S["stg"].shape[1], 128], [264, 2], [66, 4], [1, 2]]):
                                 E.memset(o, 1.0), reads=("stg",), writes=("stg",), inc="v")
                    b.wait("scalar", vones)
                    b.wait("sync", vones)
                for c in range(NC):
                    slot = f"p{c % 2}"
                    vof = 1024 + (c % 2) * 264
                    self.usepsum("tensor", slot)
                    tm = None
                    for dc in range(NDC):
                        tm = b.op("tensor", lambda E, o=P[slot][:, 0:256],
                                  l=S["xnT"][:, dc * T + c * 128: dc * T + (c + 1) * 128],
                                  r=wsl[:, dc * 256: (dc + 1) * 256],
                                  st=(dc == 0), sp=(dc == NDC - 1):
                                  E.matmul(o, l, r, start=st, stop=sp), inc="t")
                    b.wait("scalar", tm)
                    b.wait("scalar", self.pfree.get(f"stv{c % 2}"))
                    dst = self.apx(S["stg"][:, :], vof, [[S["stg"].shape[1], 128], [66, 4], [1, 64]])
                    te = b.op("scalar", lambda E, o=dst,
                              i=P[slot][:, 0:256].rearrange("p (h e) -> p h e", e=64):
                              E.copy(o, i), writes=("stg",), inc="s")
                    self.pfree[slot] = te
                    b.wait("sync", te)
                    self.pfree[f"stv{c % 2}"] = b.dma(
                        "sync", self.scr["s_V"][c * 128:(c + 1) * 128,
                                                g * 264:(g + 1) * 264],
                        S["stg"][:, vof:vof + 264], reads=("stg",))

    def _attn_group_common(self, g, wq, wk, wv, scale, projT, causal, biased, tag):
        nc, b, S, P, d = self.nc, self.b, self.S, self.P, self.d
        b.wait("sync", ("t", b.cnt["t"]))
        b.wait("sync", ("s", b.cnt["s"]))
        b.wait("vector", ("t", b.cnt["t"]))
        b.op("vector", lambda E, o=S["QTg"][64:128, :]: E.memset(o, 0.0), writes=("QTg",))
        b.op("vector", lambda E, o=S["KTg"][64:128, :]: E.memset(o, 0.0), writes=("KTg",))
        tms = b.op("vector", lambda E, o=S["KTg"][64:65, :]: E.memset(o, -1.0),
                   writes=("KTg",), inc="v")
        toks = []
        for h in range(4):
            gh4 = g * 4 + h
            toks.append(b.dma("sync", S["QTg"][0:64, h * T:(h + 1) * T],
                              self.scr["s_QT"][gh4 * 64:(gh4 + 1) * 64, :], writes=("QTg",)))
            toks.append(b.dma("sync", S["KTg"][0:64, h * T:(h + 1) * T],
                              self.scr["s_KT"][gh4 * 64:(gh4 + 1) * 64, :], writes=("KTg",)))
        toks.append(b.dma("sync", S["Vg"][:, :].rearrange("p (c w) -> p c w", w=264),
                          self.apx(self.scr["s_V"], g * 264,
                                   [[1056, 128], [128 * 1056, NC], [1, 264]]), writes=("Vg",)))
        for tok in toks:
            b.wait("tensor", tok)
            b.wait("scalar", tok)
            b.wait("vector", tok)
        b.wait("tensor", tms)
        b.wait("scalar", tms)
        stage = getattr(self, "l0_attn_stage", "full")
        if stage == "proj":
            return
        self._attn_mrow(g)
        if stage == "mrow":
            return
        sq = b.barrier("scalar", "s")
        b.wait("tensor", sq)
        nheads = 1 if stage == "head0" else 4
        for gh in range(nheads):
            self._attn_head(g, gh, causal=causal, biased=biased)
        if stage in ("head0", "heads"):
            return
        self._attn_out_proj(g, projT)

    def _attn_out_proj(self, g, projT):
        self.mark("_attn_out_proj")
        nc, b, S, P, d = self.nc, self.b, self.S, self.P, self.d
        ot_rdy = b.barrier("scalar", "s")
        b.wait("tensor", ot_rdy)
        b.wait("gpsimd", ("t", b.cnt["t"]))
        src = self.apx(projT, g * 256 * 1024, [[1024, 128], [128 * 1024, 2], [1, 1024]])
        wt = b.dma("gpsimd", S["wsl"][:, 0:2048].rearrange("p (cc e) -> p cc e", e=1024), src,
                   writes=("wsl",))
        b.wait("tensor", wt)
        ep0 = (g == 0) and getattr(self, "_epi0", None)
        for c in range(NC):
            if ep0:
                mu = S["cols"][:, c:c + 1]
                rs = S["cols"][:, 16 + c:17 + c]
                rs2 = S["cols"][:, 32 + c:33 + c]
                bcol = S["tiny"][:, 24:25]
                b.op("vector", lambda E, o=bcol, a=mu, c2=rs: E.tensor_mul(o, a, c2),
                     reads=("cols",), writes=("tiny",))
                b.op("vector", lambda E, o=bcol: E.tensor_single_scalar(o, o, -1.0, ALU.mult),
                     reads=("tiny",), writes=("tiny",))
                b.op("vector", lambda E, o=S["EP"][:, :], i=self.xc(c), a=rs, bb=bcol:
                     E.tensor_scalar(o, i, a, bb, ALU.mult, ALU.add),
                     reads=("xres", "cols", "tiny"), writes=("EP",))
                b.op("vector", lambda E, o=S["EP"][:, :], a=rs2:
                     E.tensor_single_scalar(o, o, a, ALU.mult),
                     reads=("EP", "cols"), writes=("EP",))
                b.op("vector", lambda E, o=S["f2"][:, :], i=S["EP"][:, :], j=S["f1"][:, :]:
                     E.tensor_mul(o, i, j), reads=("EP", "f1"), writes=("f2",))
                b.op("vector", lambda E, o=S["f2"][:, :], i=S["f2"][:, :], j=S["f0"][:, :]:
                     E.tensor_add(o, i, j), reads=("f2", "f0"), writes=("f2",))
            for half in range(2):
                slot = f"p{half + 2}"
                self.usepsum("tensor", slot)
                tm = None
                for cc in range(2):
                    tm = b.op("tensor", lambda E, o=P[slot][:, :],
                              l=S["OTg"][:, cc * T + c * 128: cc * T + (c + 1) * 128],
                              r=S["wsl"][:, cc * 1024 + half * 512: cc * 1024 + (half + 1) * 512],
                              st=(cc == 0), sp=(cc == 1):
                              E.matmul(o, l, r, start=st, stop=sp), inc="t")
                b.wait("vector", tm)
                xh = self.xc(c)[:, half * 512:(half + 1) * 512]
                if ep0:
                    tv = b.op("vector", lambda E, o=xh, i=P[slot][:, :], j=xh:
                              E.tensor_add(o, i, j), reads=("xres",), writes=("xres",), inc="v")
                    tv = b.op("vector", lambda E, o=xh, i=S["f2"][:, half * 512:(half + 1) * 512], j=xh:
                              E.tensor_add(o, i, j), reads=("xres", "f2"), writes=("xres",), inc="v")
                else:
                    tv = b.op("vector", lambda E, o=xh, i=P[slot][:, :], j=xh:
                              E.tensor_add(o, i, j), reads=("xres",), writes=("xres",), inc="v")
                self.pfree[slot] = tv

    def _attn_mrow(self, g):
        self.mark("_attn_mrow")
        nc, b, S, P, d = self.nc, self.b, self.S, self.P, self.d
        acc = S["tiny"][64:65, 26:30]
        b.op("vector", lambda E, o=acc: E.memset(o, 0.0), writes=("tiny",))
        tz = b.barrier("vector", "v")
        b.wait("scalar", tz)
        for h in range(4):
            for qh in range(2):
                b.wait("scalar", ("t", b.cnt["t"]))
                tq = b.op("scalar", lambda E, o=S["sq"][0:64, :],
                          i=S["KTg"][0:64, h * T + qh * 1024: h * T + (qh + 1) * 1024]:
                          E.activation(o, i, AF.Square), writes=("sq",), inc="s")
                b.wait("tensor", tq)
                for qt2 in range(2):
                    bank = "p4" if (h * 2 + qh + qt2) % 2 == 0 else "p5"
                    self.usepsum("tensor", bank)
                    tm = b.op("tensor", lambda E, o=P[bank][64:65, 0:512], l=S["h2"][0:64, 0:1],
                              r=S["sq"][0:64, qt2 * 512:(qt2 + 1) * 512]:
                              E.matmul(o, l, r, start=True, stop=True,
                                       tile_position=(0, 64)), inc="t")
                    b.wait("vector", tm)
                    b.op("vector", lambda E, o=S["tiny"][64:65, 30:31], i=P[bank][64:65, 0:512]:
                         E.reduce_max(o, i, axis=AX), writes=("tiny",))
                    tv = b.op("vector", lambda E, o=acc[:, h:h + 1], i=S["tiny"][64:65, 30:31],
                              j=acc[:, h:h + 1]: E.tensor_max(o, i, j),
                              reads=("tiny",), writes=("tiny",), inc="v")
                    self.pfree[bank] = tv
        tv = b.barrier("vector", "v")
        b.wait("scalar", tv)
        ts = b.op("scalar", lambda E, o=S["tiny"][64:65, 20:24], i=acc:
                  E.activation(o, i, AF.Sqrt), reads=("tiny",), writes=("tiny",), inc="s")
        mrs_g = [None, None]
        tvm = None
        for h in range(4):
            kmax = S["tiny"][64:65, 20 + h:21 + h]
            for qh in range(2):
                b.wait("scalar", ("t", b.cnt["t"]))
                tq = b.op("scalar", lambda E, o=S["sq"][0:64, :],
                          i=S["QTg"][0:64, h * T + qh * 1024: h * T + (qh + 1) * 1024]:
                          E.activation(o, i, AF.Square), writes=("sq",), inc="s")
                b.wait("tensor", tq)
                for qt2 in range(2):
                    qt = qh * 2 + qt2
                    bank = "p4" if (h * 2 + qh + qt2) % 2 == 0 else "p5"
                    mslot = (h * 2 + qh + qt2) % 2
                    self.usepsum("tensor", bank)
                    tm = b.op("tensor", lambda E, o=P[bank][64:65, 0:512], l=S["h2"][0:64, 0:1],
                              r=S["sq"][0:64, qt2 * 512:(qt2 + 1) * 512]:
                              E.matmul(o, l, r, start=True, stop=True,
                                       tile_position=(0, 64)), inc="t")
                    b.wait("scalar", tm)
                    b.wait("scalar", mrs_g[mslot])
                    msl = S["mrs"][64:65, mslot * 512: mslot * 512 + 512]
                    tsq = b.op("scalar", lambda E, o=msl, i=P[bank][64:65, 0:512]:
                               E.activation(o, i, AF.Sqrt), writes=("mrs",), inc="s")
                    self.pfree[bank] = tsq
                    b.wait("vector", tsq)
                    b.op("vector", lambda E, o=msl, a=kmax:
                         E.tensor_single_scalar(o, o, a, ALU.mult),
                         reads=("mrs", "tiny"), writes=("mrs",))
                    tvm = b.op("vector", lambda E,
                               o=S["QTg"][64:65, h * T + qt * 512: h * T + (qt + 1) * 512],
                               i=msl:
                               E.tensor_scalar(o, i, 1.01, 0.05, ALU.mult, ALU.add),
                               reads=("mrs",), writes=("QTg",), inc="v")
                    mrs_g[mslot] = tvm
        b.wait("tensor", tvm)

    def _attn_head(self, g, gh, causal, biased):
        self.mark("_attn_head")
        nc, b, S, P, d = self.nc, self.b, self.S, self.P, self.d
        ec, hf = gh // 2, gh % 2
        p0 = hf * 64
        ptq_guard = {0: None, 1: None}
        f2g = [None, None]
        for qt in range(4):
            po = (qt % 2) * 8192
            kts = range(NC) if not causal else range(0, qt * 4 + 4)
            b.wait("scalar", ptq_guard[qt % 2])
            b.wait("vector", ptq_guard[qt % 2])
            te = None
            te_kt = {}
            for kt in kts:
                slot = f"p{kt % 2}"
                self.usepsum("tensor", slot)
                tm = b.op("tensor", lambda E, o=P[slot][:, :],
                          l=S["KTg"][:, gh * T + kt * 128: gh * T + (kt + 1) * 128],
                          r=S["QTg"][:, gh * T + qt * 512: gh * T + (qt + 1) * 512]:
                          E.matmul(o, l, r, start=True, stop=True), inc="t")
                dst = S["PTq"][:, po + kt * 512: po + (kt + 1) * 512]
                straddle = causal and (kt >= qt * 4)
                fsl = S["f2"][:, (kt % 2) * 512:(kt % 2) * 512 + 512]
                if biased and (kt <= 8) and (qt <= 2):
                    bsl = S["B2"][:, kt * 1536 + qt * 512: kt * 1536 + (qt + 1) * 512]
                    b.wait("vector", tm)
                    b.wait("vector", f2g[kt % 2])
                    tv = b.op("vector", lambda E, o=fsl, i=P[slot][:, :], j=bsl:
                              E.tensor_add(o, i, j), reads=("B2", "f2"), writes=("f2",), inc="v")
                    b.wait("scalar", tv)
                    te = b.op("scalar", lambda E, o=dst, i=fsl:
                              E.activation(o, i, AF.Exp),
                              reads=("f2",), writes=("PTq",), inc="s")
                    f2g[kt % 2] = te
                elif straddle:
                    sidx = kt - qt * 4
                    b.wait("vector", tm)
                    b.wait("vector", f2g[kt % 2])
                    tv = b.op("vector", lambda E, o=fsl, i=P[slot][:, :],
                              j=S["fills"][:, sidx * 512:(sidx + 1) * 512]:
                              E.tensor_add(o, i, j), reads=("fills", "f2"), writes=("f2",), inc="v")
                    b.wait("scalar", tv)
                    te = b.op("scalar", lambda E, o=dst, i=fsl:
                              E.activation(o, i, AF.Exp),
                              reads=("f2",), writes=("PTq",), inc="s")
                    f2g[kt % 2] = te
                else:
                    b.wait("scalar", tm)
                    te = b.op("scalar", lambda E, o=dst, i=P[slot][:, :]:
                              E.activation(o, i, AF.Exp),
                              writes=("PTq",), inc="s")
                self.pfree[slot] = te
                te_kt[kt] = te
            if getattr(self, "l0_head_stage", "full") == "sp":
                continue
            ktl = list(kts)
            otoks = {}
            for sub in range(4):
                qc = qt * 4 + sub
                bank = f"p{2 + sub}"
                self.usepsum("tensor", bank)
                tm = None
                for kt in ktl:
                    if causal and kt > qc:
                        break
                    b.wait("tensor", te_kt[kt])
                    tm = b.op("tensor", lambda E, o=P[bank][:, 0:65],
                              l=S["PTq"][:, po + kt * 512 + sub * 128: po + kt * 512 + (sub + 1) * 128],
                              r=S["Vg"][:, kt * 264 + gh * 66: kt * 264 + gh * 66 + 65],
                              st=(kt == 0), sp=(kt == (qc if causal else NC - 1)):
                              E.matmul(o, l, r, start=st, stop=sp), inc="t")
                otoks[sub] = tm
            ptq_guard[qt % 2] = tm
            if getattr(self, "l0_head_stage", "full") == "pv":
                continue
            for sub in range(4):
                qc = qt * 4 + sub
                bank = f"p{2 + sub}"
                ps = P[bank]
                b.wait("vector", otoks[sub])
                b.wait("vector", self.pfree.get(f"obf_g{sub % 2}"))
                rec = S["tiny"][:, 35 + sub:36 + sub]
                b.op("vector", lambda E, o=rec, i=ps[:, 64:65]: E.reciprocal(o, i),
                     writes=("tiny",), reads=())
                obs = S["obf"][:, (sub % 2) * 64:(sub % 2) * 64 + 64]
                tv = b.op("vector", lambda E, o=obs, i=ps[:, 0:64], a=rec:
                          E.tensor_single_scalar(o, i, a, ALU.mult),
                          reads=("tiny", "obf"), writes=("obf",), inc="v")
                self.pfree[bank] = tv
                b.wait("tensor", tv)
                bslot = "bt0" if sub % 2 == 0 else "bt1"
                self.usepsum("tensor", bslot)
                tt = b.op("tensor", lambda E, o=P[bslot][p0:p0 + 64, 0:128], i=obs,
                          idm=S["id_bf"][:, :], tp=(0, p0):
                          E.transpose(o, i, idm, tile_position=tp), inc="t")
                self.pfree[f"obf_g{sub % 2}"] = tt
                b.wait("scalar", tt)
                te2 = b.op("scalar", lambda E,
                           o=S["OTg"][p0:p0 + 64, ec * T + qc * 128: ec * T + (qc + 1) * 128],
                           i=P[bslot][p0:p0 + 64, 0:128]: E.copy(o, i),
                           writes=("OTg",), inc="s")
                self.pfree[bslot] = te2

    def _blw_pass1(self, wgT, wuT, wdT):
        b, S = self.b, self.S
        nslab = 0
        for wi, nrows in ((0, 8), (1, 8), (2, 32)):
            wap = (wgT, wuT, wdT)[wi]
            acc = S["tiny"][:, 40 + wi:41 + wi]
            b.op("vector", lambda E, o=acc: E.memset(o, 0.0), writes=("tiny",))
            for rc in range(nrows):
                buf = S["p1w0"] if nslab % 2 == 0 else S["p1w1"]
                bn = "p1w0" if nslab % 2 == 0 else "p1w1"
                b.wait("sync", ("v", b.cnt["v"]) if nslab >= 2 else None)
                nslab += 1
                wtok = b.dma("sync", buf[:, 0:wap.shape[1]],
                             wap[rc * 128:(rc + 1) * 128, :], writes=(bn,))
                b.wait("vector", wtok)
                b.op("vector", lambda E, o=S["tiny"][:, 39:40], i=buf[:, 0:wap.shape[1]]:
                     E.reduce_sum(o, i, axis=AX, apply_absolute_value=True),
                     reads=(bn,), writes=("tiny",))
                b.op("vector", lambda E, o=acc, i=S["tiny"][:, 39:40], j=acc:
                     E.tensor_add(o, i, j), reads=("tiny",), writes=("tiny",), inc="v")

    def _bitlinear_ffn(self, wgT, wuT, wdT, tag):
        self.mark("_bitlinear_ffn")
        nc, b, S, P, d = self.nc, self.b, self.S, self.P, self.d
        with ExitStack() as ctx:
            for nm, sh, dt in (("wf0", [128, 4096], F32), ("wf1", [128, 4096], F32),
                               ("wqb", [128, 4096], BF16), ("wg8", [128, 4096], F32)):
                S[nm] = ctx.enter_context(nc.sbuf_tensor(tag + nm, sh, dt))
            with nc.Block() as blk:
                for wi, (wap, scr_n, nrows) in enumerate(
                        (("_", "s_wgq", 8), ("_", "s_wuq", 8), ("_", "s_wdq", 32))):
                    wap = (wgT, wuT, wdT)[wi]
                    acc = S["tiny"][:, 40 + wi:41 + wi]
                    self.usepsum("tensor", "p5")
                    tm = b.op("tensor", lambda E, o=P["p5"][:, 0:1], l=S["ones_f32"][:, :], r=acc:
                              E.matmul(o, l, r, start=True, stop=True), inc="t")
                    b.wait("vector", tm)
                    nelem = float(nrows * 128 * wap.shape[1])
                    mcol = S["tiny"][:, 44 + wi * 3:45 + wi * 3]
                    b.op("vector", lambda E, o=mcol, i=P["p5"][:, 0:1], ne=nelem:
                         E.tensor_single_scalar(o, i, 1.0 / ne, ALU.mult), writes=("tiny",))
                    b.op("vector", lambda E, o=mcol: E.tensor_single_scalar(o, o, 1e-5, ALU.max),
                         reads=("tiny",), writes=("tiny",))
                    thr = S["tiny"][:, 45 + wi * 3:46 + wi * 3]
                    nthr = S["tiny"][:, 46 + wi * 3:47 + wi * 3]
                    b.op("vector", lambda E, o=thr, i=mcol: E.tensor_single_scalar(o, i, 0.5, ALU.mult),
                         reads=("tiny",), writes=("tiny",))
                    tv = b.op("vector", lambda E, o=nthr, i=mcol: E.tensor_single_scalar(o, i, -0.5, ALU.mult),
                              reads=("tiny",), writes=("tiny",), inc="v")
                    self.pfree["p5"] = tv
                    for rc in range(nrows):
                        buf = S["wf0"] if rc % 2 == 0 else S["wf1"]
                        bn = "wf0" if rc % 2 == 0 else "wf1"
                        b.wait("sync", ("v", b.cnt["v"]))
                        wtok = b.dma("sync", buf[:, 0:wap.shape[1]],
                                     wap[rc * 128:(rc + 1) * 128, :], writes=(bn,))
                        b.wait("vector", wtok)
                        b.wait_dma_all("vector", "sync")
                        b.op("vector", lambda E, o=S["wg8"][:, 0:wap.shape[1]], i=buf[:, 0:wap.shape[1]], a=thr:
                             E.tensor_single_scalar(o, i, a, ALU.is_ge),
                             reads=(bn, "tiny"), writes=("wg8",))
                        tq = b.op("vector", lambda E, o=S["wqb"][:, 0:wap.shape[1]], i=buf[:, 0:wap.shape[1]],
                                  a=nthr, j=S["wg8"][:, 0:wap.shape[1]]:
                                  E.scalar_tensor_tensor(o, i, a, j, ALU.is_le, ALU.subtract),
                                  reads=(bn, "tiny", "wg8"), writes=("wqb",), inc="v")
                        b.wait("sync", tq)
                        if wi < 2:
                            dst = self.apx(self.scr[scr_n], rc * 128 * 512,
                                           [[512, 128], [1024 * 512, 8], [1, 512]])
                            b.dma("sync", dst, S["wqb"][:, 0:4096].rearrange("p (eb e) -> p eb e", e=512),
                                  reads=("wqb",))
                        else:
                            b.dma("sync", self.scr[scr_n][rc * 128:(rc + 1) * 128, :],
                                  S["wqb"][:, 0:1024], reads=("wqb",))
                self.phase_barrier()
                b.flush_into(blk)
        with ExitStack() as ctx:
            for nm, sh, dt in (("zbuf", [128, 4 * 4096], BF16), ("XqTg", [128, 4096], BF16),
                               ("Zqc", [128, 2 * 4096], BF16), ("ZqT2", [128, 4 * 4096], BF16),
                               ("wgsl", [128, 2 * 4096], BF16), ("wusl", [128, 2 * 4096], BF16),
                               ("wdsl", [128, 3 * 1024], BF16),
                               ("fg", [128, 1024], F32), ("qs0", [128, 1024], F32),
                               ("qs1", [128, 512], F32), ("qI", [128, 1024], I32)):
                S[nm] = ctx.enter_context(nc.sbuf_tensor(tag + nm, sh, dt))
            with nc.Block() as blk:
                for grp in range(4):
                    self._bl_group(grp, tag)
                self.phase_barrier()
                b.flush_into(blk)
        for nm in ("wf0", "wf1", "wqb", "wg8", "zbuf", "XqTg", "Zqc", "ZqT2",
                   "wgsl", "wusl", "wdsl", "fg", "qs0", "qs1", "qI"):
            S.pop(nm, None)

    def _bl_group(self, grp, tag):
        self.mark("_bl_group")
        nc, b, S, P, d = self.nc, self.b, self.S, self.P, self.d
        b.wait("sync", ("t", b.cnt["t"]))
        xt = b.dma("sync", S["XqTg"][:, :].rearrange("p (dc e) -> p dc e", e=512),
                   self.apx(self.scr["s_XqT"], grp * 512,
                            [[2048, 128], [128 * 2048, 8], [1, 512]]), writes=("XqTg",))
        b.wait("tensor", xt)
        for qg in range(4):
            c = grp * 4 + qg
            adq = S["cols"][:, 64 + c:65 + c]
            for wi, coff in ((0, 52), (1, 56)):
                mcol = S["tiny"][:, 44 + wi * 3:45 + wi * 3]
                b.op("vector", lambda E, o=S["tiny"][:, coff + qg:coff + qg + 1], i=adq, j=mcol:
                     E.tensor_mul(o, i, j), reads=("cols", "tiny"), writes=("tiny",))
        tsc = b.barrier("vector", "v")
        b.wait("scalar", tsc)
        b.wait("scalar", self.pfree.get("qfg0"))
        b.wait("scalar", self.pfree.get("qfg1"))
        ebt = {}

        def issue_gu(eb):
            if eb >= 8:
                return
            sl = (eb % 2) * 4096
            g_t = b.dma("sync", S["wgsl"][:, sl:sl + 4096].rearrange("p (dc e) -> p dc e", e=512),
                        self.apx(self.scr["s_wgq"], eb * 1024 * 512,
                                 [[512, 128], [128 * 512, 8], [1, 512]]), writes=("wgsl",))
            u_t = b.dma("sync", S["wusl"][:, sl:sl + 4096].rearrange("p (dc e) -> p dc e", e=512),
                        self.apx(self.scr["s_wuq"], eb * 1024 * 512,
                                 [[512, 128], [128 * 512, 8], [1, 512]]), writes=("wusl",))
            ebt[eb] = (g_t, u_t)
        b.wait("sync", ("t", b.cnt["t"]))
        issue_gu(0)
        issue_gu(1)
        for eb in range(8):
            sl = (eb % 2) * 4096
            b.wait("tensor", ebt[eb][0])
            b.wait("tensor", ebt[eb][1])
            for qg in range(4):
                tmg = tmu = None
                self.usepsum("tensor", "p4")
                self.usepsum("tensor", "p5")
                for dc in range(NDC):
                    tmg = b.op("tensor", lambda E, o=P["p4"][:, :],
                               l=S["XqTg"][:, dc * 512 + qg * 128: dc * 512 + (qg + 1) * 128],
                               r=S["wgsl"][:, sl + dc * 512: sl + (dc + 1) * 512],
                               st=(dc == 0), sp=(dc == NDC - 1):
                               E.matmul(o, l, r, start=st, stop=sp), inc="t")
                for dc in range(NDC):
                    tmu = b.op("tensor", lambda E, o=P["p5"][:, :],
                               l=S["XqTg"][:, dc * 512 + qg * 128: dc * 512 + (qg + 1) * 128],
                               r=S["wusl"][:, sl + dc * 512: sl + (dc + 1) * 512],
                               st=(dc == 0), sp=(dc == NDC - 1):
                               E.matmul(o, l, r, start=st, stop=sp), inc="t")
                b.wait("scalar", tmg)
                scg = S["tiny"][:, 52 + qg:53 + qg]
                te = b.op("scalar", lambda E, o=S["fg"][:, 0:512], i=P["p4"][:, :], a=scg:
                          E.activation(o, i, AF.Silu, scale=a),
                          reads=("tiny",), writes=("fg",), inc="s")
                self.pfree["p4"] = te
                b.wait("vector", te)
                b.wait("vector", tmu)
                scu = S["tiny"][:, 56 + qg:57 + qg]
                tv = b.op("vector", lambda E, o=S["zbuf"][:, qg * 4096 + eb * 512: qg * 4096 + (eb + 1) * 512],
                          i=P["p5"][:, :], a=scu, j=S["fg"][:, 0:512]:
                          E.scalar_tensor_tensor(o, i, a, j, ALU.mult, ALU.mult),
                          reads=("tiny", "fg"), writes=("zbuf",), inc="v")
                self.pfree["p5"] = tv
            b.wait("sync", ("t", b.cnt["t"]))
            issue_gu(eb + 2)
        for qg in range(2):
            self._quant_z_chunk(qg, grp * 4 + qg)
            self._transpose_zq(qg, qg)
        ev0 = self._down_pair(grp, 0, defer_evac=True)
        for qg in range(2, 4):
            self._quant_z_chunk(qg, grp * 4 + qg)
            self._transpose_zq(qg, qg)
        ev0()
        self._down_pair(grp, 1)

    def _quant_z_chunk(self, qg, c):
        self.mark("_quant_z_chunk")
        b, S = self.b, self.S
        b.wait("vector", self.pfree.get(f"zqc_g{qg % 2}"))
        z = S["zbuf"][:, qg * 4096:(qg + 1) * 4096]
        rm = S["tiny"][:, 60:61]
        b.op("vector", lambda E, o=rm, i=z: E.reduce_max(o, i, axis=AX, apply_absolute_value=True),
             reads=("zbuf",), writes=("tiny",))
        b.op("vector", lambda E, o=rm: E.tensor_single_scalar(o, rm, 1e-5, ALU.max),
             reads=("tiny",), writes=("tiny",))
        zadq = S["cols"][:, 80 + c:81 + c]
        b.op("vector", lambda E, o=zadq, i=rm: E.tensor_single_scalar(o, i, 1.0 / 127.0, ALU.mult),
             reads=("tiny",), writes=("cols",))
        b.wait("scalar", ("v", b.cnt["v"]))
        scq = S["tiny"][:, 61:62]
        b.op("vector", lambda E, o=S["tiny"][:, 62:63], i=rm: E.reciprocal(o, i),
             reads=("tiny",), writes=("tiny",))
        b.op("vector", lambda E, o=scq, i=S["tiny"][:, 62:63]:
             E.tensor_single_scalar(o, i, 127.0, ALU.mult), reads=("tiny",), writes=("tiny",))
        for sp in range(4):
            tst = {}
            for k in range(2):
                sl = sp * 2 + k
                fsl = S["fg"][:, k * 512:(k + 1) * 512]
                b.wait("scalar", self.pfree.get(f"qfg{k}"))
                tst[k] = b.op("scalar", lambda E, o=fsl, i=z[:, sl * 512:(sl + 1) * 512]:
                              E.copy(o, i), reads=("zbuf",), writes=("fg",), inc="s")
            for k in range(2):
                b.wait("vector", tst[k])
                b.op("vector", lambda E, o=S["qI"][:, k * 512:(k + 1) * 512],
                     i=S["fg"][:, k * 512:(k + 1) * 512], a=scq:
                     E.tensor_single_scalar(o, i, a, ALU.mult),
                     reads=("fg", "tiny"), writes=("qI",))
            for k in range(2):
                tq0 = b.op("vector", lambda E, o=S["qs0"][:, k * 512:(k + 1) * 512],
                           i=S["fg"][:, k * 512:(k + 1) * 512], a=scq,
                           ci=S["qI"][:, k * 512:(k + 1) * 512]:
                           E.scalar_tensor_tensor(o, i, a, ci, ALU.mult, ALU.subtract),
                           reads=("fg", "tiny", "qI"), writes=("qs0",), inc="v")
                self.pfree[f"qfg{k}"] = tq0
            for k in range(2):
                sl = sp * 2 + k
                b.op("vector", lambda E, o=S["qs1"][:, :],
                     i=S["qs0"][:, k * 512:(k + 1) * 512], ci=S["qI"][:, k * 512:(k + 1) * 512]:
                     E.scalar_tensor_tensor(o, i, 0.5, ci, ALU.is_ge, ALU.add),
                     reads=("qs0", "qI"), writes=("qs1",))
                b.op("vector", lambda E,
                     o=S["Zqc"][:, (qg % 2) * 4096 + sl * 512:(qg % 2) * 4096 + (sl + 1) * 512],
                     i=S["qs0"][:, k * 512:(k + 1) * 512], t1=S["qs1"][:, :]:
                     E.scalar_tensor_tensor(o, i, -0.5, t1, ALU.is_le, ALU.subtract),
                     reads=("qs0", "qs1"), writes=("Zqc",))

    def _transpose_zq(self, qg, sub):
        self.mark("_transpose_zq")
        b, S, P = self.b, self.S, self.P
        tz = b.barrier("vector", "v")
        b.wait("tensor", tz)
        zoff = (qg % 2) * 4096
        for blkk in range(8):
            slot = "bt0" if blkk % 2 == 0 else "bt1"
            pt = P[slot]
            self.usepsum("tensor", slot)
            tt = None
            for k in range(4):
                hc = blkk * 4 + k
                tt = b.op("tensor", lambda E, o=pt[:, k * 128:(k + 1) * 128],
                          i=S["Zqc"][:, zoff + hc * 128: zoff + (hc + 1) * 128],
                          idm=S["id_bf"][:, :]:
                          E.transpose(o, i, idm), inc="t")
            b.wait("scalar", tt)
            self.pfree[f"zqc_g{qg % 2}"] = tt
            te = b.op("scalar", lambda E, o=S["ZqT2"][:, sub * 4096 + blkk * 512:(sub * 4096 + (blkk + 1) * 512)],
                      i=pt[:, 0:512]: E.copy(o, i), writes=("ZqT2",), inc="s")
            self.pfree[slot] = te

    def _down_pair(self, grp, pair, defer_evac=False):
        self.mark("_down_pair")
        nc, b, S, P, d = self.nc, self.b, self.S, self.P, self.d
        tz = b.barrier("scalar", "s")
        b.wait("tensor", tz)
        mwd = S["tiny"][:, 50:51]
        toks = [None] * 4
        wdt = {}

        def issue_wd(hc):
            if hc >= 32:
                return
            b.wait("sync", self.pfree.get(f"wd{hc % 3}"))
            wdt[hc] = b.dma("sync", S["wdsl"][:, (hc % 3) * 1024:(hc % 3 + 1) * 1024],
                            self.scr["s_wdq"][hc * 128:(hc + 1) * 128, :], writes=("wdsl",))
        for hh in range(3):
            issue_wd(hh)
        for hc in range(32):
            buf = S["wdsl"][:, (hc % 3) * 1024:(hc % 3 + 1) * 1024]
            b.wait("tensor", wdt[hc])
            for sub in range(2):
                for half in range(2):
                    pslot = f"p{sub * 2 + half}"
                    if hc == 0:
                        self.usepsum("tensor", pslot)
                    tm = b.op("tensor", lambda E, o=P[pslot][:, :],
                              l=S["ZqT2"][:, (pair * 2 + sub) * 4096 + hc * 128:
                                          ((pair * 2 + sub) * 4096 + (hc + 1) * 128)],
                              r=buf[:, half * 512:(half + 1) * 512],
                              st=(hc == 0), sp=(hc == 31):
                              E.matmul(o, l, r, start=st, stop=sp), inc="t")
                    if hc == 31:
                        toks[sub * 2 + half] = tm
            self.pfree[f"wd{hc % 3}"] = tm
            issue_wd(hc + 3)

        def evac():
            for sub in range(2):
                qg = pair * 2 + sub
                c = grp * 4 + qg
                zadq = S["cols"][:, 80 + c:81 + c]
                scd = S["tiny"][:, 43:44]
                b.op("vector", lambda E, o=scd, i=zadq, j=mwd: E.tensor_mul(o, i, j),
                     reads=("cols", "tiny"), writes=("tiny",))
                for half in range(2):
                    pslot = f"p{sub * 2 + half}"
                    b.wait("vector", toks[sub * 2 + half])
                    xh = self.xc(c)[:, half * 512:(half + 1) * 512]
                    tv = b.op("vector", lambda E, o=xh, i=P[pslot][:, :], a=scd, j=xh:
                              E.scalar_tensor_tensor(o, i, a, j, ALU.mult, ALU.add),
                              reads=("tiny", "xres"), writes=("xres",), inc="v")
                    self.pfree[pslot] = tv
        if defer_evac:
            return evac
        evac()


def kernel(**inputs):
    shared = _prep_shared(inputs)
    x = inputs["L0_attn_x"]
    p = Prog()
    nc = p.build()
    in_maps = []
    for core in range(8):
        m = dict(shared)
        m["x"] = np.ascontiguousarray(x[core])
        in_maps.append(m)
    res = run_bass_kernel_spmd(nc, in_maps, core_ids=list(range(8)))
    out = np.stack([np.asarray(r["y"]) for r in res.results], axis=0)
    return out.astype(np.float32)


# version 15 (final)
# speedup vs baseline: 1.0209x; 1.0209x over previous
import numpy as np
import ml_dtypes
from contextlib import ExitStack
import concourse.bass as bass
import concourse.mybir as mybir
from concourse.bass_utils import run_bass_kernel_spmd

F32 = mybir.dt.float32
BF16 = mybir.dt.bfloat16
I32 = mybir.dt.int32
AF = mybir.ActivationFunctionType
ALU = mybir.AluOpType
AX = mybir.AxisListType.X

T = 2048
D = 1024
NC = T // 128
NDC = D // 128
LN_EPS = 1e-5

ENGS = ("sync", "gpsimd", "vector", "scalar", "tensor")


class Bld:
    def __init__(self, nc):
        self.nc = nc
        self.q = {e: [] for e in ENGS}
        self.dmapool = {"sync": [f"ws{i}" for i in range(8)],
                        "gpsimd": [f"gs{i}" for i in range(4)],
                        "st": ["st0", "st1"]}
        self.dmarr = {"sync": 0, "gpsimd": 0, "st": 0}
        allsems = ["t", "v", "s", "g"]
        for v in self.dmapool.values():
            allsems += v
        self.allsems = allsems
        self.cnt = {s: 0 for s in allsems}
        self.lastw = {}
        self.dirty = {e: set() for e in ENGS}
        self.tensor_ops = 0
        self.rdirty = {e: set() for e in ENGS}
        self.sems = {}

    def wait(self, eng, tok):
        if tok is None:
            return
        s, v = tok
        if v <= 0:
            return
        k = (eng, s)
        if self.lastw.get(k, 0) >= v:
            return
        self.lastw[k] = v
        self.q[eng].append(lambda E, s=s, v=v: E.wait_ge(self.sems[s], v))

    def op(self, eng, f, reads=(), writes=(), inc=None, ninc=1, track=True):
        if track:
            rw = set(reads) | set(writes)
            if (rw & self.dirty[eng]) or (set(writes) & self.rdirty[eng]):
                self.dirty[eng].clear()
                self.rdirty[eng].clear()
                self.q[eng].append(lambda E: E.drain())
            self.dirty[eng].update(writes)
            self.rdirty[eng].update(reads)
        if eng == "tensor":
            self.tensor_ops += 1
        if inc is not None:
            self.cnt[inc] += ninc
            v = self.cnt[inc]
            self.q[eng].append(lambda E, f=f, s=inc, n=ninc: f(E).then_inc(self.sems[s], n))
            return (inc, v)
        self.q[eng].append(lambda E, f=f: f(E))
        return None

    def barrier(self, eng, inc):
        self.dirty[eng].clear()
        self.rdirty[eng].clear()
        self.q[eng].append(lambda E: E.drain())
        self.cnt[inc] += 1
        v = self.cnt[inc]
        self.q[eng].append(lambda E, s=inc: E.nop().then_inc(self.sems[s], 1))
        return (inc, v)

    def dma(self, queue, out, in_, reads=(), writes=()):
        pool = self.dmapool[queue]
        sem = pool[self.dmarr[queue] % len(pool)]
        self.dmarr[queue] += 1
        qeng = "sync" if queue in ("sync", "st") else "gpsimd"
        if self.cnt[sem] > 0:
            self.wait(qeng, (sem, self.cnt[sem]))
        return self.op(qeng, lambda E, o=out, i=in_: E.dma_start(out=o, in_=i),
                       reads=reads, writes=writes, inc=sem, ninc=16, track=False)

    def dma_toks(self, queue):
        return [(sm, self.cnt[sm]) for sm in self.dmapool[queue]]

    def wait_dma_all(self, eng, queue):
        for tok in self.dma_toks(queue):
            self.wait(eng, tok)

    def flush_into(self, block):
        q = self.q
        self.q = {e: [] for e in ENGS}

        @block.sync
        def _(E):
            for f in q["sync"]:
                f(E)

        @block.gpsimd
        def _(E):
            for f in q["gpsimd"]:
                f(E)

        @block.vector
        def _(E):
            for f in q["vector"]:
                f(E)

        @block.scalar
        def _(E):
            for f in q["scalar"]:
                f(E)

        @block.tensor
        def _(E):
            for f in q["tensor"]:
                f(E)


def _host_consts():
    c = {}
    c["c_id_bf"] = np.eye(128, dtype=ml_dtypes.bfloat16)
    c["c_id_f32"] = np.eye(128, dtype=np.float32)
    c["c_ones_f32"] = np.ones((128, 128), dtype=np.float32)
    h2 = np.zeros((128, 2), dtype=ml_dtypes.bfloat16)
    h2[0:64, 0] = 1.0
    h2[64:128, 1] = 1.0
    c["c_h2"] = h2
    n1a = np.zeros((2, 128), dtype=ml_dtypes.bfloat16); n1a[0, :] = -1.0
    n1b = np.zeros((2, 128), dtype=ml_dtypes.bfloat16); n1b[1, :] = -1.0
    c["c_neg1a"] = n1a
    c["c_neg1b"] = n1b
    dk = 64
    inv = (1.0 / 10000.0 ** (np.arange(0, dk, 2, dtype=np.float32) / np.float32(dk))).astype(np.float32)
    t = np.arange(T, dtype=np.float32)
    f = (t[:, None] * inv[None, :]).astype(np.float32)
    cos = np.cos(f.astype(np.float64)).astype(np.float32)
    sin = np.sin(f.astype(np.float64)).astype(np.float32)
    CA = np.concatenate([cos, cos], axis=1)
    CB = np.concatenate([-sin, sin], axis=1)
    c["c_ropeA"] = np.ascontiguousarray(np.concatenate([CA, CA], axis=1))
    c["c_ropeB"] = np.ascontiguousarray(np.concatenate([CB, CB], axis=1))
    H, C = 16, 64
    gamma = (1.0 - np.exp2(-5.0 - np.arange(H, dtype=np.float32))).astype(np.float32)
    i = np.arange(C, dtype=np.float32)
    dq = (gamma[:, None] ** (i + 1.0)).astype(np.float32)
    dkv = (gamma[:, None] ** (C - 1.0 - i)).astype(np.float32)
    rel = i[:, None] - i[None, :]
    Dmat = np.where(rel >= 0, gamma[:, None, None] ** np.maximum(rel, 0.0), 0.0).astype(np.float32)
    c["c_dq"] = np.ascontiguousarray(np.tile(dq.T, (2, 1)))
    c["c_dkv"] = np.ascontiguousarray(np.tile(dkv.T, (2, 1)))
    dmT = Dmat.transpose(2, 0, 1).reshape(64, 16 * 64)
    c["c_dmatT"] = np.ascontiguousarray(np.tile(dmT, (2, 1)))
    jj = np.arange(128)[:, None]
    ii = np.arange(128)[None, :]
    c["c_keepji"] = (jj <= ii).astype(np.float32)
    for sidx in range(4):
        w = sidx * 128
        fill = np.zeros((128, 512), dtype=np.float32)
        kk = np.arange(128)[:, None]
        qq = np.arange(512)[None, :]
        fill[kk > qq + (0 - 0) + (w * 0)] = 0.0
        keep = kk <= (qq - w + 0)
        fill = np.where(keep, 0.0, -1e30).astype(np.float32)
        c[f"c_fill{sidx}"] = fill
    return c


def _prep_shared(inp):
    d = {}
    d["w_l0_qkvT"] = np.ascontiguousarray(inp["L0_attn_W_qkv"].T)
    d["w_l0_projT"] = np.ascontiguousarray(inp["L0_attn_W_proj"].T)
    d["w_l0_bpg"] = np.ascontiguousarray(np.broadcast_to(inp["L0_attn_b_proj"][None, :], (128, 1024)))
    d["w_l0_gam"] = np.ascontiguousarray(np.broadcast_to(inp["L0_attn_ln_gamma"][None, :], (128, 1024)))
    d["w_l0_bet"] = np.ascontiguousarray(np.broadcast_to(inp["L0_attn_ln_beta"][None, :], (128, 1024)))
    d["w_l0_cent"] = np.ascontiguousarray(inp["L0_attn_spatial_centers"])
    d["w_l0_wgT"] = np.ascontiguousarray(inp["L0_ffn_w_gate"].T)
    d["w_l0_wuT"] = np.ascontiguousarray(inp["L0_ffn_w_up"].T)
    d["w_l0_wdT"] = np.ascontiguousarray(inp["L0_ffn_w_down"].T)
    d["w_l1_q"] = np.ascontiguousarray(inp["L1_attn_Wq"])
    d["w_l1_k"] = np.ascontiguousarray(inp["L1_attn_Wk"])
    d["w_l1_v"] = np.ascontiguousarray(inp["L1_attn_Wv"])
    d["w_l1_g"] = np.ascontiguousarray(inp["L1_attn_Wg"])
    d["w_l1_o"] = np.ascontiguousarray(inp["L1_attn_Wo"])
    d["w_l1_gn"] = np.ascontiguousarray(np.tile(inp["L1_attn_g_norm_w"][None, :], (128, 2)))
    d["w_l1_f0"] = np.ascontiguousarray(inp["L1_ffn_weight"])
    d["w_l1_f1"] = np.ascontiguousarray(inp["L1_ffn_aux_weight1"])
    d["w_l1_f2"] = np.ascontiguousarray(inp["L1_ffn_aux_weight2"])
    d["w_l1_fb"] = np.ascontiguousarray(inp["L1_ffn_bias"].reshape(8, 128).T)
    d["w_l2_q"] = np.ascontiguousarray(inp["L2_attn_w_q"].transpose(1, 0, 2).reshape(1024, 1024))
    d["w_l2_k"] = np.ascontiguousarray(inp["L2_attn_w_k"].transpose(1, 0, 2).reshape(1024, 1024))
    d["w_l2_v"] = np.ascontiguousarray(inp["L2_attn_w_v"].transpose(1, 0, 2).reshape(1024, 1024))
    d["w_l2_o"] = np.ascontiguousarray(inp["L2_attn_w_o"])
    d["w_l2_wr"] = np.ascontiguousarray(inp["L2_ffn_Wr"])
    d["w_l2_w1"] = np.ascontiguousarray(inp["L2_ffn_W1"].reshape(8, 1024, 12, 128).transpose(0, 2, 1, 3))
    d["w_l2_w3"] = np.ascontiguousarray(inp["L2_ffn_W3"].reshape(8, 1024, 12, 128).transpose(0, 2, 1, 3))
    d["w_l2_w2"] = np.ascontiguousarray(inp["L2_ffn_W2"])
    d["w_l3_q"] = np.ascontiguousarray(inp["L3_attn_Wq"])
    d["w_l3_k"] = np.ascontiguousarray(inp["L3_attn_Wk"])
    d["w_l3_v"] = np.ascontiguousarray(inp["L3_attn_Wv"])
    d["w_l3_o"] = np.ascontiguousarray(inp["L3_attn_Wo"])
    d["w_l3_proj"] = np.ascontiguousarray(inp["L3_attn_proj"])
    d["w_l3_wgT"] = np.ascontiguousarray(inp["L3_ffn_w_gate"].T)
    d["w_l3_wuT"] = np.ascontiguousarray(inp["L3_ffn_w_up"].T)
    d["w_l3_wdT"] = np.ascontiguousarray(inp["L3_ffn_w_down"].T)
    d.update(_host_consts())
    return d


_DRAM_SHAPES = {
    "w_l0_qkvT": ([1024, 3072], F32), "w_l0_projT": ([1024, 1024], F32),
    "w_l0_bpg": ([128, 1024], F32), "w_l0_gam": ([128, 1024], F32),
    "w_l0_bet": ([128, 1024], F32), "w_l0_cent": ([1024, 2], F32),
    "w_l0_wgT": ([1024, 4096], F32), "w_l0_wuT": ([1024, 4096], F32),
    "w_l0_wdT": ([4096, 1024], F32),
    "w_l1_q": ([1024, 1024], F32), "w_l1_k": ([1024, 1024], F32),
    "w_l1_v": ([1024, 2048], F32), "w_l1_g": ([1024, 2048], F32),
    "w_l1_o": ([2048, 1024], F32), "w_l1_gn": ([128, 256], F32),
    "w_l1_f0": ([1024, 1024], F32), "w_l1_f1": ([1024, 1024], F32),
    "w_l1_f2": ([1024, 1024], F32), "w_l1_fb": ([128, 8], F32),
    "w_l2_q": ([1024, 1024], F32), "w_l2_k": ([1024, 1024], F32),
    "w_l2_v": ([1024, 1024], F32), "w_l2_o": ([1024, 1024], F32),
    "w_l2_wr": ([1024, 8], F32),
    "w_l2_w1": ([8, 12, 1024, 128], F32), "w_l2_w3": ([8, 12, 1024, 128], F32),
    "w_l2_w2": ([8, 1536, 1024], F32),
    "w_l3_q": ([1024, 1024], F32), "w_l3_k": ([1024, 1024], F32),
    "w_l3_v": ([1024, 1024], F32), "w_l3_o": ([1024, 1024], F32),
    "w_l3_proj": ([64, 64], F32),
    "w_l3_wgT": ([1024, 4096], F32), "w_l3_wuT": ([1024, 4096], F32),
    "w_l3_wdT": ([4096, 1024], F32),
    "c_ropeA": ([2048, 128], F32), "c_ropeB": ([2048, 128], F32),
    "c_dq": ([128, 16], F32), "c_dkv": ([128, 16], F32),
    "c_dmatT": ([128, 1024], F32),
    "c_keepji": ([128, 128], F32),
    "c_fill0": ([128, 512], F32), "c_fill1": ([128, 512], F32),
    "c_fill2": ([128, 512], F32), "c_fill3": ([128, 512], F32),
    "c_id_bf": ([128, 128], BF16), "c_id_f32": ([128, 128], F32),
    "c_h2": ([128, 2], BF16), "c_neg1a": ([2, 128], BF16), "c_neg1b": ([2, 128], BF16),
    "c_ones_f32": ([128, 128], F32),
}


class Prog:
    def __init__(self, layers=(0, 1, 2, 3)):
        self.layers = layers
        self.nc = bass.Bass("TRN2", target_bir_lowering=False)
        self.b = Bld(self.nc)
        self.d = {}
        self.d["x"] = self.nc.dram_tensor("x", [T, D], F32, kind="ExternalInput").ap()
        for n, (sh, dt) in _DRAM_SHAPES.items():
            self.d[n] = self.nc.dram_tensor(n, sh, dt, kind="ExternalInput").ap()
        self.d_out = self.nc.dram_tensor("y", [T, D], F32, kind="ExternalOutput").ap()
        self.scr = {}
        for n, sh, dt in (("s_XqT", [1024, 2048], BF16),
                          ("s_QT", [1024, 2048], BF16),
                          ("s_KT", [1024, 2048], BF16),
                          ("s_V", [2048, 1056], BF16),
                          ("s_wgq", [8, 1024, 512], BF16),
                          ("s_wuq", [8, 1024, 512], BF16),
                          ("s_wdq", [4096, 1024], BF16)):
            self.scr[n] = self.nc.dram_tensor(n, sh, dt, kind="Internal").ap()
        self.pfree = {}
        self.marks = []

    def mark(self, name):
        self.marks.append((name, self.b.tensor_ops))

    def usepsum(self, eng, slot):
        self.b.wait(eng, self.pfree.get(slot))

    def xc(self, c):
        return self.S["xres"][:, c * D:(c + 1) * D]

    def build(self):
        nc = self.nc
        b = self.b
        with ExitStack() as outer:
            S = {}
            ent = outer.enter_context

            def sbo(name, shape, dt=F32):
                S[name] = ent(nc.sbuf_tensor(name, shape, dt))
            sbo("xres", [128, NC * D])
            sbo("cols", [128, 128])
            sbo("tiny", [128, 64])
            sbo("id_bf", [128, 128], BF16)
            sbo("id_f32", [128, 128])
            sbo("ones_f32", [128, 128])
            sbo("keepji", [128, 128])
            self.S = S
            for s in b.allsems:
                b.sems[s] = ent(nc.semaphore(s))
            P = {}
            for i in range(6):
                P[f"p{i}"] = ent(nc.psum_tensor(f"p{i}", [128, 512], F32))
            P["bt0"] = ent(nc.psum_tensor("bt0", [128, 1024], BF16))
            P["bt1"] = ent(nc.psum_tensor("bt1", [128, 1024], BF16))
            self.P = P

            with nc.Block() as blk:
                for nm, dn in (("id_bf", "c_id_bf"), ("id_f32", "c_id_f32"),
                               ("ones_f32", "c_ones_f32"), ("keepji", "c_keepji")):
                    b.dma("sync", S[nm][:, :], self.d[dn], writes=(nm,))
                for c in range(NC):
                    b.dma("sync", self.xc(c), self.d["x"][c * 128:(c + 1) * 128, :],
                          writes=("xres",))
                b.op("vector", lambda E: E.memset(S["tiny"][:, 63:64], LN_EPS),
                     writes=("tiny",))
                for tok in b.dma_toks("sync"):
                    b.wait("vector", tok)
                    b.wait("scalar", tok)
                    b.wait("tensor", tok)
                    b.wait("gpsimd", tok)
                b.flush_into(blk)

            for li in self.layers:
                self.mark(f"=L{li}")
                getattr(self, f"layer{li}")()
            self.mark("=end")

            with nc.Block() as blk:
                vt = b.barrier("vector", "v")
                st_ = b.barrier("scalar", "s")
                b.wait("sync", vt)
                b.wait("sync", st_)
                last = None
                for c in range(NC):
                    last = b.dma("st", self.d_out[c * 128:(c + 1) * 128, :], self.xc(c),
                                 reads=("xres",))
                b.wait("sync", last)
                b.flush_into(blk)
        return nc

    def phaseA(self, xnT, nXqT=None, want_var2=False, comb=None, wr=None):
        self.mark("phaseA")
        b, S, P = self.b, self.S, self.P
        sc0, sc1 = S["A_sc0"], S["A_sc1"]
        scb = S["A_scb"]
        scI = S["A_scI"]
        if comb is not None:
            wr = S["A_wr"][:, 0:64]
            wr_tok = b.dma("sync", wr.rearrange("p (dc e) -> p dc e", e=8),
                           self.apx(self.d["w_l2_wr"], 0, [[8, 128], [128 * 8, 8], [1, 8]]),
                           writes=("A_wr",))
            b.wait("tensor", wr_tok)
        for c in range(NC):
            xcc = self.xc(c)
            mu = S["cols"][:, c:c + 1]
            rs = S["cols"][:, 16 + c:17 + c]
            t = b.op("vector", lambda E, o=mu, i=xcc: E.reduce_sum(o, i, axis=AX),
                     reads=("xres",), writes=("cols",), inc="v")
            b.op("vector", lambda E, o=mu: E.tensor_single_scalar(o, o, 1.0 / D, ALU.mult),
                 reads=("cols",), writes=("cols",))
            nm = S["tiny"][:, 0:1]
            b.op("vector", lambda E, o=nm, i=mu: E.tensor_single_scalar(o, i, -1.0, ALU.mult),
                 reads=("cols",), writes=("tiny",))
            tv = b.barrier("vector", "v")
            b.wait("scalar", tv)
            ssq = S["tiny"][:, 1:2]
            ts = b.op("scalar", lambda E, o=sc0[:, :], i=xcc, nm=nm, a=ssq:
                      E.activation(o, i, AF.Square, bias=nm, accum_out=a),
                      reads=("xres", "tiny"), writes=("A_sc0", "tiny"), inc="s")
            ts2 = b.op("scalar", lambda E, o=S["tiny"][:, 2:3], i=ssq:
                       E.activation(o, i, AF.Sqrt, scale=1.0 / D, bias=S["tiny"][:, 63:64]),
                       reads=("tiny",), writes=("tiny",), inc="s")
            b.wait("vector", ts2)
            b.op("vector", lambda E, o=rs, i=S["tiny"][:, 2:3]: E.reciprocal(o, i),
                 reads=("tiny",), writes=("cols",))
            bcol = S["tiny"][:, 3:4]
            b.op("vector", lambda E, o=bcol, a=mu, c2=rs: E.tensor_mul(o, a, c2),
                 reads=("cols",), writes=("tiny",))
            b.op("vector", lambda E, o=bcol: E.tensor_single_scalar(o, o, -1.0, ALU.mult),
                 reads=("tiny",), writes=("tiny",))
            tx = b.op("vector", lambda E, o=sc1[:, :], i=xcc, a=rs, bb=bcol:
                      E.tensor_scalar(o, i, a, bb, ALU.mult, ALU.add),
                      reads=("xres", "cols", "tiny"), writes=("A_sc1",), inc="v")
            tb = b.op("vector", lambda E, o=scb[:, 0:D], i=sc1[:, :]: E.tensor_copy(o, i),
                      reads=("A_sc1",), writes=("A_scb",), inc="v")
            if want_var2:
                b.wait("scalar", tx)
                ss2 = S["tiny"][:, 4:5]
                t5 = b.op("scalar", lambda E, o=sc0[:, :], i=sc1[:, :], a=ss2:
                          E.activation(o, i, AF.Square, accum_out=a),
                          reads=("A_sc1",), writes=("A_sc0", "tiny"), inc="s")
                t6 = b.op("scalar", lambda E, o=S["tiny"][:, 5:6], i=ss2:
                          E.activation(o, i, AF.Sqrt, scale=1.0 / D, bias=S["tiny"][:, 63:64]),
                          reads=("tiny",), writes=("tiny",), inc="s")
                b.wait("vector", t6)
                rs2 = S["cols"][:, 32 + c:33 + c]
                b.op("vector", lambda E, o=rs2, i=S["tiny"][:, 5:6]: E.reciprocal(o, i),
                     reads=("tiny",), writes=("cols",))
            if nXqT is not None:
                rm = S["tiny"][:, 6:7]
                b.op("vector", lambda E, o=rm, i=sc1[:, :]:
                     E.reduce_max(o, i, axis=AX, apply_absolute_value=True),
                     reads=("A_sc1",), writes=("tiny",))
                b.op("vector", lambda E, o=rm: E.tensor_single_scalar(o, rm, 1e-5, ALU.max),
                     reads=("tiny",), writes=("tiny",))
                adq = S["cols"][:, 64 + c:65 + c]
                b.op("vector", lambda E, o=adq, i=rm: E.tensor_single_scalar(o, i, 1.0 / 127.0, ALU.mult),
                     reads=("tiny",), writes=("cols",))
                scq = S["cols"][:, 48 + c:49 + c]
                b.op("vector", lambda E, o=S["tiny"][:, 7:8], i=rm: E.reciprocal(o, i),
                     reads=("tiny",), writes=("tiny",))
                b.op("vector", lambda E, o=scq, i=S["tiny"][:, 7:8]:
                     E.tensor_single_scalar(o, i, 127.0, ALU.mult),
                     reads=("tiny",), writes=("cols",))
                b.op("vector", lambda E, o=scI[:, :], i=sc1[:, :], a=scq:
                     E.tensor_single_scalar(o, i, a, ALU.mult),
                     reads=("A_sc1", "cols"), writes=("A_scI",))
                b.op("vector", lambda E, o=sc0[:, :], i=sc1[:, :], a=scq, ci=scI[:, :]:
                     E.scalar_tensor_tensor(o, i, a, ci, ALU.mult, ALU.subtract),
                     reads=("A_sc1", "cols", "A_scI"), writes=("A_sc0",))
                b.op("vector", lambda E, o=sc1[:, :], i=sc0[:, :], ci=scI[:, :]:
                     E.scalar_tensor_tensor(o, i, 0.5, ci, ALU.is_ge, ALU.add),
                     reads=("A_sc0", "A_scI"), writes=("A_sc1",))
                tq = b.op("vector", lambda E, o=scb[:, D:2 * D], i=sc0[:, :], t1=sc1[:, :]:
                          E.scalar_tensor_tensor(o, i, -0.5, t1, ALU.is_le, ALU.subtract),
                          reads=("A_sc0", "A_sc1"), writes=("A_scb",), inc="v")
            if comb is not None:
                self._router_chunk(c, sc1, comb, wr)
            self._transpose_chunk_to_T(scb[:, 0:D], "A_scb", xnT, c, tb)
            if nXqT is not None:
                self._transpose_chunk_to_T(scb[:, D:2 * D], "A_scb", nXqT, c, tq)

    def _transpose_chunk_to_T(self, src_ap, src_name, dstT, c, ready_tok):
        self.mark("_transpose_chunk_to_T")
        b, S, P = self.b, self.S, self.P
        dname, dten = dstT
        b.wait("tensor", ready_tok)
        for half in range(2):
            slot = "bt0" if half == 0 else "bt1"
            pt = P[slot]
            self.usepsum("tensor", slot)
            tt = None
            for k in range(4):
                dc = half * 4 + k
                tt = b.op("tensor", lambda E, o=pt[:, k * 128:(k + 1) * 128],
                          i=src_ap[:, dc * 128:(dc + 1) * 128], idm=S["id_bf"][:, :]:
                          E.transpose(o, i, idm), inc="t")
            b.wait("scalar", tt)
            dst = dten[:, :].rearrange("p (dc t) -> p dc t", t=T)[:, half * 4:(half + 1) * 4,
                                                                 c * 128:(c + 1) * 128]
            te = b.op("scalar", lambda E, o=dst, i=pt[:, 0:512].rearrange("p (k f) -> p k f", f=128):
                      E.copy(o, i), writes=(dname,), inc="s")
            self.pfree[slot] = te

    def _router_chunk(self, c, xn_sc1, comb, wr):
        self.mark("_router_chunk")
        b, S, P = self.b, self.S, self.P
        tx = b.barrier("vector", "v")
        b.wait("tensor", tx)
        tpos = S["A_rt"]
        for dc in range(NDC):
            self.usepsum("tensor", "p5")
            tt = b.op("tensor", lambda E, o=P["p5"][:, 0:128], i=xn_sc1[:, dc * 128:(dc + 1) * 128],
                      idm=S["id_f32"][:, :]: E.transpose(o, i, idm), inc="t")
            b.wait("scalar", tt)
            te = b.op("scalar", lambda E, o=tpos[:, dc * 128:(dc + 1) * 128], i=P["p5"][:, 0:128]:
                      E.copy(o, i), writes=("A_rt",), inc="s")
            self.pfree["p5"] = te
        tsc = b.barrier("scalar", "s")
        b.wait("tensor", tsc)
        self.usepsum("tensor", "p4")
        tl = None
        for dc in range(NDC):
            tl = b.op("tensor", lambda E, o=P["p4"][:, 0:8], l=tpos[:, dc * 128:(dc + 1) * 128],
                      r=wr[:, dc * 8:(dc + 1) * 8], st=(dc == 0), sp=(dc == NDC - 1):
                      E.matmul(o, l, r, start=st, stop=sp), inc="t")
        b.wait("vector", tl)
        l8 = S["A_mrow"][:, 0:8]
        tcp = b.op("vector", lambda E, o=l8, i=P["p4"][:, 0:8]: E.tensor_copy(o, i),
                   writes=("A_mrow",), inc="v")
        self.pfree["p4"] = tcp
        m1 = S["tiny"][:, 8:9]
        b.op("vector", lambda E, o=m1, i=l8: E.reduce_max(o, i, axis=AX),
             reads=("A_mrow",), writes=("tiny",))
        t8 = S["A_mrow"][:, 8:16]
        b.op("vector", lambda E, o=t8, i=l8, a=m1: E.tensor_single_scalar(o, i, a, ALU.is_equal),
             reads=("A_mrow", "tiny"), writes=("A_mrow",))
        b.op("vector", lambda E, o=t8, i=t8, l=l8: E.scalar_tensor_tensor(o, i, -1e30, l, ALU.mult, ALU.add),
             reads=("A_mrow",), writes=("A_mrow",))
        m2 = S["tiny"][:, 9:10]
        b.op("vector", lambda E, o=m2, i=t8: E.reduce_max(o, i, axis=AX),
             reads=("A_mrow",), writes=("tiny",))
        nm1 = S["tiny"][:, 10:11]
        b.op("vector", lambda E, o=nm1, i=m1: E.tensor_single_scalar(o, i, -1.0, ALU.mult),
             reads=("tiny",), writes=("tiny",))
        tv2 = b.barrier("vector", "v")
        b.wait("scalar", tv2)
        e8 = S["A_mrow"][:, 16:24]
        ts3 = b.op("scalar", lambda E, o=e8, i=l8, bb=nm1: E.activation(o, i, AF.Exp, bias=bb),
                   reads=("A_mrow", "tiny"), writes=("A_mrow",), inc="s")
        b.wait("vector", ts3)
        g8 = S["A_mrow"][:, 24:32]
        b.op("vector", lambda E, o=g8, i=l8, a=m2: E.tensor_single_scalar(o, i, a, ALU.is_ge),
             reads=("A_mrow", "tiny"), writes=("A_mrow",))
        b.op("vector", lambda E, o=e8, i=e8, j=g8: E.tensor_mul(o, i, j),
             reads=("A_mrow",), writes=("A_mrow",))
        zs = S["tiny"][:, 11:12]
        b.op("vector", lambda E, o=zs, i=e8: E.reduce_sum(o, i, axis=AX),
             reads=("A_mrow",), writes=("tiny",))
        rz = S["tiny"][:, 12:13]
        b.op("vector", lambda E, o=rz, i=zs: E.reciprocal(o, i),
             reads=("tiny",), writes=("tiny",))
        b.op("vector", lambda E, o=comb[:, c * 8:(c + 1) * 8], i=e8, a=rz:
             E.tensor_single_scalar(o, i, a, ALU.mult),
             reads=("A_mrow", "tiny"), writes=("comb",))

    def mmsep(self, base):
        b = self.b
        last = getattr(self, "_lastbase", None)
        if last is not None and base is not None and {last, base} == {0, 64}:
            self.usepsum("tensor", "p2")
            b.op("tensor", lambda E: E.matmul(self.P["p2"][:, 504:512],
                                              self.S["id_bf"][:, :],
                                              self.S["id_bf"][:, 0:8],
                                              start=True, stop=True))
        self._lastbase = base

    def phase_barrier(self):
        b = self.b
        toks = b.dma_toks("sync") + b.dma_toks("gpsimd") + b.dma_toks("st")
        toks.append(b.barrier("vector", "v"))
        toks.append(b.barrier("scalar", "s"))
        toks.append(b.barrier("tensor", "t"))
        toks.append(b.barrier("gpsimd", "g"))
        for e in ENGS:
            own = {"vector": "v", "scalar": "s", "tensor": "t", "gpsimd": "g"}.get(e)
            for tok in toks:
                if tok[0] != own:
                    b.wait(e, tok)
        self.pfree = {}
        return toks

    def layer0(self):
        nc, b, S, P, d = self.nc, self.b, self.S, self.P, self.d
        parts = getattr(self, "l0_parts", ("A", "bias", "attn", "ffn"))
        with ExitStack() as lctx:
            S["xnT"] = lctx.enter_context(nc.sbuf_tensor("l0_xnT", [128, NDC * T], BF16))
            xnT = ("xnT", S["xnT"])
            with ExitStack() as actx:
                for nm, sh, dt in (("A_sc0", [128, D], F32), ("A_sc1", [128, D], F32),
                                   ("A_scb", [128, 2 * D], BF16), ("A_scI", [128, D], I32),
                                   ("A_mrow", [128, 32], F32),
                                   ("nXqT", [128, NDC * T], BF16)):
                    S[nm] = actx.enter_context(nc.sbuf_tensor("l0" + nm, sh, dt))
                with nc.Block() as blk:
                    if "A" in parts:
                        self.phaseA(xnT, nXqT=("nXqT", S["nXqT"]), want_var2=True)
                    st = b.barrier("scalar", "s")
                    b.wait("sync", st)
                    for dc in range(NDC):
                        b.dma("sync", self.scr["s_XqT"][dc * 128:(dc + 1) * 128, :],
                              S["nXqT"][:, dc * T:(dc + 1) * T], reads=("nXqT",))
                    self.phase_barrier()
                    b.flush_into(blk)
            if "attn" in parts:
                with ExitStack() as actx:
                    for nm, sh, dt in (("wsl", [128, 2 * 2048], BF16),
                                       ("stg", [128, 2048], BF16)):
                        S[nm] = actx.enter_context(nc.sbuf_tensor("l0p" + nm, sh, dt))
                    with nc.Block() as blk:
                        for g in range(4):
                            self._attn_proj_group(
                                g, wq=(d["w_l0_qkvT"], 0), wk=(d["w_l0_qkvT"], 1024),
                                wv=(d["w_l0_qkvT"], 2048), scale=0.125)
                        self.phase_barrier()
                        b.flush_into(blk)
        self._epi0 = True
        with ExitStack() as actx:
            for nm, sh, dt in (("B2", [128, 9 * 1536], BF16),
                               ("QTg", [128, 4 * T], BF16), ("KTg", [128, 4 * T], BF16),
                               ("Vg", [128, NC * 264], BF16),
                               ("PTq", [128, 2 * 16 * 512], BF16),
                               ("OTg", [128, 2 * T], BF16),
                               ("wsl", [128, 2 * 2048], BF16),
                               ("f0", [128, D], F32), ("f1", [128, D], F32),
                               ("f2", [128, D], F32), ("EP", [128, D], F32),
                               ("nrow", [128, 16], F32),
                               ("h2", [128, 2], BF16),
                               ("mrs", [128, 1024], F32),
                               ("obf", [128, 128], BF16), ("sq", [128, 1024], BF16)):
                S[nm] = actx.enter_context(nc.sbuf_tensor("l0" + nm, sh, dt))
            S["N2B"] = S["EP"]
            with nc.Block() as blk:
                b.dma("sync", S["h2"][:, :], self.d["c_h2"], writes=("h2",))
                if "bias" in parts:
                    self._l0_build_bias()
                self.phase_barrier()
                t0 = b.dma("sync", S["f0"][:, :], d["w_l0_bpg"], writes=("f0",))
                t1 = b.dma("sync", S["f2"][:, :], d["w_l0_bet"], writes=("f2",))
                t2 = b.dma("sync", S["f1"][:, :], d["w_l0_gam"], writes=("f1",))
                b.wait_dma_all("vector", "sync")
                b.op("vector", lambda E, o=S["f0"][:, :], i=S["f0"][:, :], j=S["f2"][:, :]:
                     E.tensor_add(o, i, j), reads=("f0", "f2"), writes=("f0",))
                if "attn" in parts:
                    for g in range(4):
                        self._l0_attn_group(g)
                self.phase_barrier()
                b.flush_into(blk)
        self._epi0 = False
        if "ffn" in parts:
            self._bitlinear_ffn(
                wgT=d["w_l0_wgT"], wuT=d["w_l0_wuT"], wdT=d["w_l0_wdT"], tag="l0f")
        S.pop("xnT", None)


    def layer2(self):
        nc, b, S, P, d = self.nc, self.b, self.S, self.P, self.d
        with ExitStack() as lctx:
            S["xnT"] = lctx.enter_context(nc.sbuf_tensor("l2_xnT", [128, NDC * T], BF16))
            S["comb"] = lctx.enter_context(nc.sbuf_tensor("l2_comb", [128, NC * 8], F32))
            xnT = ("xnT", S["xnT"])
            with ExitStack() as actx:
                for nm, sh, dt in (("A_sc0", [128, D], F32), ("A_sc1", [128, D], F32),
                                   ("A_scb", [128, 2 * D], BF16), ("A_scI", [128, D], I32),
                                   ("A_mrow", [128, 32], F32), ("A_wr", [128, 64], F32),
                                   ("A_rt", [128, D], F32)):
                    S[nm] = actx.enter_context(nc.sbuf_tensor("l2" + nm, sh, dt))
                with nc.Block() as blk:
                    self.phaseA(xnT, comb=S["comb"])
                    self.phase_barrier()
                    b.flush_into(blk)
            with ExitStack() as actx:
                for nm, sh, dt in (("wsl", [128, 2 * 2048], BF16),
                                   ("stg", [128, 2048], BF16)):
                    S[nm] = actx.enter_context(nc.sbuf_tensor("l2p" + nm, sh, dt))
                with nc.Block() as blk:
                    for g in range(4):
                        self._attn_proj_group(
                            g, wq=(d["w_l2_q"], 0), wk=(d["w_l2_k"], 0),
                            wv=(d["w_l2_v"], 0), scale=0.125)
                    self.phase_barrier()
                    b.flush_into(blk)
            with ExitStack() as actx:
                for nm, sh, dt in (("QTg", [128, 4 * T], BF16), ("KTg", [128, 4 * T], BF16),
                                   ("Vg", [128, NC * 264], BF16),
                                   ("PTq", [128, 2 * 16 * 512], BF16),
                                   ("OTg", [128, 2 * T], BF16),
                                   ("wsl", [128, 2 * 2048], BF16),
                                   ("f2", [128, D], F32),
                                   ("fills", [128, 4 * 512], F32),
                                   ("h2", [128, 2], BF16),
                                   ("mrs", [128, 1024], F32),
                                   ("obf", [128, 128], BF16), ("sq", [128, 1024], BF16)):
                    S[nm] = actx.enter_context(nc.sbuf_tensor("l2" + nm, sh, dt))
                with nc.Block() as blk:
                    b.dma("sync", S["h2"][:, :], self.d["c_h2"], writes=("h2",))
                    for i in range(4):
                        b.dma("sync", S["fills"][:, i * 512:(i + 1) * 512],
                              self.d[f"c_fill{i}"], writes=("fills",))
                    for eng in ("vector", "scalar", "tensor", "gpsimd"):
                        b.wait_dma_all(eng, "sync")
                    for g in range(4):
                        self._attn_group_common(
                            g, wq=None, wk=None, wv=None, scale=0.125,
                            projT=d["w_l2_o"], causal=True, biased=False,
                            tag=f"l2g{g}")
                    self.phase_barrier()
                    b.flush_into(blk)
            with ExitStack() as actx:
                for nm, sh, dt in (("zT", [128, 12 * T], BF16),
                                   ("W2b", [128, 12 * 1024], BF16),
                                   ("w1s", [128, 3 * 8 * 128], BF16),
                                   ("w3s", [128, 3 * 8 * 128], BF16),
                                   ("fg", [128, 512], F32)):
                    S[nm] = actx.enter_context(nc.sbuf_tensor("l2m" + nm, sh, dt))
                with nc.Block() as blk:
                    for e in range(8):
                        self._moe_expert(e)
                    self.phase_barrier()
                    b.flush_into(blk)
        for nm in ("xnT", "comb"):
            S.pop(nm, None)

    def _moe_expert(self, e):
        self.mark("_moe_expert")
        nc, b, S, P, d = self.nc, self.b, self.S, self.P, self.d
        b.wait("gpsimd", ("t", b.cnt["t"]))
        w2t = b.dma("gpsimd", S["W2b"][:, :].rearrange("p (hc e2) -> p hc e2", e2=1024),
                    self.apx(d["w_l2_w2"], e * 1536 * 1024,
                             [[1024, 128], [128 * 1024, 12], [1, 1024]]), writes=("W2b",))
        zt_guard = self.pfree.get("l2_zt")
        wtoks = {}

        def issue13(hc):
            if hc >= 12:
                return
            sl = (hc % 3) * 1024
            t1 = b.dma("gpsimd", S["w1s"][:, sl:sl + 1024].rearrange("p (dc e2) -> p dc e2", e2=128),
                       self.apx(d["w_l2_w1"], (e * 12 + hc) * 1024 * 128,
                                [[128, 128], [128 * 128, 8], [1, 128]]), writes=("w1s",))
            t3 = b.dma("gpsimd", S["w3s"][:, sl:sl + 1024].rearrange("p (dc e2) -> p dc e2", e2=128),
                       self.apx(d["w_l2_w3"], (e * 12 + hc) * 1024 * 128,
                                [[128, 128], [128 * 128, 8], [1, 128]]), writes=("w3s",))
            wtoks[hc] = (t1, t3)
        b.wait("gpsimd", ("t", b.cnt["t"]))
        issue13(0)
        issue13(1)
        issue13(2)
        for hc in range(12):
            sl = (hc % 3) * 1024
            b.wait("tensor", wtoks[hc][0])
            b.wait("tensor", wtoks[hc][1])
            for qt in range(4):
                self.usepsum("tensor", "p4")
                self.usepsum("tensor", "p5")
                th = tg = None
                for dc in range(NDC):
                    th = b.op("tensor", lambda E, o=P["p4"][:, :],
                              l=S["w1s"][:, sl + dc * 128: sl + (dc + 1) * 128],
                              r=S["xnT"][:, dc * T + qt * 512: dc * T + (qt + 1) * 512],
                              st=(dc == 0), sp=(dc == NDC - 1):
                              E.matmul(o, l, r, start=st, stop=sp), inc="t")
                for dc in range(NDC):
                    tg = b.op("tensor", lambda E, o=P["p5"][:, :],
                              l=S["w3s"][:, sl + dc * 128: sl + (dc + 1) * 128],
                              r=S["xnT"][:, dc * T + qt * 512: dc * T + (qt + 1) * 512],
                              st=(dc == 0), sp=(dc == NDC - 1):
                              E.matmul(o, l, r, start=st, stop=sp), inc="t")
                b.wait("scalar", th)
                b.wait("scalar", zt_guard)
                tsl = b.op("scalar", lambda E, o=S["fg"][:, :], i=P["p4"][:, :]:
                           E.activation(o, i, AF.Silu), writes=("fg",), inc="s")
                self.pfree["p4"] = tsl
                b.wait("vector", tsl)
                b.wait("vector", tg)
                b.wait("vector", zt_guard)
                tz = b.op("vector", lambda E, o=S["zT"][:, hc * T + qt * 512: hc * T + (qt + 1) * 512],
                          i=S["fg"][:, :], j=P["p5"][:, :]:
                          E.tensor_mul(o, i, j), reads=("fg",), writes=("zT",), inc="v")
                self.pfree["p5"] = tz
            b.wait("gpsimd", ("t", b.cnt["t"]))
            issue13(hc + 3)
        tzb = b.barrier("vector", "v")
        b.wait("tensor", tzb)
        b.wait("tensor", w2t)
        lastmm = None
        for qc in range(NC):
            combcol = S["comb"][:, qc * 8 + e: qc * 8 + e + 1]
            for half in range(2):
                slot = f"p{half}"
                self.usepsum("tensor", slot)
                tm = None
                for hc in range(12):
                    tm = b.op("tensor", lambda E, o=P[slot][:, :],
                              l=S["zT"][:, hc * T + qc * 128: hc * T + (qc + 1) * 128],
                              r=S["W2b"][:, hc * 1024 + half * 512: hc * 1024 + (half + 1) * 512],
                              st=(hc == 0), sp=(hc == 11):
                              E.matmul(o, l, r, start=st, stop=sp), inc="t")
                lastmm = tm
                b.wait("vector", tm)
                xh = self.xc(qc)[:, half * 512:(half + 1) * 512]
                tv = b.op("vector", lambda E, o=xh, i=P[slot][:, :], a=combcol, j=xh:
                          E.scalar_tensor_tensor(o, i, a, j, ALU.mult, ALU.add),
                          reads=("comb", "xres"), writes=("xres",), inc="v")
                self.pfree[slot] = tv
        self.pfree["l2_zt"] = lastmm




    def layer3(self):
        nc, b, S, P, d = self.nc, self.b, self.S, self.P, self.d
        with ExitStack() as lctx:
            S["xnT"] = lctx.enter_context(nc.sbuf_tensor("l3_xnT", [128, NDC * T], BF16))
            xnT = ("xnT", S["xnT"])
            with ExitStack() as actx:
                for nm, sh, dt in (("A_sc0", [128, D], F32), ("A_sc1", [128, D], F32),
                                   ("A_scb", [128, 2 * D], BF16), ("A_scI", [128, D], I32),
                                   ("nXqT", [128, NDC * T], BF16)):
                    S[nm] = actx.enter_context(nc.sbuf_tensor("l3" + nm, sh, dt))
                with nc.Block() as blk:
                    self.phaseA(xnT, nXqT=("nXqT", S["nXqT"]))
                    st = b.barrier("scalar", "s")
                    b.wait("sync", st)
                    for dc in range(NDC):
                        b.dma("sync", self.scr["s_XqT"][dc * 128:(dc + 1) * 128, :],
                              S["nXqT"][:, dc * T:(dc + 1) * T], reads=("nXqT",))
                    self.phase_barrier()
                    b.flush_into(blk)
            with ExitStack() as actx:
                for nm, sh, dt in (("wsl", [128, 2 * 2048], BF16),
                                   ("stg", [128, 2048], BF16)):
                    S[nm] = actx.enter_context(nc.sbuf_tensor("l3p" + nm, sh, dt))
                with nc.Block() as blk:
                    for g in range(4):
                        self._attn_proj_group(
                            g, wq=(d["w_l3_q"], 0), wk=(d["w_l3_k"], 0),
                            wv=(d["w_l3_v"], 0), scale=1.0)
                    self.phase_barrier()
                    b.flush_into(blk)
            with ExitStack() as actx:
                for nm, sh, dt in (("QTg", [128, 2 * T], BF16), ("KTg", [128, 2 * T], BF16),
                                   ("Vg", [128, NC * 264], BF16),
                                   ("OTg", [128, 2 * T], BF16),
                                   ("wsl", [128, 2 * 2048], BF16),
                                   ("f0", [128, 1024], F32), ("f1", [128, 1024], F32),
                                   ("pj", [128, 64], BF16),
                                   ("A2", [128, 128], BF16),
                                   ("fkt", [128, 4 * 64], BF16),
                                   ("KVf", [128, 2 * 72], F32), ("KVb", [128, 2 * 72], BF16),
                                   ("obf", [128, 128], BF16)):
                    S[nm] = actx.enter_context(nc.sbuf_tensor("l3" + nm, sh, dt))
                with nc.Block() as blk:
                    pt = b.dma("gpsimd", S["pj"][0:64, :], d["w_l3_proj"], writes=("pj",))
                    pt2 = b.dma("gpsimd", S["pj"][64:128, :], d["w_l3_proj"], writes=("pj",))
                    b.wait("tensor", pt)
                    b.wait("tensor", pt2)
                    for g in range(4):
                        self._l3_attn_group(g)
                    self.phase_barrier()
                    b.flush_into(blk)
        self._bitlinear_ffn(
            wgT=d["w_l3_wgT"], wuT=d["w_l3_wuT"], wdT=d["w_l3_wdT"], tag="l3f")
        S.pop("xnT", None)

    def _l3_attn_group(self, g):
        self.mark("_l3_attn_group")
        nc, b, S, P, d = self.nc, self.b, self.S, self.P, self.d
        self._lastbase = None
        b.wait("sync", ("t", b.cnt["t"]))
        b.wait("sync", ("s", b.cnt["s"]))
        tq = b.dma("sync", S["QTg"][:, :].rearrange("p (ec t) -> p ec t", t=T),
                   self.apx(self.scr["s_QT"], g * 256 * T,
                            [[T, 128], [128 * T, 2], [1, T]]), writes=("QTg",))
        tk = b.dma("sync", S["KTg"][:, :].rearrange("p (ec t) -> p ec t", t=T),
                   self.apx(self.scr["s_KT"], g * 256 * T,
                            [[T, 128], [128 * T, 2], [1, T]]), writes=("KTg",))
        tv_ = b.dma("sync", S["Vg"][:, :].rearrange("p (c w) -> p c w", w=264),
                    self.apx(self.scr["s_V"], g * 264,
                             [[1056, 128], [128 * 1056, NC], [1, 264]]), writes=("Vg",))
        for tok in (tq, tk, tv_):
            b.wait("tensor", tok)
            b.wait("scalar", tok)
            b.wait("vector", tok)
        for name in ("QTg", "KTg"):
            dstT = S[name]
            for ec in range(2):
                for qt in range(4):
                    slot = f"p{qt % 2}"
                    self.usepsum("tensor", slot)
                    tm = None
                    for hf in range(2):
                        self.mmsep(hf * 64)
                        tm = b.op("tensor", lambda E, o=P[slot][hf * 64:(hf + 1) * 64, :],
                                  l=S["pj"][hf * 64:(hf + 1) * 64, :],
                                  r=dstT[hf * 64:(hf + 1) * 64, ec * T + qt * 512: ec * T + (qt + 1) * 512],
                                  tp=(hf * 64, hf * 64):
                                  E.matmul(o, l, r, start=True, stop=True,
                                           tile_position=tp), inc="t")
                    b.wait("vector", tm)
                    b.wait("vector", self.pfree.get(f"l3fm{qt % 2}"))
                    fs0 = S["f0"][:, (qt % 2) * 512:(qt % 2) * 512 + 512]
                    fs1 = S["f1"][:, (qt % 2) * 512:(qt % 2) * 512 + 512]
                    t1 = b.op("vector", lambda E, o=fs0, i=P[slot][:, :]:
                              E.tensor_single_scalar(o, i, 0.0, ALU.min),
                              writes=("f0",), inc="v")
                    b.wait("scalar", t1)
                    t2 = b.op("scalar", lambda E, o=fs1, i=fs0:
                              E.activation(o, i, AF.Exp), reads=("f0",), writes=("f1",), inc="s")
                    b.wait("vector", t2)
                    t3 = b.op("vector", lambda E, o=dstT[:, ec * T + qt * 512: ec * T + (qt + 1) * 512],
                              i=P[slot][:, :], j=fs1:
                              E.scalar_tensor_tensor(o, i, 0.0, j, ALU.max, ALU.add),
                              reads=("f1",), writes=(name,), inc="v")
                    self.pfree[slot] = t3
                    self.pfree[f"l3fm{qt % 2}"] = t3
        tfm = b.barrier("vector", "v")
        b.wait("tensor", tfm)
        b.op("vector", lambda E, o=S["KVf"][:, :]: E.memset(o, 0.0), writes=("KVf",))
        tz = b.op("vector", lambda E, o=S["KVb"][:, :]: E.memset(o, 0.0),
                  writes=("KVb",), inc="v")
        b.wait("tensor", tz)
        for c in range(NC):
            for gh in range(4):
                ec, hf = gh // 2, gh % 2
                p0 = hf * 64
                kvo = (gh // 2) * 72
                self.usepsum("tensor", "p0")
                self.mmsep(p0)
                ta = b.op("tensor", lambda E, o=P["p0"][:, 0:128],
                          l=S["KTg"][p0:p0 + 64, ec * T + c * 128: ec * T + (c + 1) * 128],
                          r=S["QTg"][p0:p0 + 64, ec * T + c * 128: ec * T + (c + 1) * 128]:
                          E.matmul(o, l, r, start=True, stop=True), inc="t")
                b.wait("vector", ta)
                b.wait("vector", self.pfree.get("l3a2"))
                ta2 = b.op("vector", lambda E, o=S["A2"][:, :], i=P["p0"][:, 0:128],
                           j=S["keepji"][:, :]:
                           E.tensor_mul(o, i, j), reads=("keepji", "A2"), writes=("A2",), inc="v")
                self.pfree["p0"] = ta2
                bslot = "bt0" if (c * 4 + gh) % 2 == 0 else "bt1"
                self.usepsum("tensor", bslot)
                self.mmsep(p0)
                tt = b.op("tensor", lambda E, o=P[bslot][:, 0:64],
                          i=S["KTg"][p0:p0 + 64, ec * T + c * 128: ec * T + (c + 1) * 128],
                          idm=S["id_bf"][p0:p0 + 64, p0:p0 + 64]:
                          E.transpose(o, i, idm), inc="t")
                b.wait("scalar", tt)
                b.wait("scalar", self.pfree.get(f"l3fk{gh}"))
                tfk = b.op("scalar", lambda E, o=S["fkt"][:, gh * 64:(gh + 1) * 64],
                           i=P[bslot][:, 0:64]: E.copy(o, i), writes=("fkt",), inc="s")
                self.pfree[bslot] = tfk
                b.wait("tensor", ta2)
                b.wait("tensor", self.pfree.get(f"l3kv{gh}"))
                self.usepsum("tensor", "p2")
                self._lastbase = None
                b.op("tensor", lambda E, o=P["p2"][:, 0:65], l=S["A2"][:, :],
                     r=S["Vg"][:, c * 264 + gh * 66: c * 264 + gh * 66 + 65]:
                     E.matmul(o, l, r, start=True, stop=False), inc="t")
                self.mmsep(p0)
                to = b.op("tensor", lambda E, o=P["p2"][:, 0:65],
                          l=S["QTg"][p0:p0 + 64, ec * T + c * 128: ec * T + (c + 1) * 128],
                          r=S["KVb"][p0:p0 + 64, kvo:kvo + 65],
                          tp=(p0, 0):
                          E.matmul(o, l, r, start=False, stop=True, tile_position=tp), inc="t")
                self.pfree["l3a2"] = to
                b.wait("tensor", tfk)
                self.usepsum("tensor", "p4")
                tu = b.op("tensor", lambda E, o=P["p4"][p0:p0 + 64, 0:65],
                          l=S["fkt"][:, gh * 64:(gh + 1) * 64],
                          r=S["Vg"][:, c * 264 + gh * 66: c * 264 + gh * 66 + 65],
                          tp=(0, p0):
                          E.matmul(o, l, r, start=True, stop=True, tile_position=tp), inc="t")
                self.pfree[f"l3fk{gh}"] = tu
                b.wait("vector", tu)
                tkv = b.op("vector", lambda E, o=S["KVf"][p0:p0 + 64, kvo:kvo + 65],
                           i=P["p4"][p0:p0 + 64, 0:65], j=S["KVf"][p0:p0 + 64, kvo:kvo + 65]:
                           E.tensor_add(o, i, j), reads=("KVf",), writes=("KVf",), inc="v")
                self.pfree["p4"] = tkv
                b.wait("vector", to)
                tkb = b.op("vector", lambda E, o=S["KVb"][p0:p0 + 64, kvo:kvo + 65],
                           i=S["KVf"][p0:p0 + 64, kvo:kvo + 65]: E.tensor_copy(o, i),
                           reads=("KVf",), writes=("KVb",), inc="v")
                self.pfree[f"l3kv{gh}"] = tkb
                par = (c * 4 + gh) % 2
                b.wait("vector", to)
                b.wait("vector", self.pfree.get(f"obf_g{par}"))
                rec = S["tiny"][:, 35 + gh:36 + gh]
                b.op("vector", lambda E, o=rec, i=P["p2"][:, 64:65]:
                     E.tensor_single_scalar(o, i, 1e-6, ALU.add), writes=("tiny",))
                b.op("vector", lambda E, o=rec, i=rec: E.reciprocal(o, i),
                     reads=("tiny",), writes=("tiny",))
                obs = S["obf"][:, par * 64:(par + 1) * 64]
                tob = b.op("vector", lambda E, o=obs, i=P["p2"][:, 0:64],
                           a=rec: E.tensor_single_scalar(o, i, a, ALU.mult),
                           reads=("tiny", "obf"), writes=("obf",), inc="v")
                self.pfree["p2"] = tob
                b.wait("tensor", tob)
                bslot2 = "bt1" if par == 0 else "bt0"
                self.usepsum("tensor", bslot2)
                tt2 = b.op("tensor", lambda E, o=P[bslot2][p0:p0 + 64, 0:128], i=obs,
                           idm=S["id_bf"][:, :], tp=(0, p0):
                           E.transpose(o, i, idm, tile_position=tp), inc="t")
                self.pfree[f"obf_g{par}"] = tt2
                b.wait("scalar", tt2)
                te2 = b.op("scalar", lambda E,
                           o=S["OTg"][p0:p0 + 64, ec * T + c * 128: ec * T + (c + 1) * 128],
                           i=P[bslot2][p0:p0 + 64, 0:128]: E.copy(o, i),
                           writes=("OTg",), inc="s")
                self.pfree[bslot2] = te2
        self._attn_out_proj(g, d["w_l3_o"])


    def layer1(self):
        nc, b, S, P, d = self.nc, self.b, self.S, self.P, self.d
        gamma = (1.0 - np.exp2(-5.0 - np.arange(16, dtype=np.float32))).astype(np.float32)
        self._l1_gC = [float(g ** np.float32(64)) for g in gamma]
        with ExitStack() as lctx:
            S["xnT"] = lctx.enter_context(nc.sbuf_tensor("l1_xnT", [128, NDC * T], BF16))
            xnT = ("xnT", S["xnT"])
            with ExitStack() as actx:
                for nm, sh, dt in (("A_sc0", [128, D], F32), ("A_sc1", [128, D], F32),
                                   ("A_scb", [128, 2 * D], BF16), ("A_scI", [128, D], I32)):
                    S[nm] = actx.enter_context(nc.sbuf_tensor("l1" + nm, sh, dt))
                with nc.Block() as blk:
                    self.phaseA(xnT)
                    self.phase_barrier()
                    b.flush_into(blk)
            with ExitStack() as actx:
                for nm, sh, dt in (("wsl", [128, 4096], BF16),
                                   ("Qraw", [64, 32 * 128], BF16), ("Kraw", [64, 32 * 128], BF16),
                                   ("Vtok", [64, 32 * 256], BF16),
                                   ("Gs", [64, 32 * 256], BF16),
                                   ("rpA", [64, 32 * 128], BF16), ("rpB", [64, 32 * 128], BF16),
                                   ("QT", [64, 2 * T], BF16), ("KT", [64, 2 * T], BF16),
                                   ("Ktok", [64, 32 * 128], BF16),
                                   ("OTg", [128, 2 * T], BF16),
                                   ("dmat", [64, 128], BF16),
                                   ("dqc", [128, 16], F32), ("dkc", [128, 16], F32),
                                   ("gn2", [128, 256], BF16),
                                   ("A2", [128, 128], BF16),
                                   ("Sf", [128, 2 * 128], F32), ("Sb", [128, 2 * 128], BF16),
                                   ("f0", [128, 256], F32),
                                   ("ob", [128, 512], BF16), ("o2s", [128, 256], BF16)):
                    S[nm] = actx.enter_context(nc.sbuf_tensor("l1" + nm, sh, dt))
                with nc.Block() as blk:
                    for cn, dn in (("dqc", "c_dq"), ("dkc", "c_dkv")):
                        b.dma("sync", S[cn][:, :], d[dn], writes=(cn,))
                    b.dma("gpsimd", S["gn2"][:, :], d["w_l1_gn"], writes=("gn2",))
                    for qh in range(32):
                        b.dma("gpsimd", S["rpA"][0:64, qh * 128:(qh + 1) * 128],
                              d["c_ropeA"][qh * 64:(qh + 1) * 64, :], writes=("rpA",))
                        b.dma("gpsimd", S["rpB"][0:64, qh * 128:(qh + 1) * 128],
                              d["c_ropeB"][qh * 64:(qh + 1) * 64, :], writes=("rpB",))
                    for eng in ("vector", "scalar", "tensor", "gpsimd"):
                        b.wait_dma_all(eng, "sync")
                        b.wait_dma_all(eng, "gpsimd")
                    for g in range(getattr(self, "l1_ng", 8)):
                        self._l1_group(g)
                    self.phase_barrier()
                    b.flush_into(blk)
            with ExitStack() as actx:
                for nm, sh, dt in (("hA", [128, NDC * T], BF16), ("hB", [128, NDC * T], BF16),
                                   ("wfull", [128, 2 * 8 * 1024], BF16),
                                   ("fb", [128, 8], F32), ("ft", [128, 512], F32)):
                    S[nm] = actx.enter_context(nc.sbuf_tensor("l1f" + nm, sh, dt))
                with nc.Block() as blk:
                    b.dma("sync", S["fb"][:, :], d["w_l1_fb"], writes=("fb",))
                    b.wait_dma_all("scalar", "sync")
                    self._l1_ffn()
                    self.phase_barrier()
                    b.flush_into(blk)
        S.pop("xnT", None)

    def _l1_ffn(self):
        self.mark("_l1_ffn")
        nc, b, S, P, d = self.nc, self.b, self.S, self.P, self.d
        wnames = ("w_l1_f0", "w_l1_f1", "w_l1_f2")
        ftoks = {}

        def issue_f(step):
            if step >= 3:
                return
            fsl = (step % 2) * 8192
            ftoks[step] = b.dma("gpsimd",
                                S["wfull"][:, fsl:fsl + 8192].rearrange("p (dcI e) -> p dcI e", e=1024),
                                self.apx(d[wnames[step]], 0, [[1024, 128], [128 * 1024, 8], [1, 1024]]),
                                writes=("wfull",))
        b.wait("gpsimd", ("t", b.cnt["t"]))
        issue_f(0)
        issue_f(1)
        for step, wn in enumerate(wnames):
            fsl = (step % 2) * 8192
            src = S["xnT"] if step == 0 else (S["hA"] if step == 1 else S["hB"])
            srcn = "xnT" if step == 0 else ("hA" if step == 1 else "hB")
            dst = S["hA"] if step == 0 else (S["hB"] if step == 1 else None)
            dstn = "hA" if step == 0 else ("hB" if step == 1 else None)
            b.wait("tensor", ftoks[step])
            for dcO in range(NDC):
                for qt in range(4):
                    slot = f"p{qt % 2}"
                    self.usepsum("tensor", slot)
                    tm = None
                    for dcI in range(NDC):
                        tm = b.op("tensor", lambda E, o=P[slot][:, :],
                                  l=S["wfull"][:, fsl + dcI * 1024 + dcO * 128: fsl + dcI * 1024 + (dcO + 1) * 128],
                                  r=src[:, dcI * T + qt * 512: dcI * T + (qt + 1) * 512],
                                  st=(dcI == 0), sp=(dcI == NDC - 1):
                                  E.matmul(o, l, r, start=st, stop=sp), inc="t")
                    if step < 2:
                        b.wait("scalar", tm)
                        te = b.op("scalar", lambda E, o=dst[:, dcO * T + qt * 512: dcO * T + (qt + 1) * 512],
                                  i=P[slot][:, :]: E.copy(o, i), writes=(dstn,), inc="s")
                        self.pfree[slot] = te
                    else:
                        b.wait("scalar", tm)
                        b.wait("scalar", self.pfree.get("l1ft"))
                        tta = b.op("scalar", lambda E, o=S["ft"][:, :], i=P[slot][:, :],
                                   bb=S["fb"][:, dcO:dcO + 1]:
                                   E.activation(o, i, AF.Tanh, bias=bb),
                                   reads=("fb",), writes=("ft",), inc="s")
                        self.pfree[slot] = tta
                        b.wait("tensor", tta)
                        for k4 in range(4):
                            qc = qt * 4 + k4
                            self.usepsum("tensor", "p5")
                            tt = b.op("tensor", lambda E, o=P["p5"][:, 0:128],
                                      i=S["ft"][:, k4 * 128:(k4 + 1) * 128],
                                      idm=S["id_f32"][:, :]:
                                      E.transpose(o, i, idm), inc="t")
                            self.pfree["l1ft"] = tt
                            b.wait("vector", tt)
                            xh = self.xc(qc)[:, dcO * 128:(dcO + 1) * 128]
                            tv = b.op("vector", lambda E, o=xh, i=P["p5"][:, 0:128], j=xh:
                                      E.tensor_add(o, i, j), reads=("xres",), writes=("xres",), inc="v")
                            self.pfree["p5"] = tv
            b.wait("gpsimd", ("t", b.cnt["t"]))
            issue_f(step + 2)

    def _l1_group(self, g):
        nc, b, S, P, d = self.nc, self.b, self.S, self.P, self.d
        self.mark("lg_proj")
        b.wait("gpsimd", ("t", b.cnt["t"]))
        tq = b.dma("gpsimd", self.apx(S["wsl"][:, :], 0, [[4096, 128], [512, 8], [1, 128]]),
                   self.apx(d["w_l1_q"], g * 128, [[1024, 128], [128 * 1024, 8], [1, 128]]),
                   writes=("wsl",))
        tk = b.dma("gpsimd", self.apx(S["wsl"][:, :], 128, [[4096, 128], [512, 8], [1, 128]]),
                   self.apx(d["w_l1_k"], g * 128, [[1024, 128], [128 * 1024, 8], [1, 128]]),
                   writes=("wsl",))
        tv = b.dma("gpsimd", self.apx(S["wsl"][:, :], 256, [[4096, 128], [512, 8], [1, 256]]),
                   self.apx(d["w_l1_v"], g * 256, [[2048, 128], [128 * 2048, 8], [1, 256]]),
                   writes=("wsl",))
        tdm = b.dma("gpsimd", S["dmat"][:, :], d["c_dmatT"][0:64, g * 128:(g + 1) * 128],
                    writes=("dmat",))
        for tok in (tq, tk, tv):
            b.wait("tensor", tok)
        for qh in range(32):
            slot = f"p{qh % 2}"
            self.usepsum("tensor", slot)
            tm = None
            for dc in range(NDC):
                tm = b.op("tensor", lambda E, o=P[slot][0:64, 0:512],
                          l=S["xnT"][:, dc * T + qh * 64: dc * T + (qh + 1) * 64],
                          r=S["wsl"][:, dc * 512:(dc + 1) * 512],
                          st=(dc == 0), sp=(dc == NDC - 1):
                          E.matmul(o, l, r, start=st, stop=sp), inc="t")
            b.wait("scalar", tm)
            b.op("scalar", lambda E, o=S["Qraw"][0:64, qh * 128:(qh + 1) * 128],
                 i=P[slot][0:64, 0:128]: E.copy(o, i), writes=("Qraw",), inc="s")
            b.op("scalar", lambda E, o=S["Kraw"][0:64, qh * 128:(qh + 1) * 128],
                 i=P[slot][0:64, 128:256]: E.copy(o, i), writes=("Kraw",), inc="s")
            te = b.op("scalar", lambda E, o=S["Vtok"][0:64, qh * 256:(qh + 1) * 256],
                      i=P[slot][0:64, 256:512]: E.copy(o, i), writes=("Vtok",), inc="s")
            self.pfree[slot] = te
        self.mark("lg_G")
        tgb = b.barrier("tensor", "t")
        b.wait("gpsimd", tgb)
        tg = b.dma("gpsimd", self.apx(S["wsl"][:, :], 0, [[4096, 128], [256, 8], [1, 256]]),
                   self.apx(d["w_l1_g"], g * 256, [[2048, 128], [128 * 2048, 8], [1, 256]]),
                   writes=("wsl",))
        b.wait("tensor", tg)
        for qh in range(32):
            slot = f"p{qh % 2}"
            self.usepsum("tensor", slot)
            tm = None
            for dc in range(NDC):
                tm = b.op("tensor", lambda E, o=P[slot][0:64, 0:256],
                          l=S["xnT"][:, dc * T + qh * 64: dc * T + (qh + 1) * 64],
                          r=S["wsl"][:, dc * 256:(dc + 1) * 256],
                          st=(dc == 0), sp=(dc == NDC - 1):
                          E.matmul(o, l, r, start=st, stop=sp), inc="t")
            b.wait("scalar", tm)
            b.wait("scalar", self.pfree.get(f"obg{qh % 2}"))
            obs = S["ob"][0:64, (qh % 2) * 256:(qh % 2) * 256 + 256]
            tsl = b.op("scalar", lambda E, o=obs, i=P[slot][0:64, 0:256]:
                       E.activation(o, i, AF.Silu), writes=("ob",), inc="s")
            self.pfree[slot] = tsl
            b.wait("vector", tsl)
            tgs = b.op("vector", lambda E, o=S["Gs"][0:64, qh * 256:(qh + 1) * 256],
                       i=obs, j=S["gn2"][0:64, :]:
                       E.tensor_mul(o, i, j), reads=("ob", "gn2"), writes=("Gs",), inc="v")
            self.pfree[f"obg{qh % 2}"] = tgs
        self.mark("lg_rope")
        trd = b.barrier("scalar", "s")
        b.wait("vector", trd)
        for raw, rawn in ((S["Qraw"], "Qraw"), (S["Kraw"], "Kraw")):
            WR = raw.shape[1]
            for b4 in range(8):
                o1 = b4 * 512
                b.op("vector", lambda E, o=S["ob"][0:64, 0:512],
                     i=self.apx(raw[:, :], o1 + 32, [[WR, 64], [64, 8], [-32, 2], [1, 32]]),
                     j=S["rpB"][0:64, o1:o1 + 512]:
                     E.tensor_mul(o, i, j), reads=(rawn, "rpB", "ob"), writes=("ob",))
                b.op("vector", lambda E, o=raw[0:64, o1:o1 + 512], i=raw[0:64, o1:o1 + 512],
                     j=S["rpA"][0:64, o1:o1 + 512]:
                     E.tensor_mul(o, i, j), reads=(rawn, "rpA"), writes=(rawn,))
                b.op("vector", lambda E, o=raw[0:64, o1:o1 + 512], i=raw[0:64, o1:o1 + 512],
                     j=S["ob"][0:64, 0:512]:
                     E.tensor_add(o, i, j), reads=(rawn, "ob"), writes=(rawn,))
        for h in range(2):
            b.op("vector", lambda E,
                 o=self.apx(S["Ktok"][:, :], h * 64, [[4096, 64], [128, 32], [1, 64]]),
                 i=self.apx(S["Kraw"][:, :], h * 64, [[4096, 64], [128, 32], [1, 64]]),
                 a=S["dkc"][0:64, g * 2 + h: g * 2 + h + 1]:
                 E.tensor_single_scalar(o, i, a, ALU.mult),
                 reads=("Kraw", "dkc", "Ktok"), writes=("Ktok",))
        trv = b.barrier("vector", "v")
        b.wait("tensor", trv)
        for raw, dstT, dstn, sc in ((S["Qraw"], S["QT"], "QT", 0.125),
                                    (S["Kraw"], S["KT"], "KT", 1.0)):
            for qh in range(32):
                for h in range(2):
                    bslot = "bt0" if (qh * 2 + h) % 2 == 0 else "bt1"
                    self.usepsum("tensor", bslot)
                    tt = b.op("tensor", lambda E, o=P[bslot][0:64, 0:64],
                              i=raw[0:64, qh * 128 + h * 64: qh * 128 + (h + 1) * 64],
                              idm=S["id_bf"][0:64, 0:64]: E.transpose(o, i, idm), inc="t")
                    b.wait("scalar", tt)
                    te = b.op("scalar", lambda E,
                              o=dstT[0:64, h * T + qh * 64: h * T + (qh + 1) * 64],
                              i=P[bslot][0:64, 0:64], sc=sc:
                              E.activation(o, i, AF.Copy, scale=sc), writes=(dstn,), inc="s")
                    self.pfree[bslot] = te
        self.mark("lg_scan")
        b.op("vector", lambda E, o=S["Sf"][0:64, :]: E.memset(o, 0.0), writes=("Sf",))
        tz = b.op("vector", lambda E, o=S["Sb"][0:64, :]: E.memset(o, 0.0), writes=("Sb",), inc="v")
        tscl = b.barrier("scalar", "s")
        b.wait("tensor", tz)
        b.wait("tensor", tscl)
        b.wait("vector", tdm)
        b.wait("vector", ("t", b.cnt["t"]))
        gCs = [self._l1_gC[g * 2 + h] for h in range(2)]
        tutok = {}

        def issue_tu(c):
            if c >= 32:
                return
            bank = "p5" if c % 2 == 0 else "p3"
            self.usepsum("tensor", bank)
            t2 = None
            for h in range(2):
                t2 = b.op("tensor", lambda E, o=P[bank][0:64, h * 128:(h + 1) * 128],
                          l=S["Ktok"][0:64, c * 128 + h * 64: c * 128 + (h + 1) * 64],
                          r=S["Vtok"][0:64, c * 256 + h * 128: c * 256 + (h + 1) * 128]:
                          E.matmul(o, l, r, start=True, stop=True), inc="t")
            tutok[c] = t2
        issue_tu(0)
        issue_tu(1)
        tsb_prev = tz
        for c in range(32):
            bank2 = "p4" if c % 2 == 0 else "p2"
            tat = {}
            for h in range(2):
                self.usepsum("tensor", f"p{h}")
                tat[h] = b.op("tensor", lambda E, o=P[f"p{h}"][0:64, 0:64],
                              l=S["KT"][0:64, h * T + c * 64: h * T + (c + 1) * 64],
                              r=S["QT"][0:64, h * T + c * 64: h * T + (c + 1) * 64]:
                              E.matmul(o, l, r, start=True, stop=True), inc="t")
            ta2t = {}
            for h in range(2):
                b.wait("vector", tat[h])
                b.wait("vector", self.pfree.get(f"l1a2{h}"))
                ta2t[h] = b.op("vector", lambda E, o=S["A2"][0:64, h * 64:(h + 1) * 64],
                               i=P[f"p{h}"][0:64, 0:64],
                               j=S["dmat"][0:64, h * 64:(h + 1) * 64]:
                               E.tensor_mul(o, i, j), reads=("dmat", "A2"), writes=("A2",), inc="v")
                self.pfree[f"p{h}"] = ta2t[h]
            self.usepsum("tensor", bank2)
            toat = {}
            toit = {}
            for h in range(2):
                b.wait("tensor", ta2t[h])
                toat[h] = b.op("tensor", lambda E, o=P[bank2][0:64, h * 128:(h + 1) * 128],
                               l=S["A2"][0:64, h * 64:(h + 1) * 64],
                               r=S["Vtok"][0:64, c * 256 + h * 128: c * 256 + (h + 1) * 128]:
                               E.matmul(o, l, r, start=True, stop=True), inc="t")
                self.pfree[f"l1a2{h}"] = toat[h]
            b.wait("tensor", tsb_prev)
            for h in range(2):
                toit[h] = b.op("tensor", lambda E,
                               o=P[bank2][0:64, 256 + h * 128: 256 + (h + 1) * 128],
                               l=S["QT"][0:64, h * T + c * 64: h * T + (c + 1) * 64],
                               r=S["Sb"][0:64, h * 128:(h + 1) * 128]:
                               E.matmul(o, l, r, start=True, stop=True), inc="t")
            ubank = "p5" if c % 2 == 0 else "p3"
            b.wait("vector", tutok[c])
            tsu = None
            for h in range(2):
                tsu = b.op("vector", lambda E, o=S["Sf"][0:64, h * 128:(h + 1) * 128],
                           i=S["Sf"][0:64, h * 128:(h + 1) * 128], sc=gCs[h],
                           j=P[ubank][0:64, h * 128:(h + 1) * 128]:
                           E.scalar_tensor_tensor(o, i, sc, j, ALU.mult, ALU.add),
                           reads=("Sf",), writes=("Sf",), inc="v")
            self.pfree[ubank] = tsu
            b.wait("vector", toit[1])
            for h in range(2):
                tsb_prev = b.op("vector", lambda E, o=S["Sb"][0:64, h * 128:(h + 1) * 128],
                                i=S["Sf"][0:64, h * 128:(h + 1) * 128]: E.tensor_copy(o, i),
                                reads=("Sf", "Sb"), writes=("Sb",), inc="v")
            ogt, ogn = (S["Qraw"], "Qraw") if c < 16 else (S["Kraw"], "Kraw")
            co = (c % 16) * 256
            b.wait("vector", toat[1])
            for h in range(2):
                b.op("vector", lambda E, o=S["f0"][0:64, h * 128:(h + 1) * 128],
                     i=P[bank2][0:64, 256 + h * 128: 256 + (h + 1) * 128],
                     a=S["dqc"][0:64, g * 2 + h: g * 2 + h + 1]:
                     E.tensor_single_scalar(o, i, a, ALU.mult),
                     reads=("dqc", "f0"), writes=("f0",))
            tev = None
            for h in range(2):
                tev = b.op("vector", lambda E, o=ogt[0:64, co + h * 128: co + (h + 1) * 128],
                           i=S["f0"][0:64, h * 128:(h + 1) * 128],
                           j=P[bank2][0:64, h * 128:(h + 1) * 128]:
                           E.tensor_add(o, i, j), reads=("f0", ogn), writes=(ogn,), inc="v")
            self.pfree[bank2] = tev
            issue_tu(c + 2)
        self.mark("lg_epi")
        teb = b.barrier("vector", "v")
        b.wait("scalar", teb)
        for c in range(32):
            ogt, ogn = (S["Qraw"], "Qraw") if c < 16 else (S["Kraw"], "Kraw")
            osl = ogt[0:64, (c % 16) * 256:((c % 16) + 1) * 256]
            b.wait("scalar", self.pfree.get("o2h0"))
            tsq = b.op("scalar", lambda E,
                       o=S["o2s"][0:64, 0:256], i=osl:
                       E.activation(o, i, AF.Square), reads=(ogn,), writes=("o2s",), inc="s")
            b.wait("vector", tsq)
            rt = None
            for h in range(2):
                rt = b.op("vector", lambda E, o=S["Sf"][0:64, c * 2 + h: c * 2 + h + 1],
                          i=S["o2s"][0:64, h * 128:(h + 1) * 128]:
                          E.reduce_sum(o, i, axis=AX), reads=("o2s", "Sf"), writes=("Sf",), inc="v")
            self.pfree["o2h0"] = rt
        tms = b.barrier("vector", "v")
        b.wait("scalar", tms)
        tsr = b.op("scalar", lambda E, o=S["Sf"][0:64, 64:128], i=S["Sf"][0:64, 0:64]:
                   E.activation(o, i, AF.Sqrt, scale=1.0 / 128.0, bias=S["tiny"][0:64, 63:64]),
                   reads=("Sf", "tiny"), writes=("Sf",), inc="s")
        b.wait("vector", tsr)
        b.op("vector", lambda E, o=S["Sf"][0:64, 0:64], i=S["Sf"][0:64, 64:128]:
             E.reciprocal(o, i), reads=("Sf",), writes=("Sf",))
        te = None
        for c in range(32):
            ogt, ogn = (S["Qraw"], "Qraw") if c < 16 else (S["Kraw"], "Kraw")
            osl = ogt[0:64, (c % 16) * 256:((c % 16) + 1) * 256]
            ohf = (c % 2) * 256
            b.wait("vector", self.pfree.get(f"obh{c % 2}"))
            for h in range(2):
                b.op("vector", lambda E, o=S["ob"][0:64, ohf + h * 128: ohf + (h + 1) * 128],
                     i=osl[:, h * 128:(h + 1) * 128],
                     a=S["Sf"][0:64, c * 2 + h: c * 2 + h + 1]:
                     E.tensor_single_scalar(o, i, a, ALU.mult),
                     reads=(ogn, "Sf", "ob"), writes=("ob",))
            tob = b.op("vector", lambda E, o=S["ob"][0:64, ohf:ohf + 256],
                       i=S["ob"][0:64, ohf:ohf + 256], j=S["Gs"][0:64, c * 256:(c + 1) * 256]:
                       E.tensor_mul(o, i, j), reads=("ob", "Gs"), writes=("ob",), inc="v")
            b.wait("tensor", tob)
            tt = None
            for h in range(2):
                bslot = "bt0" if h == 0 else "bt1"
                self.usepsum("tensor", bslot)
                tt = b.op("tensor", lambda E, o=P[bslot][:, 0:64],
                          i=S["ob"][0:64, ohf + h * 128: ohf + (h + 1) * 128],
                          idm=S["id_bf"][0:64, 0:64]: E.transpose(o, i, idm), inc="t")
                b.wait("scalar", tt)
                te = b.op("scalar", lambda E,
                          o=S["OTg"][:, h * T + c * 64:(h * T + (c + 1) * 64)],
                          i=P[bslot][:, 0:64]: E.copy(o, i), writes=("OTg",), inc="s")
                self.pfree[bslot] = te
            self.pfree[f"obh{c % 2}"] = tt
        b.wait("tensor", te)
        self._attn_out_proj(g, d["w_l1_o"])

    def _l0_build_bias(self):
        self.mark("_l0_build_bias")
        nc, b, S, P, d = self.nc, self.b, self.S, self.P, self.d
        for cc in range(8):
            b.dma("sync", S["f0"][:, cc * 2:cc * 2 + 2],
                  d["w_l0_cent"][cc * 128:(cc + 1) * 128, :], writes=("f0",))
        b.wait_dma_all("vector", "sync")
        b.wait_dma_all("tensor", "sync")
        for cc in range(8):
            b.op("vector", lambda E, o=S["f1"][:, cc * 2:cc * 2 + 2], i=S["f0"][:, cc * 2:cc * 2 + 2]:
                 E.tensor_mul(o, i, i), reads=("f0",), writes=("f1",))
            b.op("vector", lambda E, o=S["nrow"][:, cc:cc + 1], i=S["f1"][:, cc * 2:cc * 2 + 2]:
                 E.reduce_sum(o, i, axis=AX), reads=("f1",), writes=("nrow",))
        tv = b.barrier("vector", "v")
        b.wait("tensor", tv)
        for cc in range(8):
            self.usepsum("tensor", "p5")
            tt = b.op("tensor", lambda E, o=P["p5"][0:2, 0:128], i=S["f0"][:, cc * 2:cc * 2 + 2],
                      idm=S["id_f32"][:, :]: E.transpose(o, i, idm), inc="t")
            b.wait("scalar", tt)
            te = b.op("scalar", lambda E, o=S["f1"][0:2, cc * 128:(cc + 1) * 128],
                      i=P["p5"][0:2, 0:128]:
                      E.copy(o, i), writes=("f1",), inc="s")
            self.pfree["p5"] = te
        tv = b.barrier("vector", "v")
        b.wait("tensor", tv)
        self.usepsum("tensor", "p4")
        t1 = b.op("tensor", lambda E, o=P["p4"][0:8, 0:128], i=S["nrow"][:, 0:8],
                  idm=S["id_f32"][:, :]: E.transpose(o, i, idm), inc="t")
        b.wait("scalar", t1)
        t2 = b.op("scalar", lambda E, o=S["f0"][0:8, 512:640], i=P["p4"][0:8, 0:128]:
                  E.copy(o, i), writes=("f0",), inc="s")
        self.pfree["p4"] = t2
        b.wait("sync", t2)
        src = S["f0"]
        fl = b.dma("sync", S["N2B"][0:1, :].rearrange("o (a b) -> o a b", b=128),
                   src[0:8, 512:640], reads=("f0",), writes=("N2B",))
        b.wait("gpsimd", fl)
        n2b = S["N2B"]
        rp = b.dma("gpsimd", n2b[:, :].rearrange("p (o n) -> p o n", o=1),
                   bass.AP(n2b.tensor if hasattr(n2b, "tensor") else n2b, 0,
                           [[1, 1], [0, 128], [1, D]]),
                   reads=("N2B",), writes=("N2B",))
        b.wait("tensor", rp)
        b.wait("vector", rp)
        mx = S["tiny"][:, 16:17]
        b.op("vector", lambda E, o=mx: E.memset(o, 0.0), writes=("tiny",))
        tvv = b.barrier("vector", "v")
        b.wait("tensor", tvv)
        for icc in range(8):
            for jt in range(2):
                slot = f"p{jt}"
                self.usepsum("tensor", slot)
                tm = b.op("tensor", lambda E, o=P[slot][:, :], l=S["f1"][0:2, icc * 128:(icc + 1) * 128],
                          r=S["f1"][0:2, jt * 512:(jt + 1) * 512]:
                          E.matmul(o, l, r, start=True, stop=True), inc="t")
                b.wait("vector", tm)
                b.op("vector", lambda E, o=S["f0"][:, jt * 512:(jt + 1) * 512], i=P[slot][:, :],
                     n2j=S["N2B"][:, jt * 512:(jt + 1) * 512]:
                     E.scalar_tensor_tensor(o, i, -2.0, n2j, ALU.mult, ALU.add),
                     reads=("N2B",), writes=("f0",))
                tfree = b.op("vector", lambda E, o=S["f0"][:, jt * 512:(jt + 1) * 512],
                             a=S["nrow"][:, icc:icc + 1]:
                             E.tensor_single_scalar(o, o, a, ALU.add),
                             reads=("f0", "nrow"), writes=("f0",), inc="v")
                self.pfree[slot] = tfree
            b.op("vector", lambda E, o=S["f0"][:, :]: E.tensor_single_scalar(o, o, 0.0, ALU.max),
                 reads=("f0",), writes=("f0",))
            tv2 = b.barrier("vector", "v")
            b.wait("scalar", tv2)
            td = b.op("scalar", lambda E, o=S["f0"][:, :], i=S["f0"][:, :]:
                      E.activation(o, i, AF.Sqrt), reads=("f0",), writes=("f0",), inc="s")
            b.wait("vector", td)
            b.op("vector", lambda E, o=S["tiny"][:, 17:18], i=S["f0"][:, :]:
                 E.reduce_max(o, i, axis=AX), reads=("f0",), writes=("tiny",))
            b.op("vector", lambda E, o=mx, i=S["tiny"][:, 17:18], j=mx:
                 E.tensor_max(o, i, j), reads=("tiny",), writes=("tiny",))
        tv3 = b.barrier("vector", "v")
        b.wait("tensor", tv3)
        self.usepsum("tensor", "p4")
        t4 = b.op("tensor", lambda E, o=P["p4"][0:1, 0:128], i=mx, idm=S["id_f32"][:, :]:
                  E.transpose(o, i, idm), inc="t")
        b.wait("vector", t4)
        b.op("vector", lambda E, o=S["tiny"][0:1, 18:19], i=P["p4"][0:1, 0:128]:
             E.reduce_max(o, i, axis=AX), writes=("tiny",))
        tmx = b.barrier("vector", "v")
        self.pfree["p4"] = tmx
        b.op("vector", lambda E, o=S["tiny"][:, 20:21]: E.memset(o, 0.0), writes=("tiny",))
        b.op("vector", lambda E, o=S["tiny"][0:1, 19:20], i=S["tiny"][0:1, 18:19]:
             E.tensor_single_scalar(o, i, 1e-6, ALU.max), reads=("tiny",), writes=("tiny",))
        b.op("vector", lambda E, o=S["tiny"][0:1, 20:21], i=S["tiny"][0:1, 19:20]:
             E.reciprocal(o, i), reads=("tiny",), writes=("tiny",))
        b.op("vector", lambda E, o=S["tiny"][0:1, 20:21], i=S["tiny"][0:1, 20:21]:
             E.tensor_single_scalar(o, i, -0.1, ALU.mult), reads=("tiny",), writes=("tiny",))
        b.op("vector", lambda E, o=S["tiny"][0:1, 21:22], i=S["tiny"][0:1, 18:19]:
             E.tensor_single_scalar(o, i, 1e-6, ALU.is_gt), reads=("tiny",), writes=("tiny",))
        tf = b.op("vector", lambda E, o=S["tiny"][0:1, 20:21], i=S["tiny"][0:1, 20:21],
                  j=S["tiny"][0:1, 21:22]: E.tensor_mul(o, i, j),
                  reads=("tiny",), writes=("tiny",), inc="v")
        b.wait("tensor", tf)
        self.usepsum("tensor", "p4")
        trp = b.op("tensor", lambda E, o=P["p4"][:, 0:1], l=S["ones_f32"][:, :],
                   r=S["tiny"][:, 20:21]:
                   E.matmul(o, l, r, start=True, stop=True), inc="t")
        b.wait("vector", trp)
        rp2 = b.op("vector", lambda E, o=S["tiny"][:, 22:23], i=P["p4"][:, 0:1]:
                   E.tensor_copy(o, i), writes=("tiny",), inc="v")
        self.pfree["p4"] = rp2
        fcol = S["tiny"][:, 22:23]
        for kt in range(9):
            b.op("vector", lambda E, o=S["B2"][:, kt * 1536:(kt + 1) * 1536]: E.memset(o, 0.0),
                 writes=("B2",))
        tz = b.barrier("vector", "v")
        b.wait("sync", tz)
        for icc in range(8):
            for jt in range(2):
                slot = f"p{jt}"
                self.usepsum("tensor", slot)
                tm = b.op("tensor", lambda E, o=P[slot][:, :], l=S["f1"][0:2, icc * 128:(icc + 1) * 128],
                          r=S["f1"][0:2, jt * 512:(jt + 1) * 512]:
                          E.matmul(o, l, r, start=True, stop=True), inc="t")
                b.wait("vector", tm)
                b.op("vector", lambda E, o=S["f0"][:, jt * 512:(jt + 1) * 512], i=P[slot][:, :],
                     n2j=S["N2B"][:, jt * 512:(jt + 1) * 512]:
                     E.scalar_tensor_tensor(o, i, -2.0, n2j, ALU.mult, ALU.add),
                     reads=("N2B",), writes=("f0",))
                tfree = b.op("vector", lambda E, o=S["f0"][:, jt * 512:(jt + 1) * 512],
                             a=S["nrow"][:, icc:icc + 1]:
                             E.tensor_single_scalar(o, o, a, ALU.add),
                             reads=("f0", "nrow"), writes=("f0",), inc="v")
                self.pfree[slot] = tfree
            b.op("vector", lambda E, o=S["f0"][:, :]: E.tensor_single_scalar(o, o, 0.0, ALU.max),
                 reads=("f0",), writes=("f0",))
            tv2 = b.barrier("vector", "v")
            b.wait("scalar", tv2)
            td = b.op("scalar", lambda E, o=S["f0"][:, :], i=S["f0"][:, :]:
                      E.activation(o, i, AF.Sqrt), reads=("f0",), writes=("f0",), inc="s")
            b.wait("vector", td)
            tb = b.op("vector", lambda E, o=S["f0"][:, :], a=fcol:
                      E.tensor_single_scalar(o, o, a, ALU.mult),
                      reads=("f0", "tiny"), writes=("f0",), inc="v")
            b.wait("gpsimd", tb)
            f0t = S["f0"]
            d1 = b.dma("gpsimd", S["B2"][1:128, icc * 1536 + 1: icc * 1536 + 1 + D],
                       f0t[0:127, :], reads=("f0",), writes=("B2",))
            d2 = b.dma("gpsimd", S["B2"][0:1, (icc + 1) * 1536 + 1: (icc + 1) * 1536 + 1 + D],
                       f0t[127:128, :], reads=("f0",), writes=("B2",))
            b.wait("vector", d1)
            b.wait("vector", d2)

    def _l0_attn_group(self, g):
        nc, b, S, P, d = self.nc, self.b, self.S, self.P, self.d
        self._attn_group_common(
            g, wq=(d["w_l0_qkvT"], 0), wk=(d["w_l0_qkvT"], 1024),
            wv=(d["w_l0_qkvT"], 2048), scale=0.125,
            projT=d["w_l0_projT"], causal=False, biased=True, tag=f"l0g{g}")

    @staticmethod
    def apx(base, offset_elems, dims):
        th = base.tensor
        return bass.AP(th, offset_elems, [list(x) for x in dims])

    def _attn_proj_group(self, g, wq, wk, wv, scale):
        self.mark("_attn_proj_group")
        nc, b, S, P, d = self.nc, self.b, self.S, self.P, self.d
        for (wap, coff), kind in ((wq, "q"), (wk, "k"), (wv, "v")):
            half = {"q": 0, "k": 1, "v": 0}[kind]
            wsl = S["wsl"][:, half * 2048:(half + 1) * 2048]
            src = self.apx(wap, (coff + g * 256), [[wap.shape[1], 128], [128 * wap.shape[1], 8], [1, 256]])
            b.wait("gpsimd", ("t", b.cnt["t"]))
            wt = b.dma("gpsimd", wsl.rearrange("p (dc e) -> p dc e", e=256), src,
                       writes=("wsl",))
            b.wait("tensor", wt)
            if kind in ("q", "k"):
                scr = self.scr["s_QT" if kind == "q" else "s_KT"]
                sc = scale if kind == "q" else 1.0
                for ec in range(2):
                    for qt in range(4):
                        slot = f"p{qt % 2}"
                        ssl = (qt % 2) * 512
                        self.usepsum("tensor", slot)
                        tm = None
                        for dc in range(NDC):
                            tm = b.op("tensor", lambda E, o=P[slot][:, :],
                                      l=wsl[:, dc * 256 + ec * 128: dc * 256 + (ec + 1) * 128],
                                      r=S["xnT"][:, dc * T + qt * 512: dc * T + (qt + 1) * 512],
                                      st=(dc == 0), sp=(dc == NDC - 1):
                                      E.matmul(o, l, r, start=st, stop=sp), inc="t")
                        b.wait("scalar", tm)
                        b.wait("scalar", self.pfree.get(f"stq{qt % 2}"))
                        te = b.op("scalar", lambda E, o=S["stg"][:, ssl:ssl + 512],
                                  i=P[slot][:, :], sc=sc:
                                  E.activation(o, i, AF.Copy, scale=sc),
                                  writes=("stg",), inc="s")
                        self.pfree[slot] = te
                        b.wait("sync", te)
                        self.pfree[f"stq{qt % 2}"] = b.dma(
                            "sync", scr[g * 256 + ec * 128: g * 256 + (ec + 1) * 128,
                                        qt * 512:(qt + 1) * 512],
                            S["stg"][:, ssl:ssl + 512], reads=("stg",))
            else:
                if g == 0:
                    vones = b.op("vector", lambda E, o=self.apx(S["stg"][:, :], 1024 + 64,
                                 [[S["stg"].shape[1], 128], [264, 2], [66, 4], [1, 2]]):
                                 E.memset(o, 1.0), reads=("stg",), writes=("stg",), inc="v")
                    b.wait("scalar", vones)
                    b.wait("sync", vones)
                for c in range(NC):
                    slot = f"p{c % 2}"
                    vof = 1024 + (c % 2) * 264
                    self.usepsum("tensor", slot)
                    tm = None
                    for dc in range(NDC):
                        tm = b.op("tensor", lambda E, o=P[slot][:, 0:256],
                                  l=S["xnT"][:, dc * T + c * 128: dc * T + (c + 1) * 128],
                                  r=wsl[:, dc * 256: (dc + 1) * 256],
                                  st=(dc == 0), sp=(dc == NDC - 1):
                                  E.matmul(o, l, r, start=st, stop=sp), inc="t")
                    b.wait("scalar", tm)
                    b.wait("scalar", self.pfree.get(f"stv{c % 2}"))
                    dst = self.apx(S["stg"][:, :], vof, [[S["stg"].shape[1], 128], [66, 4], [1, 64]])
                    te = b.op("scalar", lambda E, o=dst,
                              i=P[slot][:, 0:256].rearrange("p (h e) -> p h e", e=64):
                              E.copy(o, i), writes=("stg",), inc="s")
                    self.pfree[slot] = te
                    b.wait("sync", te)
                    self.pfree[f"stv{c % 2}"] = b.dma(
                        "sync", self.scr["s_V"][c * 128:(c + 1) * 128,
                                                g * 264:(g + 1) * 264],
                        S["stg"][:, vof:vof + 264], reads=("stg",))

    def _attn_group_common(self, g, wq, wk, wv, scale, projT, causal, biased, tag):
        nc, b, S, P, d = self.nc, self.b, self.S, self.P, self.d
        b.wait("sync", ("t", b.cnt["t"]))
        b.wait("sync", ("s", b.cnt["s"]))
        b.wait("vector", ("t", b.cnt["t"]))
        b.op("vector", lambda E, o=S["QTg"][64:128, :]: E.memset(o, 0.0), writes=("QTg",))
        b.op("vector", lambda E, o=S["KTg"][64:128, :]: E.memset(o, 0.0), writes=("KTg",))
        tms = b.op("vector", lambda E, o=S["KTg"][64:65, :]: E.memset(o, -1.0),
                   writes=("KTg",), inc="v")
        toks = []
        for h in range(4):
            gh4 = g * 4 + h
            toks.append(b.dma("sync", S["QTg"][0:64, h * T:(h + 1) * T],
                              self.scr["s_QT"][gh4 * 64:(gh4 + 1) * 64, :], writes=("QTg",)))
            toks.append(b.dma("sync", S["KTg"][0:64, h * T:(h + 1) * T],
                              self.scr["s_KT"][gh4 * 64:(gh4 + 1) * 64, :], writes=("KTg",)))
        toks.append(b.dma("sync", S["Vg"][:, :].rearrange("p (c w) -> p c w", w=264),
                          self.apx(self.scr["s_V"], g * 264,
                                   [[1056, 128], [128 * 1056, NC], [1, 264]]), writes=("Vg",)))
        for tok in toks:
            b.wait("tensor", tok)
            b.wait("scalar", tok)
            b.wait("vector", tok)
        b.wait("tensor", tms)
        b.wait("scalar", tms)
        stage = getattr(self, "l0_attn_stage", "full")
        if stage == "proj":
            return
        self._attn_mrow(g)
        if stage == "mrow":
            return
        sq = b.barrier("scalar", "s")
        b.wait("tensor", sq)
        nheads = 1 if stage == "head0" else 4
        for gh in range(nheads):
            self._attn_head(g, gh, causal=causal, biased=biased)
        if stage in ("head0", "heads"):
            return
        self._attn_out_proj(g, projT)

    def _attn_out_proj(self, g, projT):
        self.mark("_attn_out_proj")
        nc, b, S, P, d = self.nc, self.b, self.S, self.P, self.d
        ot_rdy = b.barrier("scalar", "s")
        b.wait("tensor", ot_rdy)
        b.wait("gpsimd", ("t", b.cnt["t"]))
        src = self.apx(projT, g * 256 * 1024, [[1024, 128], [128 * 1024, 2], [1, 1024]])
        wt = b.dma("gpsimd", S["wsl"][:, 0:2048].rearrange("p (cc e) -> p cc e", e=1024), src,
                   writes=("wsl",))
        b.wait("tensor", wt)
        ep0 = (g == 0) and getattr(self, "_epi0", None)
        for c in range(NC):
            if ep0:
                mu = S["cols"][:, c:c + 1]
                rs = S["cols"][:, 16 + c:17 + c]
                rs2 = S["cols"][:, 32 + c:33 + c]
                bcol = S["tiny"][:, 24:25]
                b.op("vector", lambda E, o=bcol, a=mu, c2=rs: E.tensor_mul(o, a, c2),
                     reads=("cols",), writes=("tiny",))
                b.op("vector", lambda E, o=bcol: E.tensor_single_scalar(o, o, -1.0, ALU.mult),
                     reads=("tiny",), writes=("tiny",))
                b.op("vector", lambda E, o=S["EP"][:, :], i=self.xc(c), a=rs, bb=bcol:
                     E.tensor_scalar(o, i, a, bb, ALU.mult, ALU.add),
                     reads=("xres", "cols", "tiny"), writes=("EP",))
                b.op("vector", lambda E, o=S["EP"][:, :], a=rs2:
                     E.tensor_single_scalar(o, o, a, ALU.mult),
                     reads=("EP", "cols"), writes=("EP",))
                b.op("vector", lambda E, o=S["f2"][:, :], i=S["EP"][:, :], j=S["f1"][:, :]:
                     E.tensor_mul(o, i, j), reads=("EP", "f1"), writes=("f2",))
                b.op("vector", lambda E, o=S["f2"][:, :], i=S["f2"][:, :], j=S["f0"][:, :]:
                     E.tensor_add(o, i, j), reads=("f2", "f0"), writes=("f2",))
            for half in range(2):
                slot = f"p{half + 2}"
                self.usepsum("tensor", slot)
                tm = None
                for cc in range(2):
                    tm = b.op("tensor", lambda E, o=P[slot][:, :],
                              l=S["OTg"][:, cc * T + c * 128: cc * T + (c + 1) * 128],
                              r=S["wsl"][:, cc * 1024 + half * 512: cc * 1024 + (half + 1) * 512],
                              st=(cc == 0), sp=(cc == 1):
                              E.matmul(o, l, r, start=st, stop=sp), inc="t")
                b.wait("vector", tm)
                xh = self.xc(c)[:, half * 512:(half + 1) * 512]
                if ep0:
                    tv = b.op("vector", lambda E, o=xh, i=P[slot][:, :], j=xh:
                              E.tensor_add(o, i, j), reads=("xres",), writes=("xres",), inc="v")
                    tv = b.op("vector", lambda E, o=xh, i=S["f2"][:, half * 512:(half + 1) * 512], j=xh:
                              E.tensor_add(o, i, j), reads=("xres", "f2"), writes=("xres",), inc="v")
                else:
                    tv = b.op("vector", lambda E, o=xh, i=P[slot][:, :], j=xh:
                              E.tensor_add(o, i, j), reads=("xres",), writes=("xres",), inc="v")
                self.pfree[slot] = tv

    def _attn_mrow(self, g):
        self.mark("_attn_mrow")
        nc, b, S, P, d = self.nc, self.b, self.S, self.P, self.d
        acc = S["tiny"][64:65, 26:30]
        b.op("vector", lambda E, o=acc: E.memset(o, 0.0), writes=("tiny",))
        tz = b.barrier("vector", "v")
        b.wait("scalar", tz)
        for h in range(4):
            for qh in range(2):
                b.wait("scalar", ("t", b.cnt["t"]))
                tq = b.op("scalar", lambda E, o=S["sq"][0:64, :],
                          i=S["KTg"][0:64, h * T + qh * 1024: h * T + (qh + 1) * 1024]:
                          E.activation(o, i, AF.Square), writes=("sq",), inc="s")
                b.wait("tensor", tq)
                for qt2 in range(2):
                    bank = "p4" if (h * 2 + qh + qt2) % 2 == 0 else "p5"
                    self.usepsum("tensor", bank)
                    tm = b.op("tensor", lambda E, o=P[bank][64:65, 0:512], l=S["h2"][0:64, 0:1],
                              r=S["sq"][0:64, qt2 * 512:(qt2 + 1) * 512]:
                              E.matmul(o, l, r, start=True, stop=True,
                                       tile_position=(0, 64)), inc="t")
                    b.wait("vector", tm)
                    b.op("vector", lambda E, o=S["tiny"][64:65, 30:31], i=P[bank][64:65, 0:512]:
                         E.reduce_max(o, i, axis=AX), writes=("tiny",))
                    tv = b.op("vector", lambda E, o=acc[:, h:h + 1], i=S["tiny"][64:65, 30:31],
                              j=acc[:, h:h + 1]: E.tensor_max(o, i, j),
                              reads=("tiny",), writes=("tiny",), inc="v")
                    self.pfree[bank] = tv
        tv = b.barrier("vector", "v")
        b.wait("scalar", tv)
        ts = b.op("scalar", lambda E, o=S["tiny"][64:65, 20:24], i=acc:
                  E.activation(o, i, AF.Sqrt), reads=("tiny",), writes=("tiny",), inc="s")
        mrs_g = [None, None]
        tvm = None
        for h in range(4):
            kmax = S["tiny"][64:65, 20 + h:21 + h]
            for qh in range(2):
                b.wait("scalar", ("t", b.cnt["t"]))
                tq = b.op("scalar", lambda E, o=S["sq"][0:64, :],
                          i=S["QTg"][0:64, h * T + qh * 1024: h * T + (qh + 1) * 1024]:
                          E.activation(o, i, AF.Square), writes=("sq",), inc="s")
                b.wait("tensor", tq)
                for qt2 in range(2):
                    qt = qh * 2 + qt2
                    bank = "p4" if (h * 2 + qh + qt2) % 2 == 0 else "p5"
                    mslot = (h * 2 + qh + qt2) % 2
                    self.usepsum("tensor", bank)
                    tm = b.op("tensor", lambda E, o=P[bank][64:65, 0:512], l=S["h2"][0:64, 0:1],
                              r=S["sq"][0:64, qt2 * 512:(qt2 + 1) * 512]:
                              E.matmul(o, l, r, start=True, stop=True,
                                       tile_position=(0, 64)), inc="t")
                    b.wait("scalar", tm)
                    b.wait("scalar", mrs_g[mslot])
                    msl = S["mrs"][64:65, mslot * 512: mslot * 512 + 512]
                    tsq = b.op("scalar", lambda E, o=msl, i=P[bank][64:65, 0:512]:
                               E.activation(o, i, AF.Sqrt), writes=("mrs",), inc="s")
                    self.pfree[bank] = tsq
                    b.wait("vector", tsq)
                    b.op("vector", lambda E, o=msl, a=kmax:
                         E.tensor_single_scalar(o, o, a, ALU.mult),
                         reads=("mrs", "tiny"), writes=("mrs",))
                    tvm = b.op("vector", lambda E,
                               o=S["QTg"][64:65, h * T + qt * 512: h * T + (qt + 1) * 512],
                               i=msl:
                               E.tensor_scalar(o, i, 1.01, 0.05, ALU.mult, ALU.add),
                               reads=("mrs",), writes=("QTg",), inc="v")
                    mrs_g[mslot] = tvm
        b.wait("tensor", tvm)

    def _attn_head(self, g, gh, causal, biased):
        self.mark("_attn_head")
        nc, b, S, P, d = self.nc, self.b, self.S, self.P, self.d
        ec, hf = gh // 2, gh % 2
        p0 = hf * 64
        ptq_guard = {0: None, 1: None}
        f2g = [None, None]
        for qt in range(4):
            po = (qt % 2) * 8192
            kts = range(NC) if not causal else range(0, qt * 4 + 4)
            b.wait("scalar", ptq_guard[qt % 2])
            b.wait("vector", ptq_guard[qt % 2])
            te = None
            te_kt = {}
            for kt in kts:
                slot = f"p{kt % 2}"
                self.usepsum("tensor", slot)
                tm = b.op("tensor", lambda E, o=P[slot][:, :],
                          l=S["KTg"][:, gh * T + kt * 128: gh * T + (kt + 1) * 128],
                          r=S["QTg"][:, gh * T + qt * 512: gh * T + (qt + 1) * 512]:
                          E.matmul(o, l, r, start=True, stop=True), inc="t")
                dst = S["PTq"][:, po + kt * 512: po + (kt + 1) * 512]
                straddle = causal and (kt >= qt * 4)
                fsl = S["f2"][:, (kt % 2) * 512:(kt % 2) * 512 + 512]
                if biased and (kt <= 8) and (qt <= 2):
                    bsl = S["B2"][:, kt * 1536 + qt * 512: kt * 1536 + (qt + 1) * 512]
                    b.wait("vector", tm)
                    b.wait("vector", f2g[kt % 2])
                    tv = b.op("vector", lambda E, o=fsl, i=P[slot][:, :], j=bsl:
                              E.tensor_add(o, i, j), reads=("B2", "f2"), writes=("f2",), inc="v")
                    b.wait("scalar", tv)
                    te = b.op("scalar", lambda E, o=dst, i=fsl:
                              E.activation(o, i, AF.Exp),
                              reads=("f2",), writes=("PTq",), inc="s")
                    f2g[kt % 2] = te
                elif straddle:
                    sidx = kt - qt * 4
                    b.wait("vector", tm)
                    b.wait("vector", f2g[kt % 2])
                    tv = b.op("vector", lambda E, o=fsl, i=P[slot][:, :],
                              j=S["fills"][:, sidx * 512:(sidx + 1) * 512]:
                              E.tensor_add(o, i, j), reads=("fills", "f2"), writes=("f2",), inc="v")
                    b.wait("scalar", tv)
                    te = b.op("scalar", lambda E, o=dst, i=fsl:
                              E.activation(o, i, AF.Exp),
                              reads=("f2",), writes=("PTq",), inc="s")
                    f2g[kt % 2] = te
                else:
                    b.wait("scalar", tm)
                    te = b.op("scalar", lambda E, o=dst, i=P[slot][:, :]:
                              E.activation(o, i, AF.Exp),
                              writes=("PTq",), inc="s")
                self.pfree[slot] = te
                te_kt[kt] = te
            if getattr(self, "l0_head_stage", "full") == "sp":
                continue
            ktl = list(kts)
            otoks = {}
            for sub in range(4):
                qc = qt * 4 + sub
                bank = f"p{2 + sub}"
                self.usepsum("tensor", bank)
                tm = None
                for kt in ktl:
                    if causal and kt > qc:
                        break
                    b.wait("tensor", te_kt[kt])
                    tm = b.op("tensor", lambda E, o=P[bank][:, 0:65],
                              l=S["PTq"][:, po + kt * 512 + sub * 128: po + kt * 512 + (sub + 1) * 128],
                              r=S["Vg"][:, kt * 264 + gh * 66: kt * 264 + gh * 66 + 65],
                              st=(kt == 0), sp=(kt == (qc if causal else NC - 1)):
                              E.matmul(o, l, r, start=st, stop=sp), inc="t")
                otoks[sub] = tm
            ptq_guard[qt % 2] = tm
            if getattr(self, "l0_head_stage", "full") == "pv":
                continue
            for sub in range(4):
                qc = qt * 4 + sub
                bank = f"p{2 + sub}"
                ps = P[bank]
                b.wait("vector", otoks[sub])
                b.wait("vector", self.pfree.get(f"obf_g{sub % 2}"))
                rec = S["tiny"][:, 35 + sub:36 + sub]
                b.op("vector", lambda E, o=rec, i=ps[:, 64:65]: E.reciprocal(o, i),
                     writes=("tiny",), reads=())
                obs = S["obf"][:, (sub % 2) * 64:(sub % 2) * 64 + 64]
                tv = b.op("vector", lambda E, o=obs, i=ps[:, 0:64], a=rec:
                          E.tensor_single_scalar(o, i, a, ALU.mult),
                          reads=("tiny", "obf"), writes=("obf",), inc="v")
                self.pfree[bank] = tv
                b.wait("tensor", tv)
                bslot = "bt0" if sub % 2 == 0 else "bt1"
                self.usepsum("tensor", bslot)
                tt = b.op("tensor", lambda E, o=P[bslot][p0:p0 + 64, 0:128], i=obs,
                          idm=S["id_bf"][:, :], tp=(0, p0):
                          E.transpose(o, i, idm, tile_position=tp), inc="t")
                self.pfree[f"obf_g{sub % 2}"] = tt
                b.wait("scalar", tt)
                te2 = b.op("scalar", lambda E,
                           o=S["OTg"][p0:p0 + 64, ec * T + qc * 128: ec * T + (qc + 1) * 128],
                           i=P[bslot][p0:p0 + 64, 0:128]: E.copy(o, i),
                           writes=("OTg",), inc="s")
                self.pfree[bslot] = te2

    def _bitlinear_ffn(self, wgT, wuT, wdT, tag):
        self.mark("_bitlinear_ffn")
        nc, b, S, P, d = self.nc, self.b, self.S, self.P, self.d
        with ExitStack() as ctx:
            for nm, sh, dt in (("wf0", [128, 4096], F32), ("wf1", [128, 4096], F32),
                               ("wqb", [128, 4096], BF16), ("wg8", [128, 4096], F32)):
                S[nm] = ctx.enter_context(nc.sbuf_tensor(tag + nm, sh, dt))
            with nc.Block() as blk:
                means = {}
                for wi, (wap, scr_n, nrows) in enumerate(
                        (("_", "s_wgq", 8), ("_", "s_wuq", 8), ("_", "s_wdq", 32))):
                    wap = (wgT, wuT, wdT)[wi]
                    acc = S["tiny"][:, 40 + wi:41 + wi]
                    b.op("vector", lambda E, o=acc: E.memset(o, 0.0), writes=("tiny",))
                    for rc in range(nrows):
                        buf = S["wf0"] if rc % 2 == 0 else S["wf1"]
                        bn = "wf0" if rc % 2 == 0 else "wf1"
                        b.wait("sync", ("v", b.cnt["v"]) if rc >= 2 else None)
                        wtok = b.dma("sync", buf[:, 0:wap.shape[1]],
                                     wap[rc * 128:(rc + 1) * 128, :], writes=(bn,))
                        b.wait("vector", wtok)
                        b.op("vector", lambda E, o=S["tiny"][:, 43:44], i=buf[:, 0:wap.shape[1]]:
                             E.reduce_sum(o, i, axis=AX, apply_absolute_value=True),
                             reads=(bn,), writes=("tiny",))
                        tv = b.op("vector", lambda E, o=acc, i=S["tiny"][:, 43:44], j=acc:
                                  E.tensor_add(o, i, j), reads=("tiny",), writes=("tiny",), inc="v")
                    b.wait("tensor", tv)
                    self.usepsum("tensor", "p5")
                    tm = b.op("tensor", lambda E, o=P["p5"][:, 0:1], l=S["ones_f32"][:, :], r=acc:
                              E.matmul(o, l, r, start=True, stop=True), inc="t")
                    b.wait("vector", tm)
                    nelem = float(nrows * 128 * wap.shape[1])
                    mcol = S["tiny"][:, 44 + wi * 3:45 + wi * 3]
                    b.op("vector", lambda E, o=mcol, i=P["p5"][:, 0:1], ne=nelem:
                         E.tensor_single_scalar(o, i, 1.0 / ne, ALU.mult), writes=("tiny",))
                    b.op("vector", lambda E, o=mcol: E.tensor_single_scalar(o, o, 1e-5, ALU.max),
                         reads=("tiny",), writes=("tiny",))
                    thr = S["tiny"][:, 45 + wi * 3:46 + wi * 3]
                    nthr = S["tiny"][:, 46 + wi * 3:47 + wi * 3]
                    b.op("vector", lambda E, o=thr, i=mcol: E.tensor_single_scalar(o, i, 0.5, ALU.mult),
                         reads=("tiny",), writes=("tiny",))
                    tv = b.op("vector", lambda E, o=nthr, i=mcol: E.tensor_single_scalar(o, i, -0.5, ALU.mult),
                              reads=("tiny",), writes=("tiny",), inc="v")
                    self.pfree["p5"] = tv
                    for rc in range(nrows):
                        buf = S["wf0"] if rc % 2 == 0 else S["wf1"]
                        bn = "wf0" if rc % 2 == 0 else "wf1"
                        b.wait("sync", ("v", b.cnt["v"]))
                        wtok = b.dma("sync", buf[:, 0:wap.shape[1]],
                                     wap[rc * 128:(rc + 1) * 128, :], writes=(bn,))
                        b.wait("vector", wtok)
                        b.wait_dma_all("vector", "sync")
                        b.op("vector", lambda E, o=S["wg8"][:, 0:wap.shape[1]], i=buf[:, 0:wap.shape[1]], a=thr:
                             E.tensor_single_scalar(o, i, a, ALU.is_ge),
                             reads=(bn, "tiny"), writes=("wg8",))
                        tq = b.op("vector", lambda E, o=S["wqb"][:, 0:wap.shape[1]], i=buf[:, 0:wap.shape[1]],
                                  a=nthr, j=S["wg8"][:, 0:wap.shape[1]]:
                                  E.scalar_tensor_tensor(o, i, a, j, ALU.is_le, ALU.subtract),
                                  reads=(bn, "tiny", "wg8"), writes=("wqb",), inc="v")
                        b.wait("sync", tq)
                        if wi < 2:
                            dst = self.apx(self.scr[scr_n], rc * 128 * 512,
                                           [[512, 128], [1024 * 512, 8], [1, 512]])
                            b.dma("sync", dst, S["wqb"][:, 0:4096].rearrange("p (eb e) -> p eb e", e=512),
                                  reads=("wqb",))
                        else:
                            b.dma("sync", self.scr[scr_n][rc * 128:(rc + 1) * 128, :],
                                  S["wqb"][:, 0:1024], reads=("wqb",))
                self.phase_barrier()
                b.flush_into(blk)
        with ExitStack() as ctx:
            for nm, sh, dt in (("zbuf", [128, 4 * 4096], BF16), ("XqTg", [128, 4096], BF16),
                               ("Zqc", [128, 2 * 4096], BF16), ("ZqT2", [128, 4 * 4096], BF16),
                               ("wgsl", [128, 2 * 4096], BF16), ("wusl", [128, 2 * 4096], BF16),
                               ("wdsl", [128, 3 * 1024], BF16),
                               ("fg", [128, 1024], F32), ("qs0", [128, 1024], F32),
                               ("qs1", [128, 512], F32), ("qI", [128, 1024], I32)):
                S[nm] = ctx.enter_context(nc.sbuf_tensor(tag + nm, sh, dt))
            with nc.Block() as blk:
                for grp in range(4):
                    self._bl_group(grp, tag)
                self.phase_barrier()
                b.flush_into(blk)
        for nm in ("wf0", "wf1", "wqb", "wg8", "zbuf", "XqTg", "Zqc", "ZqT2",
                   "wgsl", "wusl", "wdsl", "fg", "qs0", "qs1", "qI"):
            S.pop(nm, None)

    def _bl_group(self, grp, tag):
        self.mark("_bl_group")
        nc, b, S, P, d = self.nc, self.b, self.S, self.P, self.d
        b.wait("sync", ("t", b.cnt["t"]))
        xt = b.dma("sync", S["XqTg"][:, :].rearrange("p (dc e) -> p dc e", e=512),
                   self.apx(self.scr["s_XqT"], grp * 512,
                            [[2048, 128], [128 * 2048, 8], [1, 512]]), writes=("XqTg",))
        b.wait("tensor", xt)
        for qg in range(4):
            c = grp * 4 + qg
            adq = S["cols"][:, 64 + c:65 + c]
            for wi, coff in ((0, 52), (1, 56)):
                mcol = S["tiny"][:, 44 + wi * 3:45 + wi * 3]
                b.op("vector", lambda E, o=S["tiny"][:, coff + qg:coff + qg + 1], i=adq, j=mcol:
                     E.tensor_mul(o, i, j), reads=("cols", "tiny"), writes=("tiny",))
        tsc = b.barrier("vector", "v")
        b.wait("scalar", tsc)
        b.wait("scalar", self.pfree.get("qfg0"))
        b.wait("scalar", self.pfree.get("qfg1"))
        ebt = {}

        def issue_gu(eb):
            if eb >= 8:
                return
            sl = (eb % 2) * 4096
            g_t = b.dma("sync", S["wgsl"][:, sl:sl + 4096].rearrange("p (dc e) -> p dc e", e=512),
                        self.apx(self.scr["s_wgq"], eb * 1024 * 512,
                                 [[512, 128], [128 * 512, 8], [1, 512]]), writes=("wgsl",))
            u_t = b.dma("sync", S["wusl"][:, sl:sl + 4096].rearrange("p (dc e) -> p dc e", e=512),
                        self.apx(self.scr["s_wuq"], eb * 1024 * 512,
                                 [[512, 128], [128 * 512, 8], [1, 512]]), writes=("wusl",))
            ebt[eb] = (g_t, u_t)
        b.wait("sync", ("t", b.cnt["t"]))
        issue_gu(0)
        issue_gu(1)
        for eb in range(8):
            sl = (eb % 2) * 4096
            b.wait("tensor", ebt[eb][0])
            b.wait("tensor", ebt[eb][1])
            for qg in range(4):
                tmg = tmu = None
                bg = "p4" if qg % 2 == 0 else "p2"
                bu = "p5" if qg % 2 == 0 else "p3"
                fsl2 = S["fg"][:, (qg % 2) * 512:(qg % 2) * 512 + 512]
                self.usepsum("tensor", bg)
                self.usepsum("tensor", bu)
                for dc in range(NDC):
                    tmg = b.op("tensor", lambda E, o=P[bg][:, :],
                               l=S["XqTg"][:, dc * 512 + qg * 128: dc * 512 + (qg + 1) * 128],
                               r=S["wgsl"][:, sl + dc * 512: sl + (dc + 1) * 512],
                               st=(dc == 0), sp=(dc == NDC - 1):
                               E.matmul(o, l, r, start=st, stop=sp), inc="t")
                for dc in range(NDC):
                    tmu = b.op("tensor", lambda E, o=P[bu][:, :],
                               l=S["XqTg"][:, dc * 512 + qg * 128: dc * 512 + (qg + 1) * 128],
                               r=S["wusl"][:, sl + dc * 512: sl + (dc + 1) * 512],
                               st=(dc == 0), sp=(dc == NDC - 1):
                               E.matmul(o, l, r, start=st, stop=sp), inc="t")
                b.wait("scalar", tmg)
                b.wait("scalar", self.pfree.get(f"zfg{qg % 2}"))
                scg = S["tiny"][:, 52 + qg:53 + qg]
                te = b.op("scalar", lambda E, o=fsl2, i=P[bg][:, :], a=scg:
                          E.activation(o, i, AF.Silu, scale=a),
                          reads=("tiny",), writes=("fg",), inc="s")
                self.pfree[bg] = te
                b.wait("vector", te)
                b.wait("vector", tmu)
                scu = S["tiny"][:, 56 + qg:57 + qg]
                tv = b.op("vector", lambda E, o=S["zbuf"][:, qg * 4096 + eb * 512: qg * 4096 + (eb + 1) * 512],
                          i=P[bu][:, :], a=scu, j=fsl2:
                          E.scalar_tensor_tensor(o, i, a, j, ALU.mult, ALU.mult),
                          reads=("tiny", "fg"), writes=("zbuf",), inc="v")
                self.pfree[bu] = tv
                self.pfree[f"zfg{qg % 2}"] = tv
            b.wait("sync", ("t", b.cnt["t"]))
            issue_gu(eb + 2)
        for qg in range(2):
            self._quant_z_chunk(qg, grp * 4 + qg)
            self._transpose_zq(qg, qg)
        ev0 = self._down_pair(grp, 0, defer_evac=True)
        for qg in range(2, 4):
            self._quant_z_chunk(qg, grp * 4 + qg)
            self._transpose_zq(qg, qg)
        ev0()
        self._down_pair(grp, 1)

    def _quant_z_chunk(self, qg, c):
        self.mark("_quant_z_chunk")
        b, S = self.b, self.S
        b.wait("vector", self.pfree.get(f"zqc_g{qg % 2}"))
        z = S["zbuf"][:, qg * 4096:(qg + 1) * 4096]
        rm = S["tiny"][:, 60:61]
        b.op("vector", lambda E, o=rm, i=z: E.reduce_max(o, i, axis=AX, apply_absolute_value=True),
             reads=("zbuf",), writes=("tiny",))
        b.op("vector", lambda E, o=rm: E.tensor_single_scalar(o, rm, 1e-5, ALU.max),
             reads=("tiny",), writes=("tiny",))
        zadq = S["cols"][:, 80 + c:81 + c]
        b.op("vector", lambda E, o=zadq, i=rm: E.tensor_single_scalar(o, i, 1.0 / 127.0, ALU.mult),
             reads=("tiny",), writes=("cols",))
        b.wait("scalar", ("v", b.cnt["v"]))
        scq = S["tiny"][:, 61:62]
        b.op("vector", lambda E, o=S["tiny"][:, 62:63], i=rm: E.reciprocal(o, i),
             reads=("tiny",), writes=("tiny",))
        b.op("vector", lambda E, o=scq, i=S["tiny"][:, 62:63]:
             E.tensor_single_scalar(o, i, 127.0, ALU.mult), reads=("tiny",), writes=("tiny",))
        for sp in range(4):
            tst = {}
            for k in range(2):
                sl = sp * 2 + k
                fsl = S["fg"][:, k * 512:(k + 1) * 512]
                b.wait("scalar", self.pfree.get(f"qfg{k}"))
                tst[k] = b.op("scalar", lambda E, o=fsl, i=z[:, sl * 512:(sl + 1) * 512]:
                              E.copy(o, i), reads=("zbuf",), writes=("fg",), inc="s")
            for k in range(2):
                b.wait("vector", tst[k])
                b.op("vector", lambda E, o=S["qI"][:, k * 512:(k + 1) * 512],
                     i=S["fg"][:, k * 512:(k + 1) * 512], a=scq:
                     E.tensor_single_scalar(o, i, a, ALU.mult),
                     reads=("fg", "tiny"), writes=("qI",))
            for k in range(2):
                tq0 = b.op("vector", lambda E, o=S["qs0"][:, k * 512:(k + 1) * 512],
                           i=S["fg"][:, k * 512:(k + 1) * 512], a=scq,
                           ci=S["qI"][:, k * 512:(k + 1) * 512]:
                           E.scalar_tensor_tensor(o, i, a, ci, ALU.mult, ALU.subtract),
                           reads=("fg", "tiny", "qI"), writes=("qs0",), inc="v")
                self.pfree[f"qfg{k}"] = tq0
            for k in range(2):
                sl = sp * 2 + k
                b.op("vector", lambda E, o=S["qs1"][:, :],
                     i=S["qs0"][:, k * 512:(k + 1) * 512], ci=S["qI"][:, k * 512:(k + 1) * 512]:
                     E.scalar_tensor_tensor(o, i, 0.5, ci, ALU.is_ge, ALU.add),
                     reads=("qs0", "qI"), writes=("qs1",))
                b.op("vector", lambda E,
                     o=S["Zqc"][:, (qg % 2) * 4096 + sl * 512:(qg % 2) * 4096 + (sl + 1) * 512],
                     i=S["qs0"][:, k * 512:(k + 1) * 512], t1=S["qs1"][:, :]:
                     E.scalar_tensor_tensor(o, i, -0.5, t1, ALU.is_le, ALU.subtract),
                     reads=("qs0", "qs1"), writes=("Zqc",))

    def _transpose_zq(self, qg, sub):
        self.mark("_transpose_zq")
        b, S, P = self.b, self.S, self.P
        tz = b.barrier("vector", "v")
        b.wait("tensor", tz)
        zoff = (qg % 2) * 4096
        for blkk in range(8):
            slot = "bt0" if blkk % 2 == 0 else "bt1"
            pt = P[slot]
            self.usepsum("tensor", slot)
            tt = None
            for k in range(4):
                hc = blkk * 4 + k
                tt = b.op("tensor", lambda E, o=pt[:, k * 128:(k + 1) * 128],
                          i=S["Zqc"][:, zoff + hc * 128: zoff + (hc + 1) * 128],
                          idm=S["id_bf"][:, :]:
                          E.transpose(o, i, idm), inc="t")
            b.wait("scalar", tt)
            self.pfree[f"zqc_g{qg % 2}"] = tt
            te = b.op("scalar", lambda E, o=S["ZqT2"][:, sub * 4096 + blkk * 512:(sub * 4096 + (blkk + 1) * 512)],
                      i=pt[:, 0:512]: E.copy(o, i), writes=("ZqT2",), inc="s")
            self.pfree[slot] = te

    def _down_pair(self, grp, pair, defer_evac=False):
        self.mark("_down_pair")
        nc, b, S, P, d = self.nc, self.b, self.S, self.P, self.d
        tz = b.barrier("scalar", "s")
        b.wait("tensor", tz)
        mwd = S["tiny"][:, 50:51]
        toks = [None] * 4
        wdt = {}

        def issue_wd(hc):
            if hc >= 32:
                return
            b.wait("sync", self.pfree.get(f"wd{hc % 3}"))
            wdt[hc] = b.dma("sync", S["wdsl"][:, (hc % 3) * 1024:(hc % 3 + 1) * 1024],
                            self.scr["s_wdq"][hc * 128:(hc + 1) * 128, :], writes=("wdsl",))
        for hh in range(3):
            issue_wd(hh)
        for hc in range(32):
            buf = S["wdsl"][:, (hc % 3) * 1024:(hc % 3 + 1) * 1024]
            b.wait("tensor", wdt[hc])
            for sub in range(2):
                for half in range(2):
                    pslot = f"p{sub * 2 + half}"
                    if hc == 0:
                        self.usepsum("tensor", pslot)
                    tm = b.op("tensor", lambda E, o=P[pslot][:, :],
                              l=S["ZqT2"][:, (pair * 2 + sub) * 4096 + hc * 128:
                                          ((pair * 2 + sub) * 4096 + (hc + 1) * 128)],
                              r=buf[:, half * 512:(half + 1) * 512],
                              st=(hc == 0), sp=(hc == 31):
                              E.matmul(o, l, r, start=st, stop=sp), inc="t")
                    if hc == 31:
                        toks[sub * 2 + half] = tm
            self.pfree[f"wd{hc % 3}"] = tm
            issue_wd(hc + 3)

        def evac():
            for sub in range(2):
                qg = pair * 2 + sub
                c = grp * 4 + qg
                zadq = S["cols"][:, 80 + c:81 + c]
                scd = S["tiny"][:, 43:44]
                b.op("vector", lambda E, o=scd, i=zadq, j=mwd: E.tensor_mul(o, i, j),
                     reads=("cols", "tiny"), writes=("tiny",))
                for half in range(2):
                    pslot = f"p{sub * 2 + half}"
                    b.wait("vector", toks[sub * 2 + half])
                    xh = self.xc(c)[:, half * 512:(half + 1) * 512]
                    tv = b.op("vector", lambda E, o=xh, i=P[pslot][:, :], a=scd, j=xh:
                              E.scalar_tensor_tensor(o, i, a, j, ALU.mult, ALU.add),
                              reads=("tiny", "xres"), writes=("xres",), inc="v")
                    self.pfree[pslot] = tv
        if defer_evac:
            return evac
        evac()


def kernel(**inputs):
    shared = _prep_shared(inputs)
    x = inputs["L0_attn_x"]
    p = Prog()
    nc = p.build()
    in_maps = []
    for core in range(8):
        m = dict(shared)
        m["x"] = np.ascontiguousarray(x[core])
        in_maps.append(m)
    res = run_bass_kernel_spmd(nc, in_maps, core_ids=list(range(8)))
    out = np.stack([np.asarray(r["y"]) for r in res.results], axis=0)
    return out.astype(np.float32)
